# version 03 (final)
# baseline (speedup 1.0000x reference)
import numpy as np

import concourse.bass as bass
import concourse.mybir as mybir
from concourse.bass_utils import run_bass_kernel_spmd

F32 = mybir.dt.float32
BF16 = mybir.dt.bfloat16
I32 = mybir.dt.int32
OP = mybir.AluOpType
AF = mybir.ActivationFunctionType
AX = mybir.AxisListType

FULL_CFG = dict(
    B=4, T=4096, D=2048, DFF=8192, I1=8192,
    NK=8, NV=16, DK=64, DV=64, KS=4, K0=64, V0=128,
    C0=64, C1=128, n_cores=8,
)


class EM:
    ENGINES = ("sync", "scalar", "vector", "tensor", "gpsimd")

    def __init__(self, nc):
        self.nc = nc
        self.ops = {e: [] for e in self.ENGINES}
        self.sems = {}
        self.cnt = {}
        self.waited = {e: {} for e in self.ENGINES}

    def sem(self, name):
        if name not in self.sems:
            self.sems[name] = self.nc.alloc_semaphore(name)
            self.cnt[name] = 0
        return self.sems[name]

    def tok(self, name):
        self.sem(name)
        return (name, self.cnt[name], None)

    @staticmethod
    def force(t):
        return (t[0], t[1], None) if t is not None else None

    def _waits(self, eng, deps):
        for d in deps or []:
            if d is None:
                continue
            s, v, src = d
            if v <= 0 or src == eng:
                continue
            if self.waited[eng].get(s, 0) >= v:
                continue
            self.waited[eng][s] = v
            h = self.sem(s)
            self.ops[eng].append(lambda e, h=h, v=v: e.wait_ge(h, v))

    def emit(self, eng, fn, deps=None, inc=None, add=1):
        self._waits(eng, deps)
        tok = None
        if inc is not None:
            h = self.sem(inc)
            self.cnt[inc] += add
            src = None if (inc.startswith("d_") or inc.startswith("cc")) else eng
            tok = (inc, self.cnt[inc], src)
            self.ops[eng].append(lambda e, fn=fn, h=h, add=add: fn(e).then_inc(h, add))
        else:
            self.ops[eng].append(lambda e, fn=fn: fn(e))
        return tok

    def dma(self, q, out, in_, deps=None, stream=None):
        stream = stream or ("d_" + q)
        return self.emit(q, lambda e: e.dma_start(out=out, in_=in_),
                         deps=deps, inc=stream, add=16)

    def dmat(self, q, out, in_, deps=None, stream=None):
        stream = stream or ("d_" + q)
        return self.emit(q, lambda e: e.dma_start_transpose(out, in_),
                         deps=deps, inc=stream, add=16)

    def barrier(self):
        dtoks = [self.tok(k) for k in list(self.cnt) if k.startswith("d_")]
        self._waits("gpsimd", dtoks)
        toks = []
        for e in self.ENGINES:
            toks.append(self.emit(e, lambda en: en.nop(), inc="bar_" + e))
        toks = [(s, v, None) for (s, v, _e) in toks]
        for e in self.ENGINES:
            self._waits(e, toks)

    def finalize(self, block):
        for e in self.ENGINES:
            ops = self.ops[e]

            def body(eng, ops=ops):
                for f in ops:
                    f(eng)
            getattr(block, e)(body)


class Psum:

    def __init__(self, nc):
        self.banks = [nc.alloc_psum_tensor(f"pb{i}", [128, 512], F32) for i in range(8)]
        self.war = [None] * 8
        self.rr = 0

    def take(self):
        i = self.rr
        self.rr = (self.rr + 1) % 8
        return i, self.banks[i], self.war[i]

    def done(self, i, tok):
        self.war[i] = tok


def host_prepare(cfg, inp):
    B, T, D = cfg["B"], cfg["T"], cfg["D"]
    Tc = T // 2
    NV, DV, KS = cfg["NV"], cfg["DV"], cfg["KS"]
    C0, C1 = cfg["C0"], cfg["C1"]
    cc = lambda a: np.ascontiguousarray(a, dtype=np.float32)

    sh = {}
    sh["Wqk"] = cc(np.concatenate([inp["L0_attn_Wq"], inp["L0_attn_Wk"]], axis=1))
    sh["Wbd"] = cc(np.concatenate([inp["L0_attn_Wb"], inp["L0_attn_Wd"]], axis=1))
    sh["Wv0"] = cc(inp["L0_attn_Wv"])
    sh["Wo0"] = cc(inp["L0_attn_Wo"])
    sh["bbd"] = cc(np.concatenate([inp["L0_attn_bb"], inp["L0_attn_bd"]])[:, None])
    sh["w1T"] = cc(np.asarray(inp["L0_ffn_w1"]).T)
    sh["w2T"] = cc(np.asarray(inp["L0_ffn_w2"]).T)
    sh["w3T"] = cc(np.asarray(inp["L0_ffn_w3"]).T)
    sh["WqkvT"] = cc(np.asarray(inp["L1_attn_W_qkv"]).T)
    sh["WzT"] = cc(np.asarray(inp["L1_attn_W_z"]).T)
    D_ = np.asarray(inp["L1_attn_W_qkv"]).shape[1]
    NV_ = np.asarray(inp["L1_attn_W_b"]).shape[0]
    wba = np.zeros((D_, 64), np.float32)
    wba[:, 0:NV_] = np.asarray(inp["L1_attn_W_b"]).T
    wba[:, 32:32 + NV_] = np.asarray(inp["L1_attn_W_a"]).T
    sh["WbaT"] = cc(wba)
    sh["WoutT"] = cc(np.asarray(inp["L1_attn_W_out"]).T)
    sh["convw"] = cc(np.asarray(inp["L1_attn_conv_w"])[:, 0, :])
    dtb = np.zeros((64, 1), np.float32)
    dtb[32:32 + NV_, 0] = np.asarray(inp["L1_attn_dt_bias"])
    sh["dt_bias"] = cc(dtb)
    sh["A_log"] = cc(np.asarray(inp["L1_attn_A_log"])[:, None])
    sh["normw_row"] = cc(np.broadcast_to(
        1.0 + np.asarray(inp["L1_attn_norm_weight"])[None, :], (C1, DV)))
    sh["wg"] = cc(inp["L1_ffn_w_gate"])
    sh["wu"] = cc(inp["L1_ffn_w_up"])
    sh["wd"] = cc(inp["L1_ffn_w_down"])

    sh["ident"] = np.eye(128, dtype=np.float32)
    sh["mask0"] = cc(np.triu(np.ones((C0, C0))))
    sh["ltstrict"] = cc(np.tril(np.ones((C1, C1)), -1))
    sh["utstrict"] = cc(np.triu(np.ones((C1, C1)), 1))
    sh["utincl"] = cc(np.triu(np.ones((C1, C1))))
    sh["ones_row"] = np.ones((1, 512), np.float32)
    sh["ones_col"] = np.ones((128, 1), np.float32)
    sh["ones_tile"] = np.ones((128, 128), np.float32)
    sh["pm1"] = cc(np.concatenate([-np.ones(C0), np.ones(C0)])[:, None])
    lr = np.zeros((128, 128), np.float32)
    lr[C1 - 1, :] = 1.0
    sh["lastrow"] = lr
    bi_ = np.zeros((128, 2), np.float32)
    bi_[:64, 0] = 1.0
    bi_[64:, 1] = 1.0
    sh["blk64"] = bi_
    sh["exp64"] = np.ascontiguousarray(bi_.T)

    per_core = []
    x = np.asarray(inp["L0_attn_x"], dtype=np.float32)
    for c in range(cfg["n_cores"]):
        b, half = c // 2, c % 2
        d = dict(sh)
        d["x_in"] = cc(x[b, half * Tc:(half + 1) * Tc, :])
        d["flag_even"] = np.full((1, 1), 1.0 if half == 0 else 0.0, np.float32)
        d["flag_odd"] = np.full((1, 1), 0.0 if half == 0 else 1.0, np.float32)
        per_core.append(d)
    return per_core


def build_program(cfg, debug=False, phases=99):
    B, T, D = cfg["B"], cfg["T"], cfg["D"]
    Tc = T // 2
    DFF, I1 = cfg["DFF"], cfg["I1"]
    NK, NV, DK, DV, KS = cfg["NK"], cfg["NV"], cfg["DK"], cfg["DV"], cfg["KS"]
    K0, V0 = cfg["K0"], cfg["V0"]
    C0, C1 = cfg["C0"], cfg["C1"]
    KD1, VD1 = NK * DK, NV * DV
    CONV = 2 * KD1 + VD1
    ND = D // 128
    NTC = Tc // 128
    NC0 = Tc // C0
    NC1 = Tc // C1
    NM = CONV // 128
    rep = NV // NK
    assert DK == 64 and DV == 64 and K0 == 64 and V0 == 128 and C0 == 64 and C1 == 128

    nc = bass.Bass("TRN2", target_bir_lowering=False, detect_race_conditions=False)
    em = EM(nc)
    ps = Psum(nc)

    def din(name, shape, dt=F32):
        return nc.dram_tensor(name, list(shape), dt, kind="ExternalInput")

    x_in = din("x_in", (Tc, D))
    Wqk = din("Wqk", (D, 128)); Wbd = din("Wbd", (D, 128))
    Wv0 = din("Wv0", (D, V0)); Wo0 = din("Wo0", (V0, D)); bbd_in = din("bbd", (128, 1))
    w1T = din("w1T", (D, DFF)); w2T = din("w2T", (D, DFF)); w3T = din("w3T", (DFF, D))
    WqkvT = din("WqkvT", (D, CONV)); WzT = din("WzT", (D, VD1))
    WbaT = din("WbaT", (D, 64)); WoutT = din("WoutT", (VD1, D))
    convw_in = din("convw", (CONV, KS))
    dtb_in = din("dt_bias", (64, 1)); alog_in = din("A_log", (NV, 1))
    normw_in = din("normw_row", (C1, DV))
    wg_in = din("wg", (D, I1)); wu_in = din("wu", (D, I1)); wd_in = din("wd", (I1, D))
    ident_in = din("ident", (128, 128))
    mask0_in = din("mask0", (C0, C0))
    lts_in = din("ltstrict", (C1, C1)); uts_in = din("utstrict", (C1, C1))
    uti_in = din("utincl", (C1, C1))
    onesr_in = din("ones_row", (1, 512)); onesc_in = din("ones_col", (128, 1))
    onest_in = din("ones_tile", (128, 128))
    pm1_in = din("pm1", (128, 1)); lastrow_in = din("lastrow", (128, 128))
    blk64_in = din("blk64", (128, 2)); exp64_in = din("exp64", (2, 128))
    feven_in = din("flag_even", (1, 1)); fodd_in = din("flag_odd", (1, 1))

    x_out = nc.dram_tensor("x_out", [Tc, D], F32, kind="ExternalOutput")

    def scratch(name, shape, dt=F32):
        if debug:
            return nc.dram_tensor(name, list(shape), dt, kind="ExternalOutput")
        return nc.dram_tensor(name, list(shape), dt)

    xn0_s = scratch("xn0_s", (Tc, D))
    w1q_s = scratch("w1q_s", (D, DFF), BF16)
    w2q_s = scratch("w2q_s", (D, DFF), BF16)
    w3q_s = scratch("w3q_s", (DFF, D), BF16)
    x1_s = scratch("x1_s", (Tc, D))
    xn1f_s = scratch("xn1f_s", (D, Tc))
    qkvp_s = scratch("qkvp_s", (CONV, Tc))
    qkvc_s = scratch("qkvc_s", (CONV, Tc))
    z1_s = scratch("z1_s", (VD1, Tc))
    s1_s = scratch("s1_s", (Tc, D))
    hsp_s = scratch("hsp_s", (I1, Tc))
    og_dbg = scratch("og_dbg", (VD1, Tc)) if debug else None
    ofm_dbg = scratch("ofm_dbg", (V0, Tc)) if debug else None
    ht_dbg = scratch("ht_dbg", (Tc, DFF)) if debug else None
    qkf_dbg = scratch("qkf_dbg", (128, Tc)) if debug else None
    p1x_dbg = scratch("p1x_dbg", (Tc, D)) if debug else None
    p1c_dbg = scratch("p1c_dbg", (128, NTC * 4)) if debug else None
    bdf_dbg = scratch("bdf_dbg", (128, Tc)) if debug else None
    vtm_dbg = scratch("vtm_dbg", (64, (Tc // C0) * V0)) if debug else None
    spf_dbg = scratch("spf_dbg", (128, Tc)) if debug else None
    qt_dbg = scratch("qt_dbg", (64, Tc)) if debug else None
    am_dbg = scratch("am_dbg", (64, Tc)) if debug else None
    U_dbg = scratch("U_dbg", (64, (Tc // C0) * V0)) if debug else None
    lam_dbg = scratch("lam_dbg", (64, Tc)) if debug else None
    xqi_dbg = scratch("xqi_dbg", (Tc, D)) if debug else None
    hg6_dbg = scratch("hg6_dbg", (Tc, I1)) if debug else None
    g1_dbg = scratch("g1_dbg", (Tc, D)) if debug else None
    hqi_dbg = scratch("hqi_dbg", (Tc, DFF)) if debug else None

    cc0_i = nc.dram_tensor("cc0_i", [K0, V0], F32)
    cc0_o = nc.dram_tensor("cc0_o", [K0, V0], F32)
    cch_i = nc.dram_tensor("cch_i", [128, NM * (KS - 1)], F32)
    cch_o = nc.dram_tensor("cch_o", [128, NM * (KS - 1)], F32)
    cc1_i = [nc.dram_tensor(f"cc1_i{h}", [DK, DV], F32) for h in range(NV)]
    cc1_o = [nc.dram_tensor(f"cc1_o{h}", [DK, DV], F32) for h in range(NV)]
    rg = [[i, i + 1] for i in range(0, cfg["n_cores"], 2)]

    arena = {"base": None, "cap": None, "g": 0, "p": 0}

    def sb(name, shape, dt=F32, life="perm", **kw):
        import math as _m
        if life == "perm" or arena["base"] is None:
            return nc.alloc_sbuf_tensor(name, list(shape), dt, **kw)
        n_free = 1
        for d in shape[1:]:
            n_free *= d
        nb = (n_free * mybir.dt.size(dt) + 63) // 64 * 64
        if life == "group":
            off = arena["base"] + arena["g"]
            arena["g"] += nb
        else:
            arena["p"] += nb
            off = arena["base"] + arena["cap"] - arena["p"]
        assert arena["g"] + arena["p"] <= arena["cap"], (
            f"sbuf arena overflow at {name}: g={arena['g']} p={arena['p']} cap={arena['cap']}")
        return nc.alloc_sbuf_tensor_at(name, list(shape), dt, offset=off)

    def arena_freeze():
        a0 = nc._sbuf_addr_for_side(None)
        arena["base"] = (a0 + 63) // 64 * 64
        arena["cap"] = nc.sbuf_bytes_remaining - 2048 - (arena["base"] - a0)

    def phase_reset():
        arena["p"] = 0

    def group_reset():
        arena["g"] = 0

    c_ident = sb("c_ident", (128, 128))
    c_mask0 = sb("c_mask0", (C0, C0))
    c_lts = sb("c_lts", (C1, C1)); c_uts = sb("c_uts", (C1, C1)); c_uti = sb("c_uti", (C1, C1))
    c_onesr = sb("c_onesr", (1, 512)); c_onesc = sb("c_onesc", (128, 1))
    c_onest = sb("c_onest", (128, 128))
    c_pm1 = sb("c_pm1", (128, 1)); c_lastrow = sb("c_lastrow", (128, 128))
    c_bbd = sb("c_bbd", (128, 1)); c_dtb = sb("c_dtb", (64, 1))
    c_negexpA = sb("c_negexpA", (NV, 1)); c_normwr = sb("c_normwr", (C1, DV))
    c_convw = sb("c_convw", (128, NM * KS))
    c_feven = sb("c_feven", (1, 1)); c_fodd = sb("c_fodd", (1, 1))
    c_blk64 = sb("c_blk64", (128, 2)); c_exp64 = sb("c_exp64", (2, 128))
    c_fe128 = sb("c_fe128", (128, 1)); c_fo128 = sb("c_fo128", (128, 1))

    ld0 = []
    for dst, src in [(c_ident, ident_in), (c_mask0, mask0_in), (c_lts, lts_in),
                     (c_uts, uts_in), (c_uti, uti_in), (c_onesr, onesr_in),
                     (c_onesc, onesc_in), (c_onest, onest_in), (c_pm1, pm1_in),
                     (c_lastrow, lastrow_in), (c_bbd, bbd_in), (c_dtb, dtb_in),
                     (c_negexpA, alog_in), (c_normwr, normw_in),
                     (c_feven, feven_in), (c_fodd, fodd_in),
                     (c_blk64, blk64_in), (c_exp64, exp64_in)]:
        ld0.append(em.dma("sync", dst[:], src[:]))
    for m in range(NM):
        ld0.append(em.dma("sync", c_convw[:, m * KS:(m + 1) * KS],
                          convw_in[m * 128:(m + 1) * 128, :]))
    t_e = em.emit("scalar", lambda e: e.activation(c_negexpA[:], c_negexpA[:], AF.Exp),
                  deps=ld0, inc="s")
    em.emit("vector", lambda e: e.tensor_scalar(c_negexpA[:], c_negexpA[:], -1.0, None,
                                                op0=OP.mult), deps=[t_e], inc="v")
    bi, bank, war = ps.take()
    t1 = em.emit("tensor", lambda e, bank=bank: e.matmul(
        bank[:, 0:1], c_onesr[0:1, 0:128], c_feven[:], start=True, stop=True),
        deps=ld0, inc="t")
    t2 = em.emit("tensor", lambda e, bank=bank: e.matmul(
        bank[:, 8:9], c_onesr[0:1, 0:128], c_fodd[:], start=True, stop=True), inc="t")
    em.emit("vector", lambda e, bank=bank: e.tensor_copy(c_fe128[:], bank[:, 0:1]),
            deps=[t2], inc="v")
    tv = em.emit("vector", lambda e, bank=bank: e.tensor_copy(c_fo128[:], bank[:, 8:9]), inc="v")
    ps.done(bi, tv)

    c_mclip = sb("c_mclip", (128, 3))
    c_wsb = sb("c_wsb", (128, 3))
    c_nwsb = sb("c_nwsb", (128, 3))
    NDF = DFF // 128
    arena_freeze()

    em.barrier()

    if phases >= 0:
        WCOL = min(4096, max(DFF, D))
        wbuf = sb("p0_wbuf", (128, WCOL), life="phase")
        wg1 = sb("p0_wg1", (128, WCOL), life="phase")
        wg2 = sb("p0_wg2", (128, WCOL), life="phase")
        wqb = sb("p0_wqb", (128, WCOL), BF16, life="phase")
        jnk = sb("p0_jnk", (128, WCOL), life="phase")
        aacc = sb("p0_aacc", (128, 1), life="phase")
        a1 = sb("p0_a1", (128, 1), life="phase")
        m11 = sb("p0_m11", (1, 1), life="phase")
        for j, (wsrc, wdst, R, CC) in enumerate(
                [(w1T, w1q_s, D, DFF), (w2T, w2q_s, D, DFF), (w3T, w3q_s, DFF, D)]):
            NR = R // 128
            NCW = (CC + WCOL - 1) // WCOL
            vtok = em.emit("vector", lambda e: e.memset(aacc[:], 0.0), inc="v")
            last_read = vtok
            for r in range(NR):
              for cc0 in range(NCW):
                c0, c1_ = cc0 * WCOL, min((cc0 + 1) * WCOL, CC)
                cw_ = c1_ - c0
                tl = em.dma("sync", wbuf[:, 0:cw_], wsrc[r * 128:(r + 1) * 128, c0:c1_],
                            deps=[last_read], stream="d_w")
                stok = em.emit("scalar", lambda e, cw_=cw_: e.activation(
                    jnk[:, 0:cw_], wbuf[:, 0:cw_], AF.Abs, accum_out=a1[:]),
                    deps=[tl], inc="s")
                last_read = stok
                vtok = em.emit("vector", lambda e: e.tensor_tensor(
                    aacc[:], aacc[:], a1[:], op=OP.add), deps=[stok, em.force(vtok)], inc="v")
            bi, bank, war = ps.take()
            tt = em.emit("tensor", lambda e, bank=bank: e.matmul(
                bank[0:1, 0:1], aacc[:], c_onesc[:], start=True, stop=True),
                deps=[vtok, war], inc="t")
            numel = float(R * CC)
            v1 = em.emit("vector", lambda e, bank=bank, numel=numel: e.tensor_scalar(
                m11[:], bank[0:1, 0:1], 1.0 / numel, 1e-5, op0=OP.mult, op1=OP.max),
                deps=[tt], inc="v")
            bi2, bank2, war2 = ps.take()
            t2 = em.emit("tensor", lambda e, bank2=bank2: e.matmul(
                bank2[:, 0:1], c_onesr[0:1, 0:128], m11[:], start=True, stop=True),
                deps=[v1, war2], inc="t")
            v2 = em.emit("vector", lambda e, bank2=bank2, j=j: e.tensor_copy(
                c_mclip[:, j:j + 1], bank2[:, 0:1]), deps=[t2], inc="v")
            ps.done(bi, v1); ps.done(bi2, v2)
            v3 = em.emit("vector", lambda e, j=j: e.reciprocal(
                c_wsb[:, j:j + 1], c_mclip[:, j:j + 1]), deps=[em.force(v2)], inc="v")
            v4 = em.emit("vector", lambda e, j=j: e.tensor_scalar(
                c_nwsb[:, j:j + 1], c_wsb[:, j:j + 1], -1.0, None, op0=OP.mult),
                deps=[em.force(v3)], inc="v")
            st_tok = None
            for r in range(NR):
              for cc0 in range(NCW):
                c0, c1_ = cc0 * WCOL, min((cc0 + 1) * WCOL, CC)
                cw_ = c1_ - c0
                tl = em.dma("sync", wbuf[:, 0:cw_], wsrc[r * 128:(r + 1) * 128, c0:c1_],
                            deps=[last_read, st_tok], stream="d_w")
                g1 = em.emit("vector", lambda e, cw_=cw_, j=j: e.tensor_scalar(
                    wg1[:, 0:cw_], wbuf[:, 0:cw_], c_wsb[:, j:j + 1], 0.5,
                    op0=OP.mult, op1=OP.is_gt), deps=[tl, em.force(v4)], inc="v")
                g2 = em.emit("vector", lambda e, cw_=cw_, j=j: e.tensor_scalar(
                    wg2[:, 0:cw_], wbuf[:, 0:cw_], c_nwsb[:, j:j + 1], 0.5,
                    op0=OP.mult, op1=OP.is_gt), inc="v")
                last_read = g2
                g3 = em.emit("vector", lambda e, cw_=cw_: e.tensor_tensor(
                    wqb[:, 0:cw_], wg1[:, 0:cw_], wg2[:, 0:cw_], op=OP.subtract), inc="v")
                st_tok = em.dma("gpsimd", wdst[r * 128:(r + 1) * 128, c0:c1_], wqb[:, 0:cw_],
                                deps=[g3], stream="d_st")
        em.emit("sync", lambda e: e.nop(), deps=[em.tok("d_st")])
    em.barrier()
    phase_reset()

    qk_f = sb("qk_f", (128, Tc), life="group")
    bd_f = sb("bd_f", (128, Tc), life="group")
    v_tm = sb("v_tm", (64, NC0 * V0), life="group")
    o_fm = sb("o_fm", (V0, Tc), life="group")
    wo0_sb = sb("wo0_sb", (V0, D), life="group")
    ld_wo = em.dma("sync", wo0_sb[:], Wo0[:], stream="d_w")

    if phases >= 1:
        w_qk = sb("p1_wqk", (128, ND * 128), life="phase")
        w_bd = sb("p1_wbd", (128, ND * 128), life="phase")
        w_v = sb("p1_wv", (128, ND * V0), life="phase")
        ldw = []
        for j in range(ND):
            ldw.append(em.dma("sync", w_qk[:, j * 128:(j + 1) * 128],
                              Wqk[j * 128:(j + 1) * 128, :], stream="d_w"))
            ldw.append(em.dma("sync", w_bd[:, j * 128:(j + 1) * 128],
                              Wbd[j * 128:(j + 1) * 128, :], stream="d_w"))
            ldw.append(em.dma("sync", w_v[:, j * V0:(j + 1) * V0],
                              Wv0[j * 128:(j + 1) * 128, :], stream="d_w"))
        xt = sb("p1_xt", (128, 2 * D), life="phase")
        xnt = sb("p1_xnt", (128, 2 * D), life="phase")
        xnf = sb("p1_xnf", (128, ND * 512), life="phase")
        sm = sb("p1_sm", (128, 1), life="phase"); sq = sb("p1_sq", (128, 1), life="phase")
        mcol = sb("p1_m", (128, 1), life="phase"); vcol = sb("p1_var", (128, 1), life="phase")
        rstd = sb("p1_rstd", (128, 1), life="phase")
        jnk1 = sb("p1_jnk", (128, D), life="phase")
        TT_ = min(512, Tc)
        NTT_ = Tc // TT_
        ntc_per_tt = TT_ // 128
        xn_done = [None] * NTC
        xnf_free = None
        for g in range(NTT_):
            for ii in range(ntc_per_tt):
                i = g * ntc_per_tt + ii
                sl = i % 2
                xv = xt[:, sl * D:(sl + 1) * D]
                xnv = xnt[:, sl * D:(sl + 1) * D]
                ld_xi = em.dma("sync", xv, x_in[i * 128:(i + 1) * 128, :],
                               deps=[xn_done[i - 2]] if i >= 2 else None,
                               stream=f"d_x{i % 2}")
                r1 = em.emit("vector", lambda e, xv=xv: e.reduce_sum(
                    sm[:], xv, axis=AX.X), deps=[ld_xi], inc="v")
                s1 = em.emit("scalar", lambda e, xv=xv: e.activation(
                    jnk1[:], xv, AF.Square, accum_out=sq[:]), deps=[ld_xi], inc="s")
                t_ = em.emit("vector", lambda e: e.tensor_scalar(
                    mcol[:], sm[:], 1.0 / D, None, op0=OP.mult),
                    deps=[em.force(r1)], inc="v")
                t_ = em.emit("vector", lambda e: e.tensor_scalar(
                    vcol[:], sq[:], 1.0 / D, None, op0=OP.mult),
                    deps=[s1, em.force(t_)], inc="v")
                t_ = em.emit("vector", lambda e: e.scalar_tensor_tensor(
                    vcol[:], mcol[:], mcol[:], vcol[:], op0=OP.mult, op1=OP.subtract),
                    deps=[em.force(t_)], inc="v")
                t_ = em.emit("vector", lambda e: e.tensor_scalar(
                    vcol[:], vcol[:], -1.0, 1e-5, op0=OP.mult, op1=OP.add),
                    deps=[em.force(t_)], inc="v")
                sv = em.emit("scalar", lambda e: e.activation(
                    rstd[:], vcol[:], AF.Sqrt), deps=[em.tok("v")], inc="s")
                rv = em.emit("vector", lambda e: e.reciprocal(
                    rstd[:], rstd[:]), deps=[sv], inc="v")
                xn_done[i] = em.emit("vector", lambda e, xv=xv, xnv=xnv: e.tensor_scalar(
                    xnv, xv, mcol[:], rstd[:], op0=OP.subtract, op1=OP.mult),
                    deps=[em.force(rv)], inc="v")
                if debug:
                    em.dma("gpsimd", p1x_dbg[i * 128:(i + 1) * 128, :], xv,
                           deps=[xn_done[i]], stream="d_st")
                    for jj, colsrc in enumerate([sm, sq, mcol, rstd]):
                        em.emit("gpsimd", lambda e, i=i, jj=jj, colsrc=colsrc: e.dma_start(
                            out=p1c_dbg[:, i * 4 + jj:i * 4 + jj + 1], in_=colsrc[:],
                            allow_slow_non_contiguous=True),
                            deps=[xn_done[i]], inc="d_st", add=16)
                em.dma("gpsimd", xn0_s[i * 128:(i + 1) * 128, :], xnv,
                       deps=[xn_done[i]], stream="d_st")
                for j in range(ND):
                    bi, bank, war = ps.take()
                    tt = em.emit("tensor", lambda e, bank=bank, xnv=xnv, j=j: e.transpose(
                        bank[:, 0:128], xnv[:, j * 128:(j + 1) * 128], c_ident[:]),
                        deps=[xn_done[i], war, xnf_free], inc="t")
                    ev = em.emit("vector", lambda e, bank=bank, j=j, ii=ii: e.tensor_copy(
                        xnf[:, j * 512 + ii * 128: j * 512 + (ii + 1) * 128], bank[:, 0:128]),
                        deps=[tt], inc="t_evac")
                    ps.done(bi, ev)
                bi, bank, war = ps.take()
                mm = None
                for j in range(ND):
                    mm = em.emit("tensor", lambda e, bank=bank, j=j, ii=ii, st=(j == 0), sp=(j == ND - 1):
                                 e.matmul(bank[:, 0:V0],
                                          xnf[:, j * 512 + ii * 128: j * 512 + (ii + 1) * 128],
                                          w_v[:, j * V0:(j + 1) * V0], start=st, stop=sp),
                                 deps=[em.tok("t_evac"), war] + ldw if j == 0 else None, inc="t")
                ev1 = em.emit("vector", lambda e, bank=bank, i=i: e.tensor_copy(
                    v_tm[:, (2 * i) * V0:(2 * i + 1) * V0], bank[0:64, 0:V0]), deps=[mm], inc="v")
                ev2 = em.emit("vector", lambda e, bank=bank, i=i: e.tensor_copy(
                    v_tm[:, (2 * i + 1) * V0:(2 * i + 2) * V0], bank[64:128, 0:V0]), inc="v")
                ps.done(bi, ev2)
            for (wsb_, dstf, bias) in ((w_qk, qk_f, None), (w_bd, bd_f, c_bbd)):
                bi, bank, war = ps.take()
                mm = None
                for j in range(ND):
                    mm = em.emit("tensor", lambda e, bank=bank, wsb_=wsb_, j=j, st=(j == 0), sp=(j == ND - 1):
                                 e.matmul(bank[:, 0:TT_], wsb_[:, j * 128:(j + 1) * 128],
                                          xnf[:, j * 512: j * 512 + TT_], start=st, stop=sp),
                                 deps=[em.tok("t_evac"), war], inc="t")
                if bias is None:
                    ev = em.emit("vector", lambda e, bank=bank, dstf=dstf, g=g: e.tensor_copy(
                        dstf[:, g * TT_:(g + 1) * TT_], bank[:, 0:TT_]), deps=[mm], inc="v")
                else:
                    ev = em.emit("vector", lambda e, bank=bank, dstf=dstf, g=g, bias=bias: e.tensor_scalar(
                        dstf[:, g * TT_:(g + 1) * TT_], bank[:, 0:TT_], bias[:], None, op0=OP.add),
                        deps=[mm], inc="v")
                ps.done(bi, ev)
                xnf_free = mm
    em.barrier()
    phase_reset()

    if phases >= 2:
        sp_f = sb("p2_sp", (128, Tc), life="phase")
        lam = sb("p2_lam", (64, Tc), life="phase")
        qt_all = sb("p2_qt", (64, Tc), life="phase")
        am_all = sb("p2_am", (64, Tc), life="phase")
        U_all = sb("p2_U", (64, NC0 * V0), life="phase")
        ec_all = sb("p2_ec", (64, NC0), life="phase")
        e1t = sb("p2_e1", (64, C0), life="phase"); e2t = sb("p2_e2", (64, C0), life="phase")
        kdt = sb("p2_kd", (64, C0), life="phase"); ktt = sb("p2_kt", (64, C0), life="phase"); kht = sb("p2_kh", (64, C0), life="phase")
        khT = sb("p2_khT", (C0, 64), life="phase")
        S_a = sb("p2_Sa", (64, V0), life="phase"); S_b = sb("p2_Sb", (64, V0), life="phase")
        S_in = sb("p2_Sin", (64, V0), life="phase")
        ones64 = c_onest[0:64, 0:C0]

        s1t = em.emit("scalar", lambda e: e.activation(
            sp_f[:], bd_f[:], AF.Exp, scale=c_pm1[:]), deps=[em.tok("v")], inc="s")
        v1t = em.emit("vector", lambda e: e.tensor_scalar(
            sp_f[:], sp_f[:], 1.0, None, op0=OP.add), deps=[s1t], inc="v")
        s2t = em.emit("scalar", lambda e: e.activation(
            sp_f[:], sp_f[:], AF.Ln), deps=[v1t], inc="s")
        sc_tok = s2t
        for c in range(NC0):
            sc_tok = em.emit("vector", lambda e, c=c: e.tensor_tensor_scan(
                lam[:, c * C0:(c + 1) * C0], ones64, sp_f[0:64, c * C0:(c + 1) * C0],
                0.0, op0=OP.mult, op1=OP.add), deps=[s2t], inc="v")

        prev_vk = None
        for c in range(NC0):
            cs = slice(c * C0, (c + 1) * C0)
            e1tok = em.emit("scalar", lambda e, cs=cs: e.activation(
                e1t[:], lam[:, cs], AF.Exp, scale=-1.0), deps=[sc_tok, prev_vk], inc="s")
            e2tok = em.emit("scalar", lambda e, cs=cs: e.activation(
                e2t[:], lam[:, cs], AF.Exp), inc="s")
            vq = em.emit("vector", lambda e, cs=cs: e.tensor_tensor(
                qt_all[:, cs], qk_f[0:64, cs], e1t[:], op=OP.mult), deps=[e1tok], inc="v")
            em.emit("vector", lambda e, c=c: e.tensor_copy(
                ec_all[:, c:c + 1], e1t[:, C0 - 1:C0]), inc="v")
            em.emit("vector", lambda e, cs=cs: e.tensor_tensor(
                kdt[:], qk_f[64:128, cs], sp_f[64:128, cs], op=OP.mult), deps=[e2tok], inc="v")
            em.emit("vector", lambda e: e.tensor_tensor(
                ktt[:], kdt[:], e2t[:], op=OP.mult), inc="v")
            vk = em.emit("vector", lambda e: e.tensor_scalar(
                kht[:], ktt[:], e1t[:, C0 - 1:C0], None, op0=OP.mult), inc="v")
            prev_vk = vk
            bi, bank, war = ps.take()
            ta = em.emit("tensor", lambda e, bank=bank, cs=cs: e.matmul(
                bank[0:C0, 0:C0], ktt[:], qt_all[:, cs], start=True, stop=True),
                deps=[vk, war], inc="t")
            va = em.emit("vector", lambda e, bank=bank, cs=cs: e.tensor_tensor(
                am_all[:, cs], bank[0:C0, 0:C0], c_mask0[:], op=OP.mult),
                deps=[ta], inc="v")
            ps.done(bi, va)
            bi, bank, war = ps.take()
            tt = em.emit("tensor", lambda e, bank=bank: e.transpose(
                bank[0:C0, 0:64], kht[:], c_ident[0:64, 0:64]), deps=[vk, war], inc="t")
            vt_ = em.emit("vector", lambda e, bank=bank: e.tensor_copy(
                khT[:], bank[0:C0, 0:64]), deps=[tt], inc="v")
            ps.done(bi, vt_)
            bi, bank, war = ps.take()
            tu = em.emit("tensor", lambda e, bank=bank, c=c: e.matmul(
                bank[0:64, 0:V0], khT[:], v_tm[:, c * V0:(c + 1) * V0], start=True, stop=True),
                deps=[vt_, war], inc="t")
            vu = em.emit("vector", lambda e, bank=bank, c=c: e.tensor_copy(
                U_all[:, c * V0:(c + 1) * V0], bank[0:64, 0:V0]), deps=[tu], inc="v")
            ps.done(bi, vu)
        vtok = em.emit("vector", lambda e: e.memset(S_a[:], 0.0), inc="v")
        for c in range(NC0):
            vtok = em.emit("vector", lambda e, c=c: e.scalar_tensor_tensor(
                S_a[:], S_a[:], ec_all[:, c:c + 1], U_all[:, c * V0:(c + 1) * V0],
                op0=OP.mult, op1=OP.add), inc="v")
        vc_ = em.emit("vector", lambda e: e.tensor_scalar(
            S_b[:], S_a[:], c_fe128[0:64, :], None, op0=OP.mult), deps=[vtok], inc="v")
        d1 = em.dma("gpsimd", cc0_i[:], S_b[:], deps=[vc_], stream="d_cc")
        cct = em.emit("gpsimd", lambda e: e.collective_compute(
            "AllReduce", OP.add, replica_groups=rg, ins=[cc0_i.ap().opt()],
            outs=[cc0_o.ap().opt()]), deps=[d1], inc="cc", add=1)
        d2 = em.dma("gpsimd", S_in[:], cc0_o[:], deps=[cct], stream="d_cc")
        vi = em.emit("vector", lambda e: e.tensor_scalar(
            S_in[:], S_in[:], c_fo128[0:64, :], None, op0=OP.mult), deps=[d2], inc="v")
        vtok = em.emit("vector", lambda e: e.tensor_copy(S_a[:], S_in[:]), deps=[vi], inc="v")
        cur = S_a
        for c in range(NC0):
            cs = slice(c * C0, (c + 1) * C0)
            bi, bank, war = ps.take()
            m1 = em.emit("tensor", lambda e, bank=bank, c=c, cs=cs: e.matmul(
                bank[0:V0, 0:C0], v_tm[:, c * V0:(c + 1) * V0], am_all[:, cs],
                start=True, stop=False), deps=[vtok, war], inc="t")
            m2 = em.emit("tensor", lambda e, bank=bank, cs=cs, cur=cur: e.matmul(
                bank[0:V0, 0:C0], cur[:], qt_all[:, cs], start=False, stop=True), inc="t")
            vo = em.emit("vector", lambda e, bank=bank, cs=cs: e.tensor_copy(
                o_fm[:, cs], bank[0:V0, 0:C0]), deps=[m2], inc="v")
            ps.done(bi, vo)
            nxt = S_b if cur is S_a else S_a
            vtok = em.emit("vector", lambda e, c=c, cur=cur, nxt=nxt: e.scalar_tensor_tensor(
                nxt[:], cur[:], ec_all[:, c:c + 1], U_all[:, c * V0:(c + 1) * V0],
                op0=OP.mult, op1=OP.add), deps=[vo], inc="v")
            cur = nxt
        if debug:
            em.dma("gpsimd", ofm_dbg[:], o_fm[:], deps=[em.tok("v")], stream="d_st")
            em.dma("gpsimd", qkf_dbg[:], qk_f[:], deps=[em.tok("v")], stream="d_st")
            em.dma("gpsimd", bdf_dbg[:], bd_f[:], deps=[em.tok("v")], stream="d_st")
            em.dma("gpsimd", vtm_dbg[:], v_tm[:], deps=[em.tok("v")], stream="d_st")
            em.dma("gpsimd", spf_dbg[:], sp_f[:], deps=[em.tok("v")], stream="d_st")
            em.dma("gpsimd", qt_dbg[:], qt_all[:], deps=[em.tok("v")], stream="d_st")
            em.dma("gpsimd", am_dbg[:], am_all[:], deps=[em.tok("v")], stream="d_st")
            em.dma("gpsimd", U_dbg[:], U_all[:], deps=[em.tok("v")], stream="d_st")
            em.dma("gpsimd", lam_dbg[:], lam[:], deps=[em.tok("v")], stream="d_st")
    em.barrier()
    phase_reset()

    if phases >= 3:
        xqf = sb("p3_xqf", (128, 2 * ND * 128), BF16, life="phase")
        hqf = sb("p3_hqf", (128, NDF * 128), BF16, life="phase")
        IBW = min(DFF, 4096)
        ibuf = sb("p3_ibuf", (128, IBW), I32, life="phase")
        xib = ibuf
        xqb = sb("p3_xqb", (128, D), BF16, life="phase")
        xnt3 = sb("p3_xnt", (128, D), life="phase")
        ht = sb("p3_ht", (128, DFF), life="phase")
        hqb = sb("p3_hqb", (128, DFF), BF16, life="phase")
        wq1 = sb("p3_wq1", (128, 2 * 512), BF16, life="phase")
        wq2 = sb("p3_wq2", (128, 2 * 512), BF16, life="phase")
        wq3 = sb("p3_wq3", (128, 2 * 512), BF16, life="phase")
        g0t = sb("p3_g0", (128, D), life="phase")
        xt3 = sb("p3_xt3", (128, D), life="phase")
        x1t = sb("p3_x1", (128, D), life="phase")
        utmp = sb("p3_utmp", (128, 512), life="phase")
        sc_in = sb("p3_scin", (128, 1), life="phase")
        colA = sb("p3_cA", (128, 1), life="phase"); colB = sb("p3_cB", (128, 1), life="phase")
        colC = sb("p3_cC", (128, 1), life="phase"); colAm = sb("p3_cAm", (128, 1), life="phase")
        sc1 = sb("p3_sc1", (128, 1), life="phase"); sc2 = sb("p3_sc2", (128, 1), life="phase"); sc3 = sb("p3_sc3", (128, 1), life="phase")
        smv = sb("p3_sm", (128, 1), life="phase"); sqv = sb("p3_sq", (128, 1), life="phase")
        sq1 = sb("p3_sq1", (128, 1), life="phase")
        mc3 = sb("p3_m", (128, 1), life="phase"); vc3 = sb("p3_v", (128, 1), life="phase"); rs3 = sb("p3_rstd", (128, 1), life="phase")

        def quant_cols(src_t, width, dep, who):
            nseg = (width + IBW - 1) // IBW
            em.emit("vector", lambda e: e.memset(sqv[:], 0.0), deps=[dep], inc="v")
            s1 = None
            for g_ in range(nseg):
                a0, a1_ = g_ * IBW, min((g_ + 1) * IBW, width)
                s1 = em.emit("scalar", lambda e, a0=a0, a1_=a1_: e.activation(
                    ibuf[:, 0:a1_ - a0], src_t[:, a0:a1_], AF.Square, accum_out=sq1[:]),
                    deps=[dep], inc="s")
                s1 = em.emit("vector", lambda e: e.tensor_tensor(
                    sqv[:], sqv[:], sq1[:], op=OP.add), deps=[s1, em.tok("v")], inc="v")
            src_ap = src_t[:, 0:width]
            r1 = em.emit("vector", lambda e: e.reduce_max(
                colAm[:], src_ap, axis=AX.X, apply_absolute_value=True), deps=[dep], inc="v")
            t_ = em.emit("vector", lambda e, width=width: e.tensor_scalar(
                colB[:], sqv[:], 1.0 / width, 1e-6, op0=OP.mult, op1=OP.add),
                deps=[em.force(s1)], inc="v")
            s2 = em.emit("scalar", lambda e: e.activation(
                colB[:], colB[:], AF.Sqrt), deps=[em.tok("v")], inc="s")
            t_ = em.emit("vector", lambda e: e.reciprocal(colB[:], colB[:]), deps=[s2], inc="v")
            t_ = em.emit("vector", lambda e: e.scalar_tensor_tensor(
                colA[:], colAm[:], 1.0, colB[:], op0=OP.mult, op1=OP.mult),
                deps=[em.force(r1), em.force(t_)], inc="v")
            t_ = em.emit("vector", lambda e: e.tensor_scalar(
                colA[:], colA[:], 1e-5, None, op0=OP.max), deps=[em.force(t_)], inc="v")
            t_ = em.emit("vector", lambda e: e.reciprocal(colC[:], colA[:]),
                         deps=[em.force(t_)], inc="v")
            t_ = em.emit("vector", lambda e: e.tensor_scalar(
                colA[:], colA[:], 1.0 / 127.0, None, op0=OP.mult),
                deps=[em.force(t_)], inc="v")
            t_ = em.emit("vector", lambda e: e.scalar_tensor_tensor(
                colC[:], colC[:], 127.0, colB[:], op0=OP.mult, op1=OP.mult),
                deps=[em.force(t_)], inc="v")
            return em.tok("v")

        slot_war1 = [None, None]
        slot_war2 = [None, None]
        slot_war3 = [None, None]
        xqf_war = [None, None]
        prev_iter = []
        hqf_war = None
        for i in range(NTC):
            tl = em.dma("sync", xnt3[:], xn0_s[i * 128:(i + 1) * 128, :],
                        deps=prev_iter, stream="d_x")
            qt_ = quant_cols(xnt3, D, tl, "x")
            vi1 = em.emit("vector", lambda e: e.tensor_scalar(
                xib[:, 0:D], xnt3[:], colC[:], None, op0=OP.mult), deps=[qt_], inc="v")
            vi2 = em.emit("vector", lambda e: e.tensor_scalar(
                xqb[:], xib[:, 0:D], 0.0, None, op0=OP.add), inc="v")
            em.emit("vector", lambda e: e.scalar_tensor_tensor(
                sc1[:], colA[:], 1.0, c_mclip[:, 0:1], op0=OP.mult, op1=OP.mult),
                deps=[em.tok("v")], inc="v")
            vs2 = em.emit("vector", lambda e: e.scalar_tensor_tensor(
                sc2[:], colA[:], 1.0, c_mclip[:, 1:2], op0=OP.mult, op1=OP.mult),
                deps=[em.tok("v")], inc="v")
            if debug:
                em.dma("gpsimd", xqi_dbg[i * 128:(i + 1) * 128, :], xqb[:],
                       deps=[vi2], stream="d_st")
            xsl = (i % 2) * ND * 128
            trt = []
            for j in range(ND):
                trt.append(em.dmat("scalar", xqf[:, xsl + j * 128:xsl + (j + 1) * 128],
                                   xqb[:, j * 128:(j + 1) * 128],
                                   deps=[vi2, xqf_war[i % 2]], stream="d_tr"))
            for n in range(DFF // 512):
                bi, bank, war = ps.take()
                bi2, bank2, war2 = ps.take()
                mG = mU = None
                for j in range(ND):
                    sl2 = j % 2
                    s0_, s1_ = sl2 * 512, sl2 * 512 + 512
                    l1 = em.dma("sync", wq1[:, s0_:s1_],
                                w1q_s[j * 128:(j + 1) * 128, n * 512:(n + 1) * 512],
                                deps=[slot_war1[sl2]], stream=f"d_wq1{sl2}")
                    mG = em.emit("tensor", lambda e, bank=bank, j=j, xsl=xsl, s0_=s0_, s1_=s1_, st=(j == 0), sp=(j == ND - 1):
                                 e.matmul(bank[:, 0:512],
                                          xqf[:, xsl + j * 128:xsl + (j + 1) * 128],
                                          wq1[:, s0_:s1_], start=st, stop=sp),
                                 deps=[l1, trt[-1], war], inc="t")
                    slot_war1[sl2] = mG
                for j in range(ND):
                    sl2 = j % 2
                    s0_, s1_ = sl2 * 512, sl2 * 512 + 512
                    l2 = em.dma("sync", wq2[:, s0_:s1_],
                                w2q_s[j * 128:(j + 1) * 128, n * 512:(n + 1) * 512],
                                deps=[slot_war2[sl2]], stream=f"d_wq2{sl2}")
                    mU = em.emit("tensor", lambda e, bank2=bank2, j=j, xsl=xsl, s0_=s0_, s1_=s1_, st=(j == 0), sp=(j == ND - 1):
                                 e.matmul(bank2[:, 0:512],
                                          xqf[:, xsl + j * 128:xsl + (j + 1) * 128],
                                          wq2[:, s0_:s1_], start=st, stop=sp),
                                 deps=[l2, war2], inc="t")
                    slot_war2[sl2] = mU
                    xqf_war[i % 2] = mU
                sg = em.emit("scalar", lambda e, bank=bank, n=n: e.activation(
                    ht[:, n * 512:(n + 1) * 512], bank[:, 0:512], AF.Silu, scale=sc1[:]),
                    deps=[mG, vs2], inc="s")
                vu_ = em.emit("vector", lambda e, bank2=bank2: e.tensor_scalar(
                    utmp[:], bank2[:, 0:512], sc2[:], None, op0=OP.mult),
                    deps=[mU], inc="v")
                hv = em.emit("vector", lambda e, n=n: e.tensor_tensor(
                    ht[:, n * 512:(n + 1) * 512], ht[:, n * 512:(n + 1) * 512],
                    utmp[:], op=OP.mult), deps=[sg, vu_], inc="v")
                ps.done(bi, hv); ps.done(bi2, hv)
            if debug:
                em.dma("gpsimd", ht_dbg[i * 128:(i + 1) * 128, :], ht[:, 0:DFF],
                       deps=[em.tok("v")], stream="d_st")
            qh_ = quant_cols(ht, DFF, em.tok("v"), "h")
            hv2 = None
            for g_ in range((DFF + IBW - 1) // IBW):
                a0, a1_ = g_ * IBW, min((g_ + 1) * IBW, DFF)
                em.emit("vector", lambda e, a0=a0, a1_=a1_: e.tensor_scalar(
                    ibuf[:, 0:a1_ - a0], ht[:, a0:a1_], colC[:], None, op0=OP.mult),
                    deps=[qh_], inc="v")
                hv2 = em.emit("vector", lambda e, a0=a0, a1_=a1_: e.tensor_scalar(
                    hqb[:, a0:a1_], ibuf[:, 0:a1_ - a0], 0.0, None, op0=OP.add), inc="v")
            em.emit("vector", lambda e: e.scalar_tensor_tensor(
                sc3[:], colA[:], 1.0, c_mclip[:, 2:3], op0=OP.mult, op1=OP.mult),
                deps=[em.tok("v")], inc="v")
            sc3tok = em.tok("v")
            if debug:
                em.dma("gpsimd", hqi_dbg[i * 128:(i + 1) * 128, :], hqb[:],
                       deps=[hv2], stream="d_st")
            trh = []
            for j in range(NDF):
                trh.append(em.dmat("scalar", hqf[:, j * 128:(j + 1) * 128],
                                   hqb[:, j * 128:(j + 1) * 128], deps=[hv2, hqf_war],
                                   stream="d_tr"))
            tlx = em.dma("sync", xt3[:], x_in[i * 128:(i + 1) * 128, :], stream="d_x2")
            for n in range(D // 512):
                bi, bank, war = ps.take()
                mD = None
                for j in range(NDF):
                    sl3 = j % 2
                    l3 = em.dma("sync", wq3[:, sl3 * 512:sl3 * 512 + 512],
                                w3q_s[j * 128:(j + 1) * 128, n * 512:(n + 1) * 512],
                                deps=[slot_war3[sl3]], stream=f"d_wq3{sl3}")
                    mD = em.emit("tensor", lambda e, bank=bank, j=j, sl3=sl3, st=(j == 0), sp=(j == NDF - 1):
                                 e.matmul(bank[:, 0:512], hqf[:, j * 128:(j + 1) * 128],
                                          wq3[:, sl3 * 512:sl3 * 512 + 512],
                                          start=st, stop=sp),
                                 deps=[l3, trh[-1], war], inc="t")
                    slot_war3[sl3] = mD
                hqf_war = mD
                sgd = em.emit("scalar", lambda e, bank=bank, n=n: e.activation(
                    g0t[:, n * 512:(n + 1) * 512], bank[:, 0:512], AF.Silu, scale=sc3[:]),
                    deps=[mD, sc3tok], inc="s")
                ps.done(bi, sgd)
                bi2, bank2, war2 = ps.take()
                ms = em.emit("tensor", lambda e, bank2=bank2, i=i, n=n: e.matmul(
                    bank2[:, 0:512], o_fm[:, i * 128:(i + 1) * 128],
                    wo0_sb[:, n * 512:(n + 1) * 512], start=True, stop=True),
                    deps=[war2, ld_wo], inc="t")
                vx = em.emit("vector", lambda e, bank2=bank2, n=n: e.tensor_tensor(
                    x1t[:, n * 512:(n + 1) * 512], bank2[:, 0:512],
                    g0t[:, n * 512:(n + 1) * 512], op=OP.mult), deps=[ms, sgd, tlx], inc="v")
                vx2 = em.emit("vector", lambda e, n=n: e.tensor_tensor(
                    x1t[:, n * 512:(n + 1) * 512], x1t[:, n * 512:(n + 1) * 512],
                    xt3[:, n * 512:(n + 1) * 512], op=OP.add), inc="v")
                ps.done(bi2, vx2)
            x1done = em.tok("v")
            em.dma("gpsimd", x1_s[i * 128:(i + 1) * 128, :], x1t[:], deps=[x1done], stream="d_st")
            r1 = em.emit("vector", lambda e: e.reduce_sum(smv[:], x1t[:], axis=AX.X),
                         deps=[x1done], inc="v")
            s1_ = em.emit("scalar", lambda e: e.activation(
                g0t[:, 0:D], x1t[:], AF.Square, accum_out=sqv[:]), deps=[x1done], inc="s")
            t_ = em.emit("vector", lambda e: e.tensor_scalar(
                mc3[:], smv[:], 1.0 / D, None, op0=OP.mult), deps=[em.force(r1)], inc="v")
            t_ = em.emit("vector", lambda e: e.tensor_scalar(
                vc3[:], sqv[:], 1.0 / D, None, op0=OP.mult),
                deps=[s1_, em.force(t_)], inc="v")
            t_ = em.emit("vector", lambda e: e.scalar_tensor_tensor(
                vc3[:], mc3[:], mc3[:], vc3[:], op0=OP.mult, op1=OP.subtract),
                deps=[em.force(t_)], inc="v")
            t_ = em.emit("vector", lambda e: e.tensor_scalar(
                vc3[:], vc3[:], -1.0, 1e-5, op0=OP.mult, op1=OP.add),
                deps=[em.force(t_)], inc="v")
            sv = em.emit("scalar", lambda e: e.activation(rs3[:], vc3[:], AF.Sqrt),
                         deps=[em.tok("v")], inc="s")
            rv3 = em.emit("vector", lambda e: e.reciprocal(rs3[:], rs3[:]), deps=[sv], inc="v")
            xn1done = em.emit("vector", lambda e: e.tensor_scalar(
                xnt3[:], x1t[:], mc3[:], rs3[:], op0=OP.subtract, op1=OP.mult),
                deps=[em.force(rv3)], inc="v")
            prev_dma = None
            for j in range(ND):
                bi, bank, war = ps.take()
                tt = em.emit("tensor", lambda e, bank=bank, j=j: e.transpose(
                    bank[:, 0:128], xnt3[:, j * 128:(j + 1) * 128], c_ident[:]),
                    deps=[xn1done, war], inc="t")
                ev = em.emit("vector", lambda e, bank=bank, j=j: e.tensor_copy(
                    g0t[:, (j % (D // 128)) * 128:(j % (D // 128)) * 128 + 128], bank[:, 0:128]),
                    deps=[tt, prev_dma], inc="v")
                prev_dma = em.dma("gpsimd", xn1f_s[j * 128:(j + 1) * 128, i * 128:(i + 1) * 128],
                                  g0t[:, (j % (D // 128)) * 128:(j % (D // 128)) * 128 + 128],
                                  deps=[ev], stream="d_st")
                ps.done(bi, prev_dma)
            prev_iter = [em.tok("d_st"), em.tok("d_tr"), em.tok("s"), em.tok("v")]
    em.barrier()
    phase_reset()
    group_reset()

    NMZ = VD1 // 128
    NQK = (2 * KD1) // 128
    NQ = KD1 // 128
    gb2 = sb("p4_gb2", (64, Tc), life="group")
    GbT_all = sb("p4_GbT", (128, NC1 * 64), life="group")
    GCb_all = sb("p4_GCb", (128, NC1 * NV), life="group")
    gamT_all = sb("p4_gamT", (128, NC1 * NV), life="group")
    gCb_all = sb("p4_gCb", (128, NC1 * NV), life="group")
    ratT_all = sb("p4_ratT", (128, NC1 * NV), life="group")
    if phases >= 4:
        ba_f = sb("p4_ba", (64, Tc), life="phase")
        xb4 = sb("p4_xb", (128, ND * 512), life="phase")
        wb4 = sb("p4_wb", (128, ND * 128), life="phase")
        ev4 = sb("p4_ev", (128, 512), life="phase")
        halo = sb("p4_halo", (128, NM * (KS - 1)), life="phase")
        halo_in = sb("p4_halo_in", (128, NM * (KS - 1)), life="phase")
        TT4 = min(512, Tc)
        NT4 = Tc // TT4
        MBLK = NM + NMZ + 1
        xld = None
        wwar = None
        ev4_war = None
        hv4 = em.emit("vector", lambda e: e.memset(halo[:], 0.0), inc="v")
        for n in range(NT4):
            xld_list = []
            for j in range(ND):
                xld_list.append(em.dma("sync", xb4[:, j * TT4:(j + 1) * TT4],
                                       xn1f_s[j * 128:(j + 1) * 128, n * TT4:(n + 1) * TT4],
                                       deps=[xld], stream="d_x"))
            for m in range(MBLK):
                if m < NM:
                    wsrc, wcol = WqkvT, m * 128
                    wwid = 128
                elif m < NM + NMZ:
                    wsrc, wcol = WzT, (m - NM) * 128
                    wwid = 128
                else:
                    wsrc, wcol = WbaT, 0
                    wwid = 64
                wld = []
                for j in range(ND):
                    wld.append(em.dma("sync", wb4[:, j * 128:j * 128 + wwid],
                                      wsrc[j * 128:(j + 1) * 128, wcol:wcol + wwid],
                                      deps=[wwar], stream="d_w"))
                bi, bank, war = ps.take()
                mmw = None
                for j in range(ND):
                    mmw = em.emit("tensor", lambda e, bank=bank, j=j, wwid=wwid, st=(j == 0), sp=(j == ND - 1):
                                  e.matmul(bank[0:wwid, 0:TT4], wb4[:, j * 128:j * 128 + wwid],
                                           xb4[:, j * TT4:(j + 1) * TT4], start=st, stop=sp),
                                  deps=[wld[-1], xld_list[-1], war], inc="t")
                wwar = mmw
                if m < NM + NMZ:
                    vv = em.emit("vector", lambda e, bank=bank, wwid=wwid: e.tensor_copy(
                        ev4[:wwid, 0:TT4], bank[0:wwid, 0:TT4]), deps=[mmw, ev4_war], inc="v")
                    dst = qkvp_s if m < NM else z1_s
                    row = m * 128 if m < NM else (m - NM) * 128
                    sd = em.dma("gpsimd", dst[row:row + 128, n * TT4:(n + 1) * TT4],
                                ev4[:, 0:TT4], deps=[vv], stream="d_st")
                    ev4_war = sd
                    if m < NM and n == NT4 - 1:
                        vv2 = em.emit("vector", lambda e, bank=bank, m=m: e.tensor_scalar(
                            halo[:, m * (KS - 1):(m + 1) * (KS - 1)],
                            bank[:, TT4 - (KS - 1):TT4], c_fe128[:], None, op0=OP.mult),
                            deps=[mmw, hv4], inc="v")
                        ps.done(bi, vv2)
                    else:
                        ps.done(bi, sd)
                else:
                    vv = em.emit("vector", lambda e, bank=bank, n=n: e.tensor_copy(
                        ba_f[:, n * TT4:(n + 1) * TT4], bank[0:64, 0:TT4]),
                        deps=[mmw], inc="v")
                    ps.done(bi, vv)
            xld = em.tok("t")
        d1 = em.dma("gpsimd", cch_i[:], halo[:], deps=[em.tok("v")], stream="d_cc")
        cht = em.emit("gpsimd", lambda e: e.collective_compute(
            "AllReduce", OP.add, replica_groups=rg, ins=[cch_i.ap().opt()],
            outs=[cch_o.ap().opt()]), deps=[d1], inc="cc", add=1)
        d2 = em.dma("gpsimd", halo_in[:], cch_o[:], deps=[cht], stream="d_cc")
        vhi = em.emit("vector", lambda e: e.tensor_scalar(
            halo_in[:], halo_in[:], c_fo128[:], None, op0=OP.mult), deps=[d2], inc="v")
        em.emit("vector", lambda e: e.memset(gb2[:], 0.0), inc="v")
        sB = em.emit("scalar", lambda e: e.activation(
            gb2[32:32 + NV, :], ba_f[0:NV, :], AF.Sigmoid), deps=[em.tok("v")], inc="s")
        sE = em.emit("scalar", lambda e: e.activation(
            gb2[0:NV, :], ba_f[32:32 + NV, :], AF.Exp, bias=c_dtb[32:32 + NV, :]), inc="s")
        vE = em.emit("vector", lambda e: e.tensor_scalar(
            gb2[0:NV, :], gb2[0:NV, :], 1.0, None, op0=OP.add), deps=[sE], inc="v")
        sL = em.emit("scalar", lambda e: e.activation(
            gb2[0:NV, :], gb2[0:NV, :], AF.Ln), deps=[vE], inc="s")
        vG = em.emit("vector", lambda e: e.tensor_scalar(
            gb2[0:NV, :], gb2[0:NV, :], c_negexpA[:], None, op0=OP.mult), deps=[sL], inc="v")
        sc_t = vG
        for c in range(NC1):
            sc_t = em.emit("vector", lambda e, c=c: e.tensor_tensor_scan(
                gb2[0:NV, c * C1:(c + 1) * C1], c_onest[0:NV, 0:C1],
                gb2[0:NV, c * C1:(c + 1) * C1], 0.0, op0=OP.mult, op1=OP.add),
                deps=[vG], inc="v")
        for c in range(NC1):
            bi, bank, war = ps.take()
            ttk = em.emit("tensor", lambda e, bank=bank, c=c: e.transpose(
                bank[:, 0:64], gb2[:, c * C1:(c + 1) * C1], c_ident[0:64, 0:64]),
                deps=[sc_t, sB, war], inc="t")
            vtk = em.emit("vector", lambda e, bank=bank, c=c: e.tensor_copy(
                GbT_all[:, c * 64:(c + 1) * 64], bank[:, 0:64]),
                deps=[ttk], inc="v")
            ps.done(bi, vtk)
            bi, bank, war = ps.take()
            tgc = em.emit("tensor", lambda e, bank=bank, c=c: e.matmul(
                bank[:, 0:NV], c_lastrow[:], GbT_all[:, c * 64:c * 64 + NV],
                start=True, stop=True), deps=[vtk, war], inc="t")
            vgc = em.emit("vector", lambda e, bank=bank, c=c: e.tensor_copy(
                GCb_all[:, c * NV:(c + 1) * NV], bank[:, 0:NV]), deps=[tgc], inc="v")
            ps.done(bi, vgc)
            em.emit("scalar", lambda e, c=c: e.activation(
                gamT_all[:, c * NV:(c + 1) * NV], GbT_all[:, c * 64:c * 64 + NV],
                AF.Exp), deps=[vtk], inc="s")
            em.emit("scalar", lambda e, c=c: e.activation(
                gCb_all[:, c * NV:(c + 1) * NV], GCb_all[:, c * NV:(c + 1) * NV],
                AF.Exp), deps=[vgc], inc="s")
            vr = em.emit("vector", lambda e, c=c: e.tensor_tensor(
                ratT_all[:, c * NV:(c + 1) * NV], GCb_all[:, c * NV:(c + 1) * NV],
                GbT_all[:, c * 64:c * 64 + NV], op=OP.subtract),
                deps=[em.force(vgc)], inc="v")
            em.emit("scalar", lambda e, c=c: e.activation(
                ratT_all[:, c * NV:(c + 1) * NV], ratT_all[:, c * NV:(c + 1) * NV],
                AF.Exp), deps=[vr], inc="s")
        qcp = sb("p4_qcp", (128, Tc + KS - 1), life="phase")
        qcv = sb("p4_qcv", (128, Tc), life="phase")
        qcs = sb("p4_qcs", (128, Tc), life="phase")
        qsq = sb("p4_qsq", (128, Tc), life="phase")
        rq2 = sb("p4_rq2", (2, 512), life="phase")
        prevm = [vhi]
        for m in range(NM):
            vp = em.emit("vector", lambda e, m=m: e.tensor_copy(
                qcp[:, 0:KS - 1], halo_in[:, m * (KS - 1):(m + 1) * (KS - 1)]),
                deps=[em.force(vhi)] + prevm, inc="v")
            lq = em.dma("sync", qcp[:, KS - 1:KS - 1 + Tc],
                        qkvp_s[m * 128:(m + 1) * 128, :], deps=[em.tok("d_st")] + prevm,
                        stream="d_x")
            cv0 = em.emit("vector", lambda e, m=m: e.tensor_scalar(
                qcv[:], qcp[:, 0:Tc], c_convw[:, m * KS:m * KS + 1], None, op0=OP.mult),
                deps=[lq, em.force(vp)], inc="v")
            for sft in range(1, KS):
                cv0 = em.emit("vector", lambda e, m=m, sft=sft: e.scalar_tensor_tensor(
                    qcv[:], qcp[:, sft:sft + Tc], c_convw[:, m * KS + sft:m * KS + sft + 1],
                    qcv[:], op0=OP.mult, op1=OP.add), inc="v")
            sq_ = em.emit("scalar", lambda e: e.activation(qcs[:], qcv[:], AF.Silu),
                          deps=[cv0], inc="s")
            prevm = [sq_]
            if m < NQK:
                ssq = em.emit("scalar", lambda e: e.activation(qsq[:], qcs[:], AF.Square),
                              deps=[sq_], inc="s")
                for n2 in range(Tc // 512):
                    bi, bank, war = ps.take()
                    tn = em.emit("tensor", lambda e, bank=bank, n2=n2: e.matmul(
                        bank[0:2, 0:512], c_blk64[:, 0:2], qsq[:, n2 * 512:(n2 + 1) * 512],
                        start=True, stop=True), deps=[ssq, war], inc="t")
                    vn = em.emit("vector", lambda e, bank=bank: e.tensor_scalar(
                        rq2[:], bank[0:2, 0:512], 1e-6, None, op0=OP.add),
                        deps=[tn], inc="v")
                    ps.done(bi, vn)
                    sn = em.emit("scalar", lambda e: e.activation(rq2[:], rq2[:], AF.Sqrt),
                                 deps=[vn], inc="s")
                    vn2 = em.emit("vector", lambda e: e.reciprocal(rq2[:], rq2[:]),
                                  deps=[sn], inc="v")
                    if m < NQ:
                        vn2 = em.emit("vector", lambda e: e.tensor_scalar(
                            rq2[:], rq2[:], float(DK) ** -0.5, None, op0=OP.mult), inc="v")
                    bi, bank, war = ps.take()
                    tb = em.emit("tensor", lambda e, bank=bank: e.matmul(
                        bank[:, 0:512], c_exp64[:, 0:128], rq2[:], start=True, stop=True),
                        deps=[vn2, war], inc="t")
                    vb_ = em.emit("vector", lambda e, bank=bank, n2=n2: e.tensor_tensor(
                        qcs[:, n2 * 512:(n2 + 1) * 512], qcs[:, n2 * 512:(n2 + 1) * 512],
                        bank[:, 0:512], op=OP.mult), deps=[tb], inc="v")
                    ps.done(bi, vb_)
                prevm = [em.tok("v")]
            prevm.append(em.dma("gpsimd", qkvc_s[m * 128:(m + 1) * 128, :], qcs[:],
                                deps=prevm, stream="d_st"))
    em.barrier()
    phase_reset()

    NVB = VD1 // 128
    og = sb("p5_og", (128, NVB * Tc), life="group")
    if phases >= 5:
        qh_sb = sb("p5_qh", (64, Tc), life="phase"); kh_sb = sb("p5_kh", (64, Tc), life="phase")
        vh_sb = sb("p5_vh", (64, Tc), life="phase")
        zh_sb = sb("p5_zh", (64, C1), life="phase")
        araw_all = sb("p5_araw", (C1, NC1 * C1), life="phase")
        mraw_all = sb("p5_mraw", (C1, NC1 * C1), life="phase")
        ktr_all = sb("p5_ktr", (C1, NC1 * 64), life="phase")
        wt_all = sb("p5_wt", (64, NC1 * C1), life="phase")
        rl_all = sb("p5_rl", (C1, NC1 * 64), life="phase")
        mt_all = sb("p5_mt", (C1, NC1 * C1), life="phase")
        kb_all = sb("p5_kb", (C1, NC1 * 64), life="phase")
        dA = sb("p5_dA", (C1, C1), life="phase"); dAT = sb("p5_dAT", (C1, C1), life="phase")
        bUT = sb("p5_bUT", (C1, C1), life="phase")
        n_sb = sb("p5_n", (C1, C1), life="phase"); nt_sb = sb("p5_nt", (C1, C1), life="phase")
        tt_a = sb("p5_tta", (C1, C1), life="phase"); tt_b = sb("p5_ttb", (C1, C1), life="phase")
        b_a = sb("p5_ba", (C1, C1), life="phase"); b_b = sb("p5_bb", (C1, C1), life="phase")
        cb_a = sb("p5_cba", (C1, C1), life="phase"); cb_b = sb("p5_cbb", (C1, C1), life="phase")
        kbg = sb("p5_kbg", (C1, 64), life="phase"); w_sb = sb("p5_w", (C1, 64), life="phase")
        vtr = sb("p5_vtr", (C1, 64), life="phase"); vb_sb = sb("p5_vb", (C1, 64), life="phase")
        mct = sb("p5_mct", (64, 64), life="phase")
        f_a = sb("p5_fa", (64, 64), life="phase"); f_b = sb("p5_fb", (64, 64), life="phase")
        s_a = sb("p5_sa", (64, 64), life="phase"); s_b = sb("p5_sb", (64, 64), life="phase")
        tmp64 = sb("p5_t64", (64, 64), life="phase")
        bgcol = sb("p5_bg", (C1, 1), life="phase")
        o12 = sb("p5_o12", (C1, 64), life="phase"); r_sb = sb("p5_r", (C1, 64), life="phase")
        ofT = sb("p5_ofT", (64, C1), life="phase"); zs_sb = sb("p5_zs", (64, C1), life="phase")
        sso = sb("p5_sso", (128, 1), life="phase"); ro = sb("p5_ro", (128, 1), life="phase")
        jnk5 = sb("p5_jnk5", (C1, 64), life="phase")
        growh = sb("p5_growh", (1, Tc), life="phase")
        browh = sb("p5_browh", (1, Tc), life="phase")
        ev5 = sb("p5_ev5", (128, 512), life="phase")

        def gcol(c, h):
            return GbT_all[:, c * 64 + h:c * 64 + h + 1]

        def bcol(c, h):
            return GbT_all[:, c * 64 + 32 + h:c * 64 + 32 + h + 1]

        for h in range(NV):
            kg = h // rep
            shr_tok = None
            if h % rep == 0:
                lq_ = em.dma("sync", qh_sb[:], qkvc_s[kg * 64:(kg + 1) * 64, :],
                             deps=[em.tok("t")], stream="d_q5")
                lk_ = em.dma("sync", kh_sb[:], qkvc_s[KD1 + kg * 64:KD1 + (kg + 1) * 64, :],
                             stream="d_k5")
                for c in range(NC1):
                    cs = slice(c * C1, (c + 1) * C1)
                    bi, bank, war = ps.take()
                    ta = em.emit("tensor", lambda e, bank=bank, cs=cs: e.matmul(
                        bank[0:C1, 0:C1], kh_sb[:, cs], kh_sb[:, cs], start=True, stop=True),
                        deps=[lk_, war], inc="t")
                    va = em.emit("vector", lambda e, bank=bank, c=c: e.tensor_copy(
                        araw_all[:, c * C1:(c + 1) * C1], bank[0:C1, 0:C1]),
                        deps=[ta], inc="v")
                    ps.done(bi, va)
                    bi, bank, war = ps.take()
                    tm_ = em.emit("tensor", lambda e, bank=bank, cs=cs: e.matmul(
                        bank[0:C1, 0:C1], kh_sb[:, cs], qh_sb[:, cs], start=True, stop=True),
                        deps=[lq_, war], inc="t")
                    vm_ = em.emit("vector", lambda e, bank=bank, c=c: e.tensor_copy(
                        mraw_all[:, c * C1:(c + 1) * C1], bank[0:C1, 0:C1]),
                        deps=[tm_], inc="v")
                    ps.done(bi, vm_)
                    bi, bank, war = ps.take()
                    tk_ = em.emit("tensor", lambda e, bank=bank, cs=cs: e.transpose(
                        bank[0:C1, 0:64], kh_sb[:, cs], c_ident[0:64, 0:64]), deps=[lk_, war], inc="t")
                    vk_ = em.emit("vector", lambda e, bank=bank, c=c: e.tensor_copy(
                        ktr_all[:, c * 64:(c + 1) * 64], bank[0:C1, 0:64]),
                        deps=[tk_], inc="v")
                    ps.done(bi, vk_)
                shr_tok = em.tok("v")
            lv_ = em.dma("sync", vh_sb[:], qkvc_s[2 * KD1 + h * 64:2 * KD1 + (h + 1) * 64, :],
                         deps=[em.tok("t")], stream="d_v5")

            lgr = em.dma("gpsimd", growh[0:1, :], gb2[h:h + 1, :],
                         deps=[em.tok("t")], stream="d_gr")
            lbr = em.dma("gpsimd", browh[0:1, :], gb2[32 + h:32 + h + 1, :], stream="d_br")
            fz = em.emit("vector", lambda e: e.memset(f_a[:], 0.0), deps=[shr_tok], inc="v")
            f_cur, f_nxt = f_a, f_b
            for c in range(NC1):
                cs = slice(c * C1, (c + 1) * C1)
                bi, bank, war = ps.take()
                tg_ = em.emit("tensor", lambda e, bank=bank, c=c: e.matmul(
                    bank[0:C1, 0:C1], c_onesr[0:1, 0:C1],
                    growh[0:1, c * C1:(c + 1) * C1], start=True, stop=True),
                    deps=[war, em.tok("v"), lgr], inc="t")
                v1_ = em.emit("vector", lambda e, bank=bank, c=c, h=h: e.tensor_scalar(
                    dA[:], bank[0:C1, 0:C1], gcol(c, h), 0.0, op0=OP.subtract, op1=OP.max),
                    deps=[tg_], inc="v")
                v2_ = em.emit("vector", lambda e, bank=bank, c=c, h=h: e.tensor_scalar(
                    dAT[:], bank[0:C1, 0:C1], gcol(c, h), 0.0, op0=OP.subtract, op1=OP.min),
                    inc="v")
                ps.done(bi, v2_)
                sa_ = em.emit("scalar", lambda e: e.activation(dA[:], dA[:], AF.Exp, scale=-1.0),
                              deps=[v1_], inc="s")
                sat = em.emit("scalar", lambda e: e.activation(dAT[:], dAT[:], AF.Exp),
                              deps=[v2_], inc="s")
                bi, bank, war = ps.take()
                tb_ = em.emit("tensor", lambda e, bank=bank, c=c: e.matmul(
                    bank[0:C1, 0:C1], c_onesr[0:1, 0:C1],
                    browh[0:1, c * C1:(c + 1) * C1], start=True, stop=True),
                    deps=[war, lbr], inc="t")
                vbu = em.emit("vector", lambda e, bank=bank: e.tensor_tensor(
                    bUT[:], bank[0:C1, 0:C1], c_uts[:], op=OP.mult), deps=[tb_], inc="v")
                ps.done(bi, vbu)
                em.emit("vector", lambda e, c=c, h=h, cs=cs: e.scalar_tensor_tensor(
                    n_sb[:], dA[:], bcol(c, h), araw_all[:, cs], op0=OP.mult, op1=OP.mult),
                    deps=[sa_], inc="v")
                em.emit("vector", lambda e: e.tensor_tensor(
                    n_sb[:], n_sb[:], c_lts[:], op=OP.mult), inc="v")
                em.emit("vector", lambda e, cs=cs: e.tensor_tensor(
                    nt_sb[:], dAT[:], araw_all[:, cs], op=OP.mult), deps=[sat], inc="v")
                em.emit("vector", lambda e: e.tensor_tensor(
                    nt_sb[:], nt_sb[:], bUT[:], op=OP.mult), inc="v")
                vI = em.emit("vector", lambda e: e.tensor_tensor(
                    tt_a[:], c_ident[:], nt_sb[:], op=OP.subtract), inc="v")
                tt_cur, tt_nxt = tt_a, tt_b
                b_cur, cb_cur = n_sb, nt_sb
                b_nxt, cb_nxt = b_a, cb_a
                kk = 1
                while 2 * kk < C1:
                    bi, bank, war = ps.take()
                    tq = em.emit("tensor", lambda e, bank=bank, cb_cur=cb_cur, b_cur=b_cur:
                                 e.matmul(bank[0:C1, 0:C1], cb_cur[:], b_cur[:],
                                          start=True, stop=True),
                                 deps=[vI, war, em.tok("v")], inc="t")
                    vq = em.emit("vector", lambda e, bank=bank, b_nxt=b_nxt: e.tensor_copy(
                        b_nxt[:], bank[0:C1, 0:C1]), deps=[tq], inc="v")
                    ps.done(bi, vq)
                    if 4 * kk < C1:
                        bi, bank, war = ps.take()
                        tq2 = em.emit("tensor", lambda e, bank=bank, cb_cur=cb_cur, b_cur=b_cur:
                                      e.matmul(bank[0:C1, 0:C1], b_cur[:], cb_cur[:],
                                               start=True, stop=True), deps=[war], inc="t")
                        vq2 = em.emit("vector", lambda e, bank=bank, cb_nxt=cb_nxt: e.tensor_copy(
                            cb_nxt[:], bank[0:C1, 0:C1]), deps=[tq2], inc="v")
                        ps.done(bi, vq2)
                    bi, bank, war = ps.take()
                    tp = em.emit("tensor", lambda e, bank=bank, b_nxt=b_nxt, tt_cur=tt_cur:
                                 e.matmul(bank[0:C1, 0:C1], b_nxt[:], tt_cur[:],
                                          start=True, stop=True),
                                 deps=[vq, war, em.tok("v")], inc="t")
                    vp_ = em.emit("vector", lambda e, bank=bank, tt_nxt=tt_nxt, tt_cur=tt_cur:
                                  e.tensor_tensor(tt_nxt[:], bank[0:C1, 0:C1], tt_cur[:],
                                                  op=OP.add), deps=[tp], inc="v")
                    ps.done(bi, vp_)
                    tt_cur, tt_nxt = tt_nxt, tt_cur
                    b_cur, cb_cur = b_nxt, cb_nxt
                    b_nxt = b_b if b_nxt is b_a else b_a
                    cb_nxt = cb_b if cb_nxt is cb_a else cb_a
                    kk *= 2
                vbg = em.emit("vector", lambda e, c=c, h=h: e.tensor_tensor(
                    bgcol[:], gamT_all[:, c * NV + h:c * NV + h + 1], bcol(c, h),
                    op=OP.mult), inc="v")
                vkbg = em.emit("vector", lambda e, c=c: e.tensor_scalar(
                    kbg[:], ktr_all[:, c * 64:(c + 1) * 64], bgcol[:], None, op0=OP.mult),
                    deps=[em.force(vbg)], inc="v")
                bi, bank, war = ps.take()
                tw = em.emit("tensor", lambda e, bank=bank, tt_cur=tt_cur: e.matmul(
                    bank[0:C1, 0:64], tt_cur[:], kbg[:], start=True, stop=True),
                    deps=[vkbg, war, em.tok("v")], inc="t")
                vw = em.emit("vector", lambda e, bank=bank: e.tensor_copy(
                    w_sb[:], bank[0:C1, 0:64]), deps=[tw], inc="v")
                ps.done(bi, vw)
                bi, bank, war = ps.take()
                twt = em.emit("tensor", lambda e, bank=bank, tt_cur=tt_cur: e.matmul(
                    bank[0:64, 0:C1], kbg[:], tt_cur[:], start=True, stop=True),
                    deps=[vkbg, war], inc="t")
                vwt = em.emit("vector", lambda e, bank=bank, c=c: e.tensor_copy(
                    wt_all[:, c * C1:(c + 1) * C1], bank[0:64, 0:C1]), deps=[twt], inc="v")
                ps.done(bi, vwt)
                bi, bank, war = ps.take()
                tv_ = em.emit("tensor", lambda e, bank=bank, cs=cs: e.transpose(
                    bank[0:C1, 0:64], vh_sb[:, cs], c_ident[0:64, 0:64]), deps=[lv_, war], inc="t")
                vv_ = em.emit("vector", lambda e, bank=bank: e.tensor_copy(
                    vtr[:], bank[0:C1, 0:64]), deps=[tv_], inc="v")
                ps.done(bi, vv_)
                vvb = em.emit("vector", lambda e, c=c, h=h: e.tensor_scalar(
                    vb_sb[:], vtr[:], bcol(c, h), None, op0=OP.mult), inc="v")
                bi, bank, war = ps.take()
                trl = em.emit("tensor", lambda e, bank=bank, tt_cur=tt_cur: e.matmul(
                    bank[0:C1, 0:64], tt_cur[:], vb_sb[:], start=True, stop=True),
                    deps=[vvb, war], inc="t")
                vrl = em.emit("vector", lambda e, bank=bank, c=c: e.tensor_copy(
                    rl_all[:, c * 64:(c + 1) * 64], bank[0:C1, 0:64]), deps=[trl], inc="v")
                ps.done(bi, vrl)
                vkb = em.emit("vector", lambda e, c=c, h=h: e.tensor_scalar(
                    kb_all[:, c * 64:(c + 1) * 64], ktr_all[:, c * 64:(c + 1) * 64],
                    ratT_all[:, c * NV + h:c * NV + h + 1], None, op0=OP.mult), inc="v")
                em.emit("vector", lambda e, cs=cs: e.tensor_tensor(
                    mt_all[:, cs], dAT[:], mraw_all[:, cs], op=OP.mult), inc="v")
                vmt = em.emit("vector", lambda e, cs=cs: e.tensor_tensor(
                    mt_all[:, cs], mt_all[:, cs], c_uti[:], op=OP.mult), inc="v")
                bi, bank, war = ps.take()
                tmc = em.emit("tensor", lambda e, bank=bank, c=c: e.matmul(
                    bank[0:64, 0:64], w_sb[:], kb_all[:, c * 64:(c + 1) * 64],
                    start=True, stop=True), deps=[vw, vkb, war], inc="t")
                vmc = em.emit("vector", lambda e, bank=bank, c=c, h=h: e.scalar_tensor_tensor(
                    mct[:], c_ident[0:64, 0:64], gCb_all[0:64, c * NV + h:c * NV + h + 1],
                    bank[0:64, 0:64], op0=OP.mult, op1=OP.subtract), deps=[tmc], inc="v")
                ps.done(bi, vmc)
                bi, bank, war = ps.take()
                tf1 = em.emit("tensor", lambda e, bank=bank, f_cur=f_cur: e.matmul(
                    bank[0:64, 0:64], mct[:], f_cur[:], start=True, stop=False),
                    deps=[vmc, fz, war, em.tok("v")], inc="t")
                tf2 = em.emit("tensor", lambda e, bank=bank, c=c: e.matmul(
                    bank[0:64, 0:64], kb_all[:, c * 64:(c + 1) * 64],
                    rl_all[:, c * 64:(c + 1) * 64], start=False, stop=True), inc="t")
                vf = em.emit("vector", lambda e, bank=bank, f_nxt=f_nxt: e.tensor_copy(
                    f_nxt[:], bank[0:64, 0:64]), deps=[tf2], inc="v")
                ps.done(bi, vf)
                f_cur, f_nxt = f_nxt, f_cur
            vsn = em.emit("vector", lambda e, f_cur=f_cur: e.tensor_scalar(
                tmp64[:], f_cur[:], c_fe128[0:64, :], None, op0=OP.mult),
                deps=[em.tok("v")], inc="v")
            dh1 = em.dma("gpsimd", cc1_i[h][:], tmp64[:], deps=[vsn], stream="d_cc")
            ccth = em.emit("gpsimd", lambda e, h=h: e.collective_compute(
                "AllReduce", OP.add, replica_groups=rg, ins=[cc1_i[h].ap().opt()],
                outs=[cc1_o[h].ap().opt()]), deps=[dh1], inc="cc", add=1)
            dh2 = em.dma("gpsimd", s_a[:], cc1_o[h][:], deps=[ccth], stream="d_cc")
            vsi = em.emit("vector", lambda e: e.tensor_scalar(
                s_a[:], s_a[:], c_fo128[0:64, :], None, op0=OP.mult), deps=[dh2], inc="v")
            s_cur, s_nxt = s_a, s_b
            zh_war = None
            for c in range(NC1):
                cs = slice(c * C1, (c + 1) * C1)
                bi, bank, war = ps.take()
                tr_ = em.emit("tensor", lambda e, bank=bank, cs=cs, s_cur=s_cur: e.matmul(
                    bank[0:C1, 0:64], wt_all[:, cs], s_cur[:], start=True, stop=True),
                    deps=[vsi, war, em.tok("v")], inc="t")
                vr_ = em.emit("vector", lambda e, bank=bank, c=c: e.scalar_tensor_tensor(
                    r_sb[:], rl_all[:, c * 64:(c + 1) * 64], 1.0, bank[0:C1, 0:64],
                    op0=OP.mult, op1=OP.subtract), deps=[tr_], inc="v")
                ps.done(bi, vr_)
                bi, bank, war = ps.take()
                to1 = em.emit("tensor", lambda e, bank=bank, cs=cs, s_cur=s_cur: e.matmul(
                    bank[0:C1, 0:64], qh_sb[:, cs], s_cur[:], start=True, stop=True),
                    deps=[war], inc="t")
                vo1 = em.emit("vector", lambda e, bank=bank, c=c, h=h: e.tensor_scalar(
                    o12[:], bank[0:C1, 0:64], gamT_all[:, c * NV + h:c * NV + h + 1],
                    None, op0=OP.mult), deps=[to1], inc="v")
                ps.done(bi, vo1)
                bi, bank, war = ps.take()
                to2 = em.emit("tensor", lambda e, bank=bank, cs=cs: e.matmul(
                    bank[0:C1, 0:64], mt_all[:, cs], r_sb[:], start=True, stop=True),
                    deps=[vr_, war], inc="t")
                vo2 = em.emit("vector", lambda e, bank=bank: e.tensor_tensor(
                    o12[:], o12[:], bank[0:C1, 0:64], op=OP.add), deps=[to2, vo1], inc="v")
                ps.done(bi, vo2)
                so_ = em.emit("scalar", lambda e: e.activation(
                    jnk5[:], o12[:], AF.Square, accum_out=sso[0:C1, :]), deps=[vo2], inc="s")
                em.emit("vector", lambda e: e.tensor_scalar(
                    ro[0:C1, :], sso[0:C1, :], 1.0 / DV, 1e-6, op0=OP.mult, op1=OP.add),
                    deps=[so_], inc="v")
                sr_ = em.emit("scalar", lambda e: e.activation(
                    ro[0:C1, :], ro[0:C1, :], AF.Sqrt), deps=[em.tok("v")], inc="s")
                vrr = em.emit("vector", lambda e: e.reciprocal(ro[0:C1, :], ro[0:C1, :]),
                              deps=[sr_], inc="v")
                vof = em.emit("vector", lambda e: e.scalar_tensor_tensor(
                    o12[:], o12[:], ro[0:C1, :], c_normwr[:], op0=OP.mult, op1=OP.mult),
                    deps=[em.force(vrr)], inc="v")
                bi, bank, war = ps.take()
                tof = em.emit("tensor", lambda e, bank=bank: e.transpose(
                    bank[0:64, 0:C1], o12[:], c_ident[:]), deps=[vof, war], inc="t")
                vofT = em.emit("vector", lambda e, bank=bank: e.tensor_copy(
                    ofT[:], bank[0:64, 0:C1]), deps=[tof], inc="v")
                ps.done(bi, vofT)
                lz_ = em.dma("sync", zh_sb[:], z1_s[h * 64:(h + 1) * 64, c * C1:(c + 1) * C1],
                             deps=[zh_war], stream="d_z5")
                sz_ = em.emit("scalar", lambda e: e.activation(
                    zs_sb[:], zh_sb[:], AF.Silu), deps=[lz_], inc="s")
                zh_war = sz_
                vog = em.emit("vector", lambda e, c=c, h=h: e.tensor_tensor(
                    og[(h % 2) * 64:(h % 2) * 64 + 64,
                       (h // 2) * Tc + c * C1:(h // 2) * Tc + (c + 1) * C1],
                    ofT[:], zs_sb[:], op=OP.mult), deps=[vofT, sz_], inc="v")
                bi, bank, war = ps.take()
                ts2 = em.emit("tensor", lambda e, bank=bank, c=c: e.matmul(
                    bank[0:64, 0:64], kb_all[:, c * 64:(c + 1) * 64], r_sb[:],
                    start=True, stop=True), deps=[vr_, war], inc="t")
                vs2 = em.emit("vector", lambda e, bank=bank, c=c, h=h, s_cur=s_cur, s_nxt=s_nxt:
                              e.scalar_tensor_tensor(
                    s_nxt[:], s_cur[:], gCb_all[0:64, c * NV + h:c * NV + h + 1],
                    bank[0:64, 0:64], op0=OP.mult, op1=OP.add), deps=[ts2], inc="v")
                ps.done(bi, vs2)
                s_cur, s_nxt = s_nxt, s_cur
        if debug:
            for mb in range(NVB):
                em.dma("gpsimd", og_dbg[mb * 128:(mb + 1) * 128, :],
                       og[:, mb * Tc:(mb + 1) * Tc], deps=[em.tok("v")], stream="d_st")
        wob = sb("p5_wob", (128, 2 * 512), life="phase")
        ev5_war = None
        wob_war = [None, None]
        for nblk in range(D // 512):
            for i in range(NTC):
                bi, bank, war = ps.take()
                mmo = None
                for mb in range(NVB):
                    sl5 = mb % 2
                    wld5_ = em.dma("sync", wob[:, sl5 * 512:(sl5 + 1) * 512],
                                   WoutT[mb * 128:(mb + 1) * 128, nblk * 512:(nblk + 1) * 512],
                                   deps=[wob_war[sl5]], stream=f"d_wo{sl5}")
                    mmo = em.emit("tensor", lambda e, bank=bank, mb=mb, i=i, sl5=sl5, st=(mb == 0), sp=(mb == NVB - 1):
                                  e.matmul(bank[:, 0:512], og[:, mb * Tc + i * 128:mb * Tc + (i + 1) * 128],
                                           wob[:, sl5 * 512:(sl5 + 1) * 512], start=st, stop=sp),
                                  deps=[wld5_, war, em.tok("v") if nblk == 0 and i == 0 and mb == 0 else None],
                                  inc="t")
                    wob_war[sl5] = mmo
                vev = em.emit("vector", lambda e, bank=bank: e.tensor_copy(
                    ev5[:, 0:512], bank[:, 0:512]), deps=[mmo, ev5_war], inc="v")
                sd5 = em.dma("gpsimd", s1_s[i * 128:(i + 1) * 128, nblk * 512:(nblk + 1) * 512],
                             ev5[:, 0:512], deps=[vev], stream="d_st")
                ev5_war = sd5
                ps.done(bi, sd5)
    em.barrier()
    phase_reset()
    group_reset()

    if phases >= 6:
        NDI = I1 // 128
        wgb = sb("p6_wgb", (128, ND * 512), life="phase")
        hg6 = sb("p6_hg6", (128, NTC * 512), life="phase")
        xb6 = sb("p6_xb6", (128, ND * 128), life="phase")
        ab6 = sb("p6_ab6", (128, 512), life="phase")
        q_u = sb("p6_qu", (128, 128), life="phase"); q_l = sb("p6_ql", (128, 128), life="phase")
        q_v = sb("p6_qv", (128, 128), life="phase"); q_w = sb("p6_qw", (128, 128), life="phase")
        sec6 = sb("p6_sec", (128, 128), life="phase"); kq6 = sb("p6_kq", (128, 128), life="phase")
        hs6 = sb("p6_hs6", (128, 512), life="phase")
        wgb_war = None
        tr6_done = None
        hg_war = [None] * NTC
        for n in range(I1 // 512):
            wldg = []
            for j in range(ND):
                wldg.append(em.dma("sync", wgb[:, j * 512:(j + 1) * 512],
                                   wg_in[j * 128:(j + 1) * 128, n * 512:(n + 1) * 512],
                                   deps=[wgb_war], stream="d_w"))
            xld6 = [None] * NTC
            for i in range(NTC):
                xl = []
                for j in range(ND):
                    xl.append(em.dma("sync", xb6[:, j * 128:(j + 1) * 128],
                                     xn1f_s[j * 128:(j + 1) * 128, i * 128:(i + 1) * 128],
                                     deps=[xld6[i - 1]] if i > 0 else [em.tok("t")],
                                     stream="d_x"))
                bi, bank, war = ps.take()
                mmg = None
                for j in range(ND):
                    mmg = em.emit("tensor", lambda e, bank=bank, j=j, st=(j == 0), sp=(j == ND - 1):
                                  e.matmul(bank[:, 0:512], xb6[:, j * 128:(j + 1) * 128],
                                           wgb[:, j * 512:(j + 1) * 512], start=st, stop=sp),
                                  deps=[xl[-1], wldg[-1], war], inc="t")
                xld6[i] = mmg
                sg6 = em.emit("scalar", lambda e, bank=bank, i=i: e.activation(
                    hg6[:, i * 512:(i + 1) * 512], bank[:, 0:512], AF.Silu),
                    deps=[mmg, hg_war[i]], inc="s")
                ps.done(bi, sg6)
            gate_done = em.tok("s")
            wldu = []
            for j in range(ND):
                wldu.append(em.dma("sync", wgb[:, j * 512:(j + 1) * 512],
                                   wu_in[j * 128:(j + 1) * 128, n * 512:(n + 1) * 512],
                                   deps=[em.tok("t"), gate_done], stream="d_w"))
            for i in range(NTC):
                xl = []
                for j in range(ND):
                    xl.append(em.dma("sync", xb6[:, j * 128:(j + 1) * 128],
                                     xn1f_s[j * 128:(j + 1) * 128, i * 128:(i + 1) * 128],
                                     deps=[xld6[i - 1] if i > 0 else xld6[NTC - 1]],
                                     stream="d_x"))
                bi, bank, war = ps.take()
                mmu = None
                for j in range(ND):
                    mmu = em.emit("tensor", lambda e, bank=bank, j=j, st=(j == 0), sp=(j == ND - 1):
                                  e.matmul(bank[:, 0:512], xb6[:, j * 128:(j + 1) * 128],
                                           wgb[:, j * 512:(j + 1) * 512], start=st, stop=sp),
                                  deps=[xl[-1], wldu[-1], war], inc="t")
                xld6[i] = mmu
                vh6 = em.emit("vector", lambda e, bank=bank, i=i: e.tensor_tensor(
                    hg6[:, i * 512:(i + 1) * 512], hg6[:, i * 512:(i + 1) * 512],
                    bank[:, 0:512], op=OP.mult), deps=[mmu, gate_done], inc="v")
                ps.done(bi, vh6)
                hsl = hg6[:, i * 512:(i + 1) * 512]
                dbg_tok = None
                if debug:
                    dbg_tok = em.dma("gpsimd", hg6_dbg[i * 128:(i + 1) * 128, n * 512:(n + 1) * 512],
                                     hsl, deps=[vh6], stream="d_st")
                sa6 = em.emit("scalar", lambda e, hsl=hsl: e.activation(
                    ab6[:], hsl, AF.Abs), deps=[vh6, tr6_done], inc="s")
                qs = [ab6[:, q:512:4] for q in range(4)]
                em.emit("vector", lambda e, qs=qs: e.tensor_tensor(q_u[:], qs[0], qs[1], op=OP.max),
                        deps=[sa6], inc="v")
                em.emit("vector", lambda e, qs=qs: e.tensor_tensor(q_l[:], qs[0], qs[1], op=OP.min), inc="v")
                em.emit("vector", lambda e, qs=qs: e.tensor_tensor(q_v[:], qs[2], qs[3], op=OP.max), inc="v")
                em.emit("vector", lambda e, qs=qs: e.tensor_tensor(q_w[:], qs[2], qs[3], op=OP.min), inc="v")
                em.emit("vector", lambda e: e.tensor_tensor(sec6[:], q_u[:], q_v[:], op=OP.min), inc="v")
                em.emit("vector", lambda e: e.tensor_tensor(q_l[:], q_l[:], q_w[:], op=OP.max), inc="v")
                em.emit("vector", lambda e: e.tensor_tensor(sec6[:], sec6[:], q_l[:], op=OP.max), inc="v")
                for q in range(4):
                    em.emit("vector", lambda e, q=q: e.tensor_tensor(
                        kq6[:], ab6[:, q:512:4], sec6[:], op=OP.is_ge), inc="v")
                    em.emit("vector", lambda e, q=q, hsl=hsl: e.tensor_tensor(
                        hs6[:, q:512:4], hsl[:, q:512:4], kq6[:], op=OP.mult), inc="v")
                vmask = em.tok("v")
                hg_war[i] = dbg_tok if dbg_tok is not None else vmask
                for kk in range(4):
                    bi, bank, war = ps.take()
                    t6 = em.emit("tensor", lambda e, bank=bank, kk=kk: e.transpose(
                        bank[:, 0:128], hs6[:, kk * 128:(kk + 1) * 128], c_ident[:]),
                        deps=[vmask, war], inc="t")
                    v6 = em.emit("vector", lambda e, bank=bank, kk=kk: e.tensor_copy(
                        ab6[:, kk * 128:(kk + 1) * 128], bank[:, 0:128]), deps=[t6], inc="v")
                    d6 = em.dma("gpsimd", hsp_s[n * 512 + kk * 128:n * 512 + (kk + 1) * 128,
                                                i * 128:(i + 1) * 128],
                                ab6[:, kk * 128:(kk + 1) * 128], deps=[v6], stream="d_st")
                    ps.done(bi, d6)
                tr6_done = em.tok("d_st")
            wgb_war = em.tok("t")
        wdb = sb("p6_wdb", (128, 2 * 512), life="phase")
        hdb = sb("p6_hdb", (128, 2 * 512), life="phase")
        g6 = sb("p6_g6", (128, 512), life="phase")
        x1t6 = sb("p6_x1t", (128, 512), life="phase")
        s1t6 = sb("p6_s1t", (128, 512), life="phase")
        wdb_war = [None, None]
        hdb_war = [None, None]
        g6_war = None
        all_sp = em.tok("d_st")
        for o2 in range(D // 512):
            for tg in range(NTC // 4):
                pbs = []
                for q in range(4):
                    pbs.append(ps.take())
                mmd = [None] * 4
                for j in range(NDI):
                    sl6 = j % 2
                    lw = em.dma("sync", wdb[:, sl6 * 512:(sl6 + 1) * 512],
                                wd_in[j * 128:(j + 1) * 128, o2 * 512:(o2 + 1) * 512],
                                deps=[wdb_war[sl6]], stream=f"d_p6w{sl6}")
                    lh = em.dma("sync", hdb[:, sl6 * 512:(sl6 + 1) * 512],
                                hsp_s[j * 128:(j + 1) * 128, tg * 512:(tg + 1) * 512],
                                deps=[hdb_war[sl6], all_sp], stream=f"d_p6h{sl6}")
                    for q in range(4):
                        bi, bank, war = pbs[q]
                        mmd[q] = em.emit("tensor", lambda e, bank=bank, j=j, sl6=sl6, q=q,
                                         st=(j == 0), sp=(j == NDI - 1):
                                         e.matmul(bank[:, 0:512],
                                                  hdb[:, sl6 * 512 + q * 128:sl6 * 512 + (q + 1) * 128],
                                                  wdb[:, sl6 * 512:(sl6 + 1) * 512],
                                                  start=st, stop=sp),
                                         deps=[lw, lh] + ([war] if j == 0 else []), inc="t")
                    wdb_war[sl6] = mmd[3]
                    hdb_war[sl6] = mmd[3]
                for q in range(4):
                    i = tg * 4 + q
                    bi, bank, war = pbs[q]
                    sg_ = em.emit("scalar", lambda e, bank=bank: e.activation(
                        g6[:], bank[:, 0:512], AF.Silu), deps=[mmd[q], g6_war], inc="s")
                    dtok6 = None
                    if debug:
                        dtok6 = em.dma("gpsimd", g1_dbg[i * 128:(i + 1) * 128, o2 * 512:(o2 + 1) * 512],
                                       g6[:], deps=[sg_], stream="d_st")
                    lx1 = em.dma("sync", x1t6[:], x1_s[i * 128:(i + 1) * 128, o2 * 512:(o2 + 1) * 512],
                                 deps=[em.tok("v")], stream="d_x1f")
                    ls1 = em.dma("sync", s1t6[:], s1_s[i * 128:(i + 1) * 128, o2 * 512:(o2 + 1) * 512],
                                 stream="d_s1f")
                    vg1 = em.emit("vector", lambda e: e.tensor_tensor(
                        g6[:], g6[:], s1t6[:], op=OP.mult), deps=[sg_, ls1, dtok6], inc="v")
                    vg2 = em.emit("vector", lambda e: e.tensor_tensor(
                        g6[:], g6[:], x1t6[:], op=OP.add), deps=[lx1], inc="v")
                    do_ = em.dma("gpsimd", x_out[i * 128:(i + 1) * 128, o2 * 512:(o2 + 1) * 512],
                                 g6[:], deps=[vg2], stream="d_out")
                    g6_war = do_
                    ps.done(bi, do_)
    em.emit("gpsimd", lambda e: e.nop(), deps=[em.tok("d_out"), em.tok("d_st")])
    em.barrier()

    with nc.Block() as block:
        em.finalize(block)
    return nc


def kernel(**inputs):
    cfg = FULL_CFG
    per_core = host_prepare(cfg, inputs)
    nc = build_program(cfg, debug=False)
    res = run_bass_kernel_spmd(nc, per_core, core_ids=list(range(cfg["n_cores"])))
    B, T, D = cfg["B"], cfg["T"], cfg["D"]
    Tc = T // 2
    out = np.empty((B, T, D), np.float32)
    for c in range(cfg["n_cores"]):
        b, half = c // 2, c % 2
        out[b, half * Tc:(half + 1) * Tc, :] = np.asarray(res.results[c]["x_out"])
    return out


# version 04
# speedup vs baseline: 1.0089x; 1.0089x over previous
import numpy as np

import concourse.bass as bass
import concourse.mybir as mybir
from concourse.bass_utils import run_bass_kernel_spmd

F32 = mybir.dt.float32
BF16 = mybir.dt.bfloat16
I32 = mybir.dt.int32
OP = mybir.AluOpType
AF = mybir.ActivationFunctionType
AX = mybir.AxisListType

FULL_CFG = dict(
    B=4, T=4096, D=2048, DFF=8192, I1=8192,
    NK=8, NV=16, DK=64, DV=64, KS=4, K0=64, V0=128,
    C0=64, C1=128, n_cores=8,
)


class EM:
    ENGINES = ("sync", "scalar", "vector", "tensor", "gpsimd")

    def __init__(self, nc):
        self.nc = nc
        self.ops = {e: [] for e in self.ENGINES}
        self.sems = {}
        self.cnt = {}
        self.waited = {e: {} for e in self.ENGINES}

    def sem(self, name):
        if name not in self.sems:
            self.sems[name] = self.nc.alloc_semaphore(name)
            self.cnt[name] = 0
        return self.sems[name]

    def tok(self, name):
        self.sem(name)
        return (name, self.cnt[name], None)

    @staticmethod
    def force(t):
        return (t[0], t[1], None) if t is not None else None

    def _waits(self, eng, deps):
        for d in deps or []:
            if d is None:
                continue
            s, v, src = d
            if v <= 0 or src == eng:
                continue
            if self.waited[eng].get(s, 0) >= v:
                continue
            self.waited[eng][s] = v
            h = self.sem(s)
            self.ops[eng].append(lambda e, h=h, v=v: e.wait_ge(h, v))

    def emit(self, eng, fn, deps=None, inc=None, add=1):
        self._waits(eng, deps)
        tok = None
        if inc is not None:
            h = self.sem(inc)
            self.cnt[inc] += add
            src = None if (inc.startswith("d_") or inc.startswith("cc")) else eng
            tok = (inc, self.cnt[inc], src)
            self.ops[eng].append(lambda e, fn=fn, h=h, add=add: fn(e).then_inc(h, add))
        else:
            self.ops[eng].append(lambda e, fn=fn: fn(e))
        return tok

    def dma(self, q, out, in_, deps=None, stream=None):
        stream = stream or ("d_" + q)
        return self.emit(q, lambda e: e.dma_start(out=out, in_=in_),
                         deps=deps, inc=stream, add=16)

    def dmat(self, q, out, in_, deps=None, stream=None):
        stream = stream or ("d_" + q)
        return self.emit(q, lambda e: e.dma_start_transpose(out, in_),
                         deps=deps, inc=stream, add=16)

    def barrier(self):
        dtoks = [self.tok(k) for k in list(self.cnt) if k.startswith("d_")]
        self._waits("gpsimd", dtoks)
        toks = []
        for e in self.ENGINES:
            toks.append(self.emit(e, lambda en: en.nop(), inc="bar_" + e))
        toks = [(s, v, None) for (s, v, _e) in toks]
        for e in self.ENGINES:
            self._waits(e, toks)

    def finalize(self, block):
        for e in self.ENGINES:
            ops = self.ops[e]

            def body(eng, ops=ops):
                for f in ops:
                    f(eng)
            getattr(block, e)(body)


class Psum:

    def __init__(self, nc):
        self.banks = [nc.alloc_psum_tensor(f"pb{i}", [128, 512], F32) for i in range(8)]
        self.war = [None] * 8
        self.rr = 0

    def take(self):
        i = self.rr
        self.rr = (self.rr + 1) % 8
        return i, self.banks[i], self.war[i]

    def done(self, i, tok):
        self.war[i] = tok


def host_prepare(cfg, inp):
    B, T, D = cfg["B"], cfg["T"], cfg["D"]
    Tc = T // 2
    NV, DV, KS = cfg["NV"], cfg["DV"], cfg["KS"]
    C0, C1 = cfg["C0"], cfg["C1"]
    cc = lambda a: np.ascontiguousarray(a, dtype=np.float32)

    sh = {}
    sh["Wqk"] = cc(np.concatenate([inp["L0_attn_Wq"], inp["L0_attn_Wk"]], axis=1))
    sh["Wbd"] = cc(np.concatenate([inp["L0_attn_Wb"], inp["L0_attn_Wd"]], axis=1))
    sh["Wv0"] = cc(inp["L0_attn_Wv"])
    sh["Wo0"] = cc(inp["L0_attn_Wo"])
    sh["bbd"] = cc(np.concatenate([inp["L0_attn_bb"], inp["L0_attn_bd"]])[:, None])
    sh["w1T"] = cc(np.asarray(inp["L0_ffn_w1"]).T)
    sh["w2T"] = cc(np.asarray(inp["L0_ffn_w2"]).T)
    sh["w3T"] = cc(np.asarray(inp["L0_ffn_w3"]).T)
    sh["WqkvT"] = cc(np.asarray(inp["L1_attn_W_qkv"]).T)
    sh["WzT"] = cc(np.asarray(inp["L1_attn_W_z"]).T)
    D_ = np.asarray(inp["L1_attn_W_qkv"]).shape[1]
    NV_ = np.asarray(inp["L1_attn_W_b"]).shape[0]
    wba = np.zeros((D_, 64), np.float32)
    wba[:, 0:NV_] = np.asarray(inp["L1_attn_W_b"]).T
    wba[:, 32:32 + NV_] = np.asarray(inp["L1_attn_W_a"]).T
    sh["WbaT"] = cc(wba)
    sh["WoutT"] = cc(np.asarray(inp["L1_attn_W_out"]).T)
    sh["convw"] = cc(np.asarray(inp["L1_attn_conv_w"])[:, 0, :])
    dtb = np.zeros((64, 1), np.float32)
    dtb[32:32 + NV_, 0] = np.asarray(inp["L1_attn_dt_bias"])
    sh["dt_bias"] = cc(dtb)
    sh["A_log"] = cc(np.asarray(inp["L1_attn_A_log"])[:, None])
    sh["normw_row"] = cc(np.broadcast_to(
        1.0 + np.asarray(inp["L1_attn_norm_weight"])[None, :], (C1, DV)))
    sh["wg"] = cc(inp["L1_ffn_w_gate"])
    sh["wu"] = cc(inp["L1_ffn_w_up"])
    sh["wd"] = cc(inp["L1_ffn_w_down"])

    sh["ident"] = np.eye(128, dtype=np.float32)
    sh["mask0"] = cc(np.triu(np.ones((C0, C0))))
    sh["ltstrict"] = cc(np.tril(np.ones((C1, C1)), -1))
    sh["utstrict"] = cc(np.triu(np.ones((C1, C1)), 1))
    sh["utincl"] = cc(np.triu(np.ones((C1, C1))))
    sh["ones_row"] = np.ones((1, 512), np.float32)
    sh["ones_col"] = np.ones((128, 1), np.float32)
    sh["ones_tile"] = np.ones((128, 128), np.float32)
    sh["pm1"] = cc(np.concatenate([-np.ones(C0), np.ones(C0)])[:, None])
    lr = np.zeros((128, 128), np.float32)
    lr[C1 - 1, :] = 1.0
    sh["lastrow"] = lr
    bi_ = np.zeros((128, 2), np.float32)
    bi_[:64, 0] = 1.0
    bi_[64:, 1] = 1.0
    sh["blk64"] = bi_
    sh["exp64"] = np.ascontiguousarray(bi_.T)

    per_core = []
    x = np.asarray(inp["L0_attn_x"], dtype=np.float32)
    for c in range(cfg["n_cores"]):
        b, half = c // 2, c % 2
        d = dict(sh)
        d["x_in"] = cc(x[b, half * Tc:(half + 1) * Tc, :])
        d["flag_even"] = np.full((1, 1), 1.0 if half == 0 else 0.0, np.float32)
        d["flag_odd"] = np.full((1, 1), 0.0 if half == 0 else 1.0, np.float32)
        per_core.append(d)
    return per_core


def build_program(cfg, debug=False, phases=99):
    B, T, D = cfg["B"], cfg["T"], cfg["D"]
    Tc = T // 2
    DFF, I1 = cfg["DFF"], cfg["I1"]
    NK, NV, DK, DV, KS = cfg["NK"], cfg["NV"], cfg["DK"], cfg["DV"], cfg["KS"]
    K0, V0 = cfg["K0"], cfg["V0"]
    C0, C1 = cfg["C0"], cfg["C1"]
    KD1, VD1 = NK * DK, NV * DV
    CONV = 2 * KD1 + VD1
    ND = D // 128
    NTC = Tc // 128
    NC0 = Tc // C0
    NC1 = Tc // C1
    NM = CONV // 128
    rep = NV // NK
    assert DK == 64 and DV == 64 and K0 == 64 and V0 == 128 and C0 == 64 and C1 == 128

    nc = bass.Bass("TRN2", target_bir_lowering=False, detect_race_conditions=False)
    em = EM(nc)
    ps = Psum(nc)

    def din(name, shape, dt=F32):
        return nc.dram_tensor(name, list(shape), dt, kind="ExternalInput")

    x_in = din("x_in", (Tc, D))
    Wqk = din("Wqk", (D, 128)); Wbd = din("Wbd", (D, 128))
    Wv0 = din("Wv0", (D, V0)); Wo0 = din("Wo0", (V0, D)); bbd_in = din("bbd", (128, 1))
    w1T = din("w1T", (D, DFF)); w2T = din("w2T", (D, DFF)); w3T = din("w3T", (DFF, D))
    WqkvT = din("WqkvT", (D, CONV)); WzT = din("WzT", (D, VD1))
    WbaT = din("WbaT", (D, 64)); WoutT = din("WoutT", (VD1, D))
    convw_in = din("convw", (CONV, KS))
    dtb_in = din("dt_bias", (64, 1)); alog_in = din("A_log", (NV, 1))
    normw_in = din("normw_row", (C1, DV))
    wg_in = din("wg", (D, I1)); wu_in = din("wu", (D, I1)); wd_in = din("wd", (I1, D))
    ident_in = din("ident", (128, 128))
    mask0_in = din("mask0", (C0, C0))
    lts_in = din("ltstrict", (C1, C1)); uts_in = din("utstrict", (C1, C1))
    uti_in = din("utincl", (C1, C1))
    onesr_in = din("ones_row", (1, 512)); onesc_in = din("ones_col", (128, 1))
    onest_in = din("ones_tile", (128, 128))
    pm1_in = din("pm1", (128, 1)); lastrow_in = din("lastrow", (128, 128))
    blk64_in = din("blk64", (128, 2)); exp64_in = din("exp64", (2, 128))
    feven_in = din("flag_even", (1, 1)); fodd_in = din("flag_odd", (1, 1))

    x_out = nc.dram_tensor("x_out", [Tc, D], F32, kind="ExternalOutput")

    def scratch(name, shape, dt=F32):
        if debug:
            return nc.dram_tensor(name, list(shape), dt, kind="ExternalOutput")
        return nc.dram_tensor(name, list(shape), dt)

    xn0_s = scratch("xn0_s", (Tc, D))
    w1q_s = scratch("w1q_s", (D, DFF), BF16)
    w2q_s = scratch("w2q_s", (D, DFF), BF16)
    w3q_s = scratch("w3q_s", (DFF, D), BF16)
    x1_s = scratch("x1_s", (Tc, D))
    xn1f_s = scratch("xn1f_s", (D, Tc))
    qkvp_s = scratch("qkvp_s", (CONV, Tc))
    qkvc_s = scratch("qkvc_s", (CONV, Tc))
    z1_s = scratch("z1_s", (VD1, Tc))
    s1_s = scratch("s1_s", (Tc, D))
    hsp_s = scratch("hsp_s", (I1, Tc))
    og_dbg = scratch("og_dbg", (VD1, Tc)) if debug else None
    ofm_dbg = scratch("ofm_dbg", (V0, Tc)) if debug else None
    ht_dbg = scratch("ht_dbg", (Tc, DFF)) if debug else None
    qkf_dbg = scratch("qkf_dbg", (128, Tc)) if debug else None
    p1x_dbg = scratch("p1x_dbg", (Tc, D)) if debug else None
    p1c_dbg = scratch("p1c_dbg", (128, NTC * 4)) if debug else None
    bdf_dbg = scratch("bdf_dbg", (128, Tc)) if debug else None
    vtm_dbg = scratch("vtm_dbg", (64, (Tc // C0) * V0)) if debug else None
    spf_dbg = scratch("spf_dbg", (128, Tc)) if debug else None
    qt_dbg = scratch("qt_dbg", (64, Tc)) if debug else None
    am_dbg = scratch("am_dbg", (64, Tc)) if debug else None
    U_dbg = scratch("U_dbg", (64, (Tc // C0) * V0)) if debug else None
    lam_dbg = scratch("lam_dbg", (64, Tc)) if debug else None
    xqi_dbg = scratch("xqi_dbg", (Tc, D)) if debug else None
    hg6_dbg = scratch("hg6_dbg", (Tc, I1)) if debug else None
    g1_dbg = scratch("g1_dbg", (Tc, D)) if debug else None
    hqi_dbg = scratch("hqi_dbg", (Tc, DFF)) if debug else None

    cc0_i = nc.dram_tensor("cc0_i", [K0, V0], F32)
    cc0_o = nc.dram_tensor("cc0_o", [K0, V0], F32)
    cch_i = nc.dram_tensor("cch_i", [128, NM * (KS - 1)], F32)
    cch_o = nc.dram_tensor("cch_o", [128, NM * (KS - 1)], F32)
    cc1_i = [nc.dram_tensor(f"cc1_i{h}", [DK, DV], F32) for h in range(NV)]
    cc1_o = [nc.dram_tensor(f"cc1_o{h}", [DK, DV], F32) for h in range(NV)]
    rg = [[i, i + 1] for i in range(0, cfg["n_cores"], 2)]

    arena = {"base": None, "cap": None, "g": 0, "p": 0}

    def sb(name, shape, dt=F32, life="perm", **kw):
        import math as _m
        if life == "perm" or arena["base"] is None:
            return nc.alloc_sbuf_tensor(name, list(shape), dt, **kw)
        n_free = 1
        for d in shape[1:]:
            n_free *= d
        nb = (n_free * mybir.dt.size(dt) + 63) // 64 * 64
        if life == "group":
            off = arena["base"] + arena["g"]
            arena["g"] += nb
        else:
            arena["p"] += nb
            off = arena["base"] + arena["cap"] - arena["p"]
        assert arena["g"] + arena["p"] <= arena["cap"], (
            f"sbuf arena overflow at {name}: g={arena['g']} p={arena['p']} cap={arena['cap']}")
        return nc.alloc_sbuf_tensor_at(name, list(shape), dt, offset=off)

    def arena_freeze():
        a0 = nc._sbuf_addr_for_side(None)
        arena["base"] = (a0 + 63) // 64 * 64
        arena["cap"] = nc.sbuf_bytes_remaining - 2048 - (arena["base"] - a0)

    def phase_reset():
        arena["p"] = 0

    def group_reset():
        arena["g"] = 0

    c_ident = sb("c_ident", (128, 128))
    c_mask0 = sb("c_mask0", (C0, C0))
    c_lts = sb("c_lts", (C1, C1)); c_uts = sb("c_uts", (C1, C1)); c_uti = sb("c_uti", (C1, C1))
    c_onesr = sb("c_onesr", (1, 512)); c_onesc = sb("c_onesc", (128, 1))
    c_onest = sb("c_onest", (128, 128))
    c_pm1 = sb("c_pm1", (128, 1)); c_lastrow = sb("c_lastrow", (128, 128))
    c_bbd = sb("c_bbd", (128, 1)); c_dtb = sb("c_dtb", (64, 1))
    c_negexpA = sb("c_negexpA", (NV, 1)); c_normwr = sb("c_normwr", (C1, DV))
    c_convw = sb("c_convw", (128, NM * KS))
    c_feven = sb("c_feven", (1, 1)); c_fodd = sb("c_fodd", (1, 1))
    c_blk64 = sb("c_blk64", (128, 2)); c_exp64 = sb("c_exp64", (2, 128))
    c_fe128 = sb("c_fe128", (128, 1)); c_fo128 = sb("c_fo128", (128, 1))

    ld0 = []
    for dst, src in [(c_ident, ident_in), (c_mask0, mask0_in), (c_lts, lts_in),
                     (c_uts, uts_in), (c_uti, uti_in), (c_onesr, onesr_in),
                     (c_onesc, onesc_in), (c_onest, onest_in), (c_pm1, pm1_in),
                     (c_lastrow, lastrow_in), (c_bbd, bbd_in), (c_dtb, dtb_in),
                     (c_negexpA, alog_in), (c_normwr, normw_in),
                     (c_feven, feven_in), (c_fodd, fodd_in),
                     (c_blk64, blk64_in), (c_exp64, exp64_in)]:
        ld0.append(em.dma("sync", dst[:], src[:]))
    for m in range(NM):
        ld0.append(em.dma("sync", c_convw[:, m * KS:(m + 1) * KS],
                          convw_in[m * 128:(m + 1) * 128, :]))
    t_e = em.emit("scalar", lambda e: e.activation(c_negexpA[:], c_negexpA[:], AF.Exp),
                  deps=ld0, inc="s")
    em.emit("vector", lambda e: e.tensor_scalar(c_negexpA[:], c_negexpA[:], -1.0, None,
                                                op0=OP.mult), deps=[t_e], inc="v")
    bi, bank, war = ps.take()
    t1 = em.emit("tensor", lambda e, bank=bank: e.matmul(
        bank[:, 0:1], c_onesr[0:1, 0:128], c_feven[:], start=True, stop=True),
        deps=ld0, inc="t")
    t2 = em.emit("tensor", lambda e, bank=bank: e.matmul(
        bank[:, 8:9], c_onesr[0:1, 0:128], c_fodd[:], start=True, stop=True), inc="t")
    em.emit("vector", lambda e, bank=bank: e.tensor_copy(c_fe128[:], bank[:, 0:1]),
            deps=[t2], inc="v")
    tv = em.emit("vector", lambda e, bank=bank: e.tensor_copy(c_fo128[:], bank[:, 8:9]), inc="v")
    ps.done(bi, tv)

    c_mclip = sb("c_mclip", (128, 3))
    c_wsb = sb("c_wsb", (128, 3))
    c_nwsb = sb("c_nwsb", (128, 3))
    NDF = DFF // 128
    arena_freeze()

    em.barrier()

    if phases >= 0:
        WCOL = min(4096, max(DFF, D))
        wbuf = sb("p0_wbuf", (128, WCOL), life="phase")
        wg1 = sb("p0_wg1", (128, WCOL), life="phase")
        wg2 = sb("p0_wg2", (128, WCOL), life="phase")
        wqb = sb("p0_wqb", (128, WCOL), BF16, life="phase")
        jnk = sb("p0_jnk", (128, WCOL), life="phase")
        aacc = sb("p0_aacc", (128, 1), life="phase")
        a1 = sb("p0_a1", (128, 1), life="phase")
        m11 = sb("p0_m11", (1, 1), life="phase")
        for j, (wsrc, wdst, R, CC) in enumerate(
                [(w1T, w1q_s, D, DFF), (w2T, w2q_s, D, DFF), (w3T, w3q_s, DFF, D)]):
            NR = R // 128
            NCW = (CC + WCOL - 1) // WCOL
            vtok = em.emit("vector", lambda e: e.memset(aacc[:], 0.0), inc="v")
            last_read = vtok
            for r in range(NR):
              for cc0 in range(NCW):
                c0, c1_ = cc0 * WCOL, min((cc0 + 1) * WCOL, CC)
                cw_ = c1_ - c0
                tl = em.dma("sync", wbuf[:, 0:cw_], wsrc[r * 128:(r + 1) * 128, c0:c1_],
                            deps=[last_read], stream="d_w")
                stok = em.emit("scalar", lambda e, cw_=cw_: e.activation(
                    jnk[:, 0:cw_], wbuf[:, 0:cw_], AF.Abs, accum_out=a1[:]),
                    deps=[tl], inc="s")
                last_read = stok
                vtok = em.emit("vector", lambda e: e.tensor_tensor(
                    aacc[:], aacc[:], a1[:], op=OP.add), deps=[stok, em.force(vtok)], inc="v")
            bi, bank, war = ps.take()
            tt = em.emit("tensor", lambda e, bank=bank: e.matmul(
                bank[0:1, 0:1], aacc[:], c_onesc[:], start=True, stop=True),
                deps=[vtok, war], inc="t")
            numel = float(R * CC)
            v1 = em.emit("vector", lambda e, bank=bank, numel=numel: e.tensor_scalar(
                m11[:], bank[0:1, 0:1], 1.0 / numel, 1e-5, op0=OP.mult, op1=OP.max),
                deps=[tt], inc="v")
            bi2, bank2, war2 = ps.take()
            t2 = em.emit("tensor", lambda e, bank2=bank2: e.matmul(
                bank2[:, 0:1], c_onesr[0:1, 0:128], m11[:], start=True, stop=True),
                deps=[v1, war2], inc="t")
            v2 = em.emit("vector", lambda e, bank2=bank2, j=j: e.tensor_copy(
                c_mclip[:, j:j + 1], bank2[:, 0:1]), deps=[t2], inc="v")
            ps.done(bi, v1); ps.done(bi2, v2)
            v3 = em.emit("vector", lambda e, j=j: e.reciprocal(
                c_wsb[:, j:j + 1], c_mclip[:, j:j + 1]), deps=[em.force(v2)], inc="v")
            v4 = em.emit("vector", lambda e, j=j: e.tensor_scalar(
                c_nwsb[:, j:j + 1], c_wsb[:, j:j + 1], -1.0, None, op0=OP.mult),
                deps=[em.force(v3)], inc="v")
            st_tok = None
            for r in range(NR):
              for cc0 in range(NCW):
                c0, c1_ = cc0 * WCOL, min((cc0 + 1) * WCOL, CC)
                cw_ = c1_ - c0
                tl = em.dma("sync", wbuf[:, 0:cw_], wsrc[r * 128:(r + 1) * 128, c0:c1_],
                            deps=[last_read, st_tok], stream="d_w")
                g1 = em.emit("vector", lambda e, cw_=cw_, j=j: e.tensor_scalar(
                    wg1[:, 0:cw_], wbuf[:, 0:cw_], c_wsb[:, j:j + 1], 0.5,
                    op0=OP.mult, op1=OP.is_gt), deps=[tl, em.force(v4)], inc="v")
                g2 = em.emit("vector", lambda e, cw_=cw_, j=j: e.tensor_scalar(
                    wg2[:, 0:cw_], wbuf[:, 0:cw_], c_nwsb[:, j:j + 1], 0.5,
                    op0=OP.mult, op1=OP.is_gt), inc="v")
                last_read = g2
                g3 = em.emit("vector", lambda e, cw_=cw_: e.tensor_tensor(
                    wqb[:, 0:cw_], wg1[:, 0:cw_], wg2[:, 0:cw_], op=OP.subtract), inc="v")
                st_tok = em.dma("gpsimd", wdst[r * 128:(r + 1) * 128, c0:c1_], wqb[:, 0:cw_],
                                deps=[g3], stream="d_st")
        em.emit("sync", lambda e: e.nop(), deps=[em.tok("d_st")])
    em.barrier()
    phase_reset()

    qk_f = sb("qk_f", (128, Tc), life="group")
    bd_f = sb("bd_f", (128, Tc), life="group")
    v_tm = sb("v_tm", (64, NC0 * V0), life="group")
    o_fm = sb("o_fm", (V0, Tc), life="group")
    wo0_sb = sb("wo0_sb", (V0, D), life="group")
    ld_wo = em.dma("sync", wo0_sb[:], Wo0[:], stream="d_w")

    if phases >= 1:
        w_qk = sb("p1_wqk", (128, ND * 128), life="phase")
        w_bd = sb("p1_wbd", (128, ND * 128), life="phase")
        w_v = sb("p1_wv", (128, ND * V0), life="phase")
        ldw = []
        for j in range(ND):
            ldw.append(em.dma("sync", w_qk[:, j * 128:(j + 1) * 128],
                              Wqk[j * 128:(j + 1) * 128, :], stream="d_w"))
            ldw.append(em.dma("sync", w_bd[:, j * 128:(j + 1) * 128],
                              Wbd[j * 128:(j + 1) * 128, :], stream="d_w"))
            ldw.append(em.dma("sync", w_v[:, j * V0:(j + 1) * V0],
                              Wv0[j * 128:(j + 1) * 128, :], stream="d_w"))
        xt = sb("p1_xt", (128, 2 * D), life="phase")
        xnt = sb("p1_xnt", (128, 2 * D), life="phase")
        xnf = sb("p1_xnf", (128, ND * 512), life="phase")
        sm = sb("p1_sm", (128, 1), life="phase"); sq = sb("p1_sq", (128, 1), life="phase")
        mcol = sb("p1_m", (128, 1), life="phase"); vcol = sb("p1_var", (128, 1), life="phase")
        rstd = sb("p1_rstd", (128, 1), life="phase")
        jnk1 = sb("p1_jnk", (128, D), life="phase")
        TT_ = min(512, Tc)
        NTT_ = Tc // TT_
        ntc_per_tt = TT_ // 128
        xn_done = [None] * NTC
        xnf_free = None
        for g in range(NTT_):
            for ii in range(ntc_per_tt):
                i = g * ntc_per_tt + ii
                sl = i % 2
                xv = xt[:, sl * D:(sl + 1) * D]
                xnv = xnt[:, sl * D:(sl + 1) * D]
                ld_xi = em.dma("sync", xv, x_in[i * 128:(i + 1) * 128, :],
                               deps=[xn_done[i - 2]] if i >= 2 else None,
                               stream=f"d_x{i % 2}")
                r1 = em.emit("vector", lambda e, xv=xv: e.reduce_sum(
                    sm[:], xv, axis=AX.X), deps=[ld_xi], inc="v")
                s1 = em.emit("scalar", lambda e, xv=xv: e.activation(
                    jnk1[:], xv, AF.Square, accum_out=sq[:]), deps=[ld_xi], inc="s")
                t_ = em.emit("vector", lambda e: e.tensor_scalar(
                    mcol[:], sm[:], 1.0 / D, None, op0=OP.mult),
                    deps=[em.force(r1)], inc="v")
                t_ = em.emit("vector", lambda e: e.tensor_scalar(
                    vcol[:], sq[:], 1.0 / D, None, op0=OP.mult),
                    deps=[s1, em.force(t_)], inc="v")
                t_ = em.emit("vector", lambda e: e.scalar_tensor_tensor(
                    vcol[:], mcol[:], mcol[:], vcol[:], op0=OP.mult, op1=OP.subtract),
                    deps=[em.force(t_)], inc="v")
                t_ = em.emit("vector", lambda e: e.tensor_scalar(
                    vcol[:], vcol[:], -1.0, 1e-5, op0=OP.mult, op1=OP.add),
                    deps=[em.force(t_)], inc="v")
                sv = em.emit("scalar", lambda e: e.activation(
                    rstd[:], vcol[:], AF.Sqrt), deps=[em.tok("v")], inc="s")
                rv = em.emit("vector", lambda e: e.reciprocal(
                    rstd[:], rstd[:]), deps=[sv], inc="v")
                xn_done[i] = em.emit("vector", lambda e, xv=xv, xnv=xnv: e.tensor_scalar(
                    xnv, xv, mcol[:], rstd[:], op0=OP.subtract, op1=OP.mult),
                    deps=[em.force(rv)], inc="v")
                if debug:
                    em.dma("gpsimd", p1x_dbg[i * 128:(i + 1) * 128, :], xv,
                           deps=[xn_done[i]], stream="d_st")
                    for jj, colsrc in enumerate([sm, sq, mcol, rstd]):
                        em.emit("gpsimd", lambda e, i=i, jj=jj, colsrc=colsrc: e.dma_start(
                            out=p1c_dbg[:, i * 4 + jj:i * 4 + jj + 1], in_=colsrc[:],
                            allow_slow_non_contiguous=True),
                            deps=[xn_done[i]], inc="d_st", add=16)
                em.dma("gpsimd", xn0_s[i * 128:(i + 1) * 128, :], xnv,
                       deps=[xn_done[i]], stream="d_st")
                for j in range(ND):
                    bi, bank, war = ps.take()
                    tt = em.emit("tensor", lambda e, bank=bank, xnv=xnv, j=j: e.transpose(
                        bank[:, 0:128], xnv[:, j * 128:(j + 1) * 128], c_ident[:]),
                        deps=[xn_done[i], war, xnf_free], inc="t")
                    ev = em.emit("vector", lambda e, bank=bank, j=j, ii=ii: e.tensor_copy(
                        xnf[:, j * 512 + ii * 128: j * 512 + (ii + 1) * 128], bank[:, 0:128]),
                        deps=[tt], inc="t_evac")
                    ps.done(bi, ev)
                bi, bank, war = ps.take()
                mm = None
                for j in range(ND):
                    mm = em.emit("tensor", lambda e, bank=bank, j=j, ii=ii, st=(j == 0), sp=(j == ND - 1):
                                 e.matmul(bank[:, 0:V0],
                                          xnf[:, j * 512 + ii * 128: j * 512 + (ii + 1) * 128],
                                          w_v[:, j * V0:(j + 1) * V0], start=st, stop=sp),
                                 deps=[em.tok("t_evac"), war] + ldw if j == 0 else None, inc="t")
                ev1 = em.emit("vector", lambda e, bank=bank, i=i: e.tensor_copy(
                    v_tm[:, (2 * i) * V0:(2 * i + 1) * V0], bank[0:64, 0:V0]), deps=[mm], inc="v")
                ev2 = em.emit("vector", lambda e, bank=bank, i=i: e.tensor_copy(
                    v_tm[:, (2 * i + 1) * V0:(2 * i + 2) * V0], bank[64:128, 0:V0]), inc="v")
                ps.done(bi, ev2)
            for (wsb_, dstf, bias) in ((w_qk, qk_f, None), (w_bd, bd_f, c_bbd)):
                bi, bank, war = ps.take()
                mm = None
                for j in range(ND):
                    mm = em.emit("tensor", lambda e, bank=bank, wsb_=wsb_, j=j, st=(j == 0), sp=(j == ND - 1):
                                 e.matmul(bank[:, 0:TT_], wsb_[:, j * 128:(j + 1) * 128],
                                          xnf[:, j * 512: j * 512 + TT_], start=st, stop=sp),
                                 deps=[em.tok("t_evac"), war], inc="t")
                if bias is None:
                    ev = em.emit("vector", lambda e, bank=bank, dstf=dstf, g=g: e.tensor_copy(
                        dstf[:, g * TT_:(g + 1) * TT_], bank[:, 0:TT_]), deps=[mm], inc="v")
                else:
                    ev = em.emit("vector", lambda e, bank=bank, dstf=dstf, g=g, bias=bias: e.tensor_scalar(
                        dstf[:, g * TT_:(g + 1) * TT_], bank[:, 0:TT_], bias[:], None, op0=OP.add),
                        deps=[mm], inc="v")
                ps.done(bi, ev)
                xnf_free = mm
    em.barrier()
    phase_reset()

    if phases >= 2:
        sp_f = sb("p2_sp", (128, Tc), life="phase")
        lam = sb("p2_lam", (64, Tc), life="phase")
        qt_all = sb("p2_qt", (64, Tc), life="phase")
        am_all = sb("p2_am", (64, Tc), life="phase")
        U_all = sb("p2_U", (64, NC0 * V0), life="phase")
        ec_all = sb("p2_ec", (64, NC0), life="phase")
        e1t = sb("p2_e1", (64, C0), life="phase"); e2t = sb("p2_e2", (64, C0), life="phase")
        kdt = sb("p2_kd", (64, C0), life="phase"); ktt = sb("p2_kt", (64, C0), life="phase"); kht = sb("p2_kh", (64, C0), life="phase")
        khT = sb("p2_khT", (C0, 64), life="phase")
        S_a = sb("p2_Sa", (64, V0), life="phase"); S_b = sb("p2_Sb", (64, V0), life="phase")
        S_in = sb("p2_Sin", (64, V0), life="phase")
        ones64 = c_onest[0:64, 0:C0]

        s1t = em.emit("scalar", lambda e: e.activation(
            sp_f[:], bd_f[:], AF.Exp, scale=c_pm1[:]), deps=[em.tok("v")], inc="s")
        v1t = em.emit("vector", lambda e: e.tensor_scalar(
            sp_f[:], sp_f[:], 1.0, None, op0=OP.add), deps=[s1t], inc="v")
        s2t = em.emit("scalar", lambda e: e.activation(
            sp_f[:], sp_f[:], AF.Ln), deps=[v1t], inc="s")
        sc_tok = s2t
        for c in range(NC0):
            sc_tok = em.emit("vector", lambda e, c=c: e.tensor_tensor_scan(
                lam[:, c * C0:(c + 1) * C0], ones64, sp_f[0:64, c * C0:(c + 1) * C0],
                0.0, op0=OP.mult, op1=OP.add), deps=[s2t], inc="v")

        prev_vk = None
        for c in range(NC0):
            cs = slice(c * C0, (c + 1) * C0)
            e1tok = em.emit("scalar", lambda e, cs=cs: e.activation(
                e1t[:], lam[:, cs], AF.Exp, scale=-1.0), deps=[sc_tok, prev_vk], inc="s")
            e2tok = em.emit("scalar", lambda e, cs=cs: e.activation(
                e2t[:], lam[:, cs], AF.Exp), inc="s")
            vq = em.emit("vector", lambda e, cs=cs: e.tensor_tensor(
                qt_all[:, cs], qk_f[0:64, cs], e1t[:], op=OP.mult), deps=[e1tok], inc="v")
            em.emit("vector", lambda e, c=c: e.tensor_copy(
                ec_all[:, c:c + 1], e1t[:, C0 - 1:C0]), inc="v")
            em.emit("vector", lambda e, cs=cs: e.tensor_tensor(
                kdt[:], qk_f[64:128, cs], sp_f[64:128, cs], op=OP.mult), deps=[e2tok], inc="v")
            em.emit("vector", lambda e: e.tensor_tensor(
                ktt[:], kdt[:], e2t[:], op=OP.mult), inc="v")
            vk = em.emit("vector", lambda e: e.tensor_scalar(
                kht[:], ktt[:], e1t[:, C0 - 1:C0], None, op0=OP.mult), inc="v")
            prev_vk = vk
            bi, bank, war = ps.take()
            ta = em.emit("tensor", lambda e, bank=bank, cs=cs: e.matmul(
                bank[0:C0, 0:C0], ktt[:], qt_all[:, cs], start=True, stop=True),
                deps=[vk, war], inc="t")
            va = em.emit("vector", lambda e, bank=bank, cs=cs: e.tensor_tensor(
                am_all[:, cs], bank[0:C0, 0:C0], c_mask0[:], op=OP.mult),
                deps=[ta], inc="v")
            ps.done(bi, va)
            bi, bank, war = ps.take()
            tt = em.emit("tensor", lambda e, bank=bank: e.transpose(
                bank[0:C0, 0:64], kht[:], c_ident[0:64, 0:64]), deps=[vk, war], inc="t")
            vt_ = em.emit("vector", lambda e, bank=bank: e.tensor_copy(
                khT[:], bank[0:C0, 0:64]), deps=[tt], inc="v")
            ps.done(bi, vt_)
            bi, bank, war = ps.take()
            tu = em.emit("tensor", lambda e, bank=bank, c=c: e.matmul(
                bank[0:64, 0:V0], khT[:], v_tm[:, c * V0:(c + 1) * V0], start=True, stop=True),
                deps=[vt_, war], inc="t")
            vu = em.emit("vector", lambda e, bank=bank, c=c: e.tensor_copy(
                U_all[:, c * V0:(c + 1) * V0], bank[0:64, 0:V0]), deps=[tu], inc="v")
            ps.done(bi, vu)
        vtok = em.emit("vector", lambda e: e.memset(S_a[:], 0.0), inc="v")
        for c in range(NC0):
            vtok = em.emit("vector", lambda e, c=c: e.scalar_tensor_tensor(
                S_a[:], S_a[:], ec_all[:, c:c + 1], U_all[:, c * V0:(c + 1) * V0],
                op0=OP.mult, op1=OP.add), inc="v")
        vc_ = em.emit("vector", lambda e: e.tensor_scalar(
            S_b[:], S_a[:], c_fe128[0:64, :], None, op0=OP.mult), deps=[vtok], inc="v")
        d1 = em.dma("gpsimd", cc0_i[:], S_b[:], deps=[vc_], stream="d_cc")
        cct = em.emit("gpsimd", lambda e: e.collective_compute(
            "AllReduce", OP.add, replica_groups=rg, ins=[cc0_i.ap().opt()],
            outs=[cc0_o.ap().opt()]), deps=[d1], inc="cc", add=1)
        d2 = em.dma("gpsimd", S_in[:], cc0_o[:], deps=[cct], stream="d_cc")
        vi = em.emit("vector", lambda e: e.tensor_scalar(
            S_in[:], S_in[:], c_fo128[0:64, :], None, op0=OP.mult), deps=[d2], inc="v")
        vtok = em.emit("vector", lambda e: e.tensor_copy(S_a[:], S_in[:]), deps=[vi], inc="v")
        cur = S_a
        for c in range(NC0):
            cs = slice(c * C0, (c + 1) * C0)
            bi, bank, war = ps.take()
            m1 = em.emit("tensor", lambda e, bank=bank, c=c, cs=cs: e.matmul(
                bank[0:V0, 0:C0], v_tm[:, c * V0:(c + 1) * V0], am_all[:, cs],
                start=True, stop=False), deps=[vtok, war], inc="t")
            m2 = em.emit("tensor", lambda e, bank=bank, cs=cs, cur=cur: e.matmul(
                bank[0:V0, 0:C0], cur[:], qt_all[:, cs], start=False, stop=True), inc="t")
            vo = em.emit("vector", lambda e, bank=bank, cs=cs: e.tensor_copy(
                o_fm[:, cs], bank[0:V0, 0:C0]), deps=[m2], inc="v")
            ps.done(bi, vo)
            nxt = S_b if cur is S_a else S_a
            vtok = em.emit("vector", lambda e, c=c, cur=cur, nxt=nxt: e.scalar_tensor_tensor(
                nxt[:], cur[:], ec_all[:, c:c + 1], U_all[:, c * V0:(c + 1) * V0],
                op0=OP.mult, op1=OP.add), deps=[vo], inc="v")
            cur = nxt
        if debug:
            em.dma("gpsimd", ofm_dbg[:], o_fm[:], deps=[em.tok("v")], stream="d_st")
            em.dma("gpsimd", qkf_dbg[:], qk_f[:], deps=[em.tok("v")], stream="d_st")
            em.dma("gpsimd", bdf_dbg[:], bd_f[:], deps=[em.tok("v")], stream="d_st")
            em.dma("gpsimd", vtm_dbg[:], v_tm[:], deps=[em.tok("v")], stream="d_st")
            em.dma("gpsimd", spf_dbg[:], sp_f[:], deps=[em.tok("v")], stream="d_st")
            em.dma("gpsimd", qt_dbg[:], qt_all[:], deps=[em.tok("v")], stream="d_st")
            em.dma("gpsimd", am_dbg[:], am_all[:], deps=[em.tok("v")], stream="d_st")
            em.dma("gpsimd", U_dbg[:], U_all[:], deps=[em.tok("v")], stream="d_st")
            em.dma("gpsimd", lam_dbg[:], lam[:], deps=[em.tok("v")], stream="d_st")
    em.barrier()
    phase_reset()

    if phases >= 3:
        xqf = sb("p3_xqf", (128, 2 * ND * 128), BF16, life="phase")
        hqf = sb("p3_hqf", (128, NDF * 128), BF16, life="phase")
        IBW = min(DFF, 4096)
        ibuf = sb("p3_ibuf", (128, IBW), I32, life="phase")
        xib = ibuf
        xqb = sb("p3_xqb", (128, D), BF16, life="phase")
        xnt3 = sb("p3_xnt", (128, D), life="phase")
        ht = sb("p3_ht", (128, DFF), life="phase")
        hqb = sb("p3_hqb", (128, DFF), BF16, life="phase")
        wq1 = sb("p3_wq1", (128, 2 * 512), BF16, life="phase")
        wq2 = sb("p3_wq2", (128, 2 * 512), BF16, life="phase")
        wq3 = sb("p3_wq3", (128, 2 * 512), BF16, life="phase")
        g0t = sb("p3_g0", (128, D), life="phase")
        xt3 = sb("p3_xt3", (128, D), life="phase")
        x1t = sb("p3_x1", (128, D), life="phase")
        utmp = sb("p3_utmp", (128, 512), life="phase")
        sc_in = sb("p3_scin", (128, 1), life="phase")
        colA = sb("p3_cA", (128, 1), life="phase"); colB = sb("p3_cB", (128, 1), life="phase")
        colC = sb("p3_cC", (128, 1), life="phase"); colAm = sb("p3_cAm", (128, 1), life="phase")
        sc1 = sb("p3_sc1", (128, 1), life="phase"); sc2 = sb("p3_sc2", (128, 1), life="phase"); sc3 = sb("p3_sc3", (128, 1), life="phase")
        smv = sb("p3_sm", (128, 1), life="phase"); sqv = sb("p3_sq", (128, 1), life="phase")
        sq1 = sb("p3_sq1", (128, 1), life="phase")
        mc3 = sb("p3_m", (128, 1), life="phase"); vc3 = sb("p3_v", (128, 1), life="phase"); rs3 = sb("p3_rstd", (128, 1), life="phase")

        def quant_cols(src_t, width, dep, who):
            nseg = (width + IBW - 1) // IBW
            em.emit("vector", lambda e: e.memset(sqv[:], 0.0), deps=[dep], inc="v")
            s1 = None
            for g_ in range(nseg):
                a0, a1_ = g_ * IBW, min((g_ + 1) * IBW, width)
                s1 = em.emit("scalar", lambda e, a0=a0, a1_=a1_: e.activation(
                    ibuf[:, 0:a1_ - a0], src_t[:, a0:a1_], AF.Square, accum_out=sq1[:]),
                    deps=[dep], inc="s")
                s1 = em.emit("vector", lambda e: e.tensor_tensor(
                    sqv[:], sqv[:], sq1[:], op=OP.add), deps=[s1, em.tok("v")], inc="v")
            src_ap = src_t[:, 0:width]
            r1 = em.emit("vector", lambda e: e.reduce_max(
                colAm[:], src_ap, axis=AX.X, apply_absolute_value=True), deps=[dep], inc="v")
            t_ = em.emit("vector", lambda e, width=width: e.tensor_scalar(
                colB[:], sqv[:], 1.0 / width, 1e-6, op0=OP.mult, op1=OP.add),
                deps=[em.force(s1)], inc="v")
            s2 = em.emit("scalar", lambda e: e.activation(
                colB[:], colB[:], AF.Sqrt), deps=[em.tok("v")], inc="s")
            t_ = em.emit("vector", lambda e: e.reciprocal(colB[:], colB[:]), deps=[s2], inc="v")
            t_ = em.emit("vector", lambda e: e.scalar_tensor_tensor(
                colA[:], colAm[:], 1.0, colB[:], op0=OP.mult, op1=OP.mult),
                deps=[em.force(r1), em.force(t_)], inc="v")
            t_ = em.emit("vector", lambda e: e.tensor_scalar(
                colA[:], colA[:], 1e-5, None, op0=OP.max), deps=[em.force(t_)], inc="v")
            t_ = em.emit("vector", lambda e: e.reciprocal(colC[:], colA[:]),
                         deps=[em.force(t_)], inc="v")
            t_ = em.emit("vector", lambda e: e.tensor_scalar(
                colA[:], colA[:], 1.0 / 127.0, None, op0=OP.mult),
                deps=[em.force(t_)], inc="v")
            t_ = em.emit("vector", lambda e: e.scalar_tensor_tensor(
                colC[:], colC[:], 127.0, colB[:], op0=OP.mult, op1=OP.mult),
                deps=[em.force(t_)], inc="v")
            return em.tok("v")

        slot_war1 = [None, None]
        slot_war2 = [None, None]
        slot_war3 = [None, None]
        xqf_war = [None, None]
        prev_iter = []
        hqf_war = None
        for i in range(NTC):
            tl = em.dma("sync", xnt3[:], xn0_s[i * 128:(i + 1) * 128, :],
                        deps=prev_iter, stream="d_x")
            qt_ = quant_cols(xnt3, D, tl, "x")
            vi1 = em.emit("vector", lambda e: e.tensor_scalar(
                xib[:, 0:D], xnt3[:], colC[:], None, op0=OP.mult), deps=[qt_], inc="v")
            vi2 = em.emit("vector", lambda e: e.tensor_scalar(
                xqb[:], xib[:, 0:D], 0.0, None, op0=OP.add), inc="v")
            em.emit("vector", lambda e: e.scalar_tensor_tensor(
                sc1[:], colA[:], 1.0, c_mclip[:, 0:1], op0=OP.mult, op1=OP.mult),
                deps=[em.tok("v")], inc="v")
            vs2 = em.emit("vector", lambda e: e.scalar_tensor_tensor(
                sc2[:], colA[:], 1.0, c_mclip[:, 1:2], op0=OP.mult, op1=OP.mult),
                deps=[em.tok("v")], inc="v")
            if debug:
                em.dma("gpsimd", xqi_dbg[i * 128:(i + 1) * 128, :], xqb[:],
                       deps=[vi2], stream="d_st")
            xsl = (i % 2) * ND * 128
            trt = []
            for j in range(ND):
                trt.append(em.dmat("scalar", xqf[:, xsl + j * 128:xsl + (j + 1) * 128],
                                   xqb[:, j * 128:(j + 1) * 128],
                                   deps=[vi2, xqf_war[i % 2]], stream="d_tr"))
            for n in range(DFF // 512):
                bi, bank, war = ps.take()
                bi2, bank2, war2 = ps.take()
                mG = mU = None
                for j in range(ND):
                    sl2 = j % 2
                    s0_, s1_ = sl2 * 512, sl2 * 512 + 512
                    l1 = em.dma("sync", wq1[:, s0_:s1_],
                                w1q_s[j * 128:(j + 1) * 128, n * 512:(n + 1) * 512],
                                deps=[slot_war1[sl2]], stream=f"d_wq1{sl2}")
                    mG = em.emit("tensor", lambda e, bank=bank, j=j, xsl=xsl, s0_=s0_, s1_=s1_, st=(j == 0), sp=(j == ND - 1):
                                 e.matmul(bank[:, 0:512],
                                          xqf[:, xsl + j * 128:xsl + (j + 1) * 128],
                                          wq1[:, s0_:s1_], start=st, stop=sp),
                                 deps=[l1, trt[-1], war], inc="t")
                    slot_war1[sl2] = mG
                for j in range(ND):
                    sl2 = j % 2
                    s0_, s1_ = sl2 * 512, sl2 * 512 + 512
                    l2 = em.dma("sync", wq2[:, s0_:s1_],
                                w2q_s[j * 128:(j + 1) * 128, n * 512:(n + 1) * 512],
                                deps=[slot_war2[sl2]], stream=f"d_wq2{sl2}")
                    mU = em.emit("tensor", lambda e, bank2=bank2, j=j, xsl=xsl, s0_=s0_, s1_=s1_, st=(j == 0), sp=(j == ND - 1):
                                 e.matmul(bank2[:, 0:512],
                                          xqf[:, xsl + j * 128:xsl + (j + 1) * 128],
                                          wq2[:, s0_:s1_], start=st, stop=sp),
                                 deps=[l2, war2], inc="t")
                    slot_war2[sl2] = mU
                    xqf_war[i % 2] = mU
                sg = em.emit("scalar", lambda e, bank=bank, n=n: e.activation(
                    ht[:, n * 512:(n + 1) * 512], bank[:, 0:512], AF.Silu, scale=sc1[:]),
                    deps=[mG, vs2], inc="s")
                vu_ = em.emit("vector", lambda e, bank2=bank2: e.tensor_scalar(
                    utmp[:], bank2[:, 0:512], sc2[:], None, op0=OP.mult),
                    deps=[mU], inc="v")
                hv = em.emit("vector", lambda e, n=n: e.tensor_tensor(
                    ht[:, n * 512:(n + 1) * 512], ht[:, n * 512:(n + 1) * 512],
                    utmp[:], op=OP.mult), deps=[sg, vu_], inc="v")
                ps.done(bi, hv); ps.done(bi2, hv)
            if debug:
                em.dma("gpsimd", ht_dbg[i * 128:(i + 1) * 128, :], ht[:, 0:DFF],
                       deps=[em.tok("v")], stream="d_st")
            qh_ = quant_cols(ht, DFF, em.tok("v"), "h")
            hv2 = None
            for g_ in range((DFF + IBW - 1) // IBW):
                a0, a1_ = g_ * IBW, min((g_ + 1) * IBW, DFF)
                em.emit("vector", lambda e, a0=a0, a1_=a1_: e.tensor_scalar(
                    ibuf[:, 0:a1_ - a0], ht[:, a0:a1_], colC[:], None, op0=OP.mult),
                    deps=[qh_], inc="v")
                hv2 = em.emit("vector", lambda e, a0=a0, a1_=a1_: e.tensor_scalar(
                    hqb[:, a0:a1_], ibuf[:, 0:a1_ - a0], 0.0, None, op0=OP.add), inc="v")
            em.emit("vector", lambda e: e.scalar_tensor_tensor(
                sc3[:], colA[:], 1.0, c_mclip[:, 2:3], op0=OP.mult, op1=OP.mult),
                deps=[em.tok("v")], inc="v")
            sc3tok = em.tok("v")
            if debug:
                em.dma("gpsimd", hqi_dbg[i * 128:(i + 1) * 128, :], hqb[:],
                       deps=[hv2], stream="d_st")
            trh = []
            for j in range(NDF):
                trh.append(em.dmat("scalar", hqf[:, j * 128:(j + 1) * 128],
                                   hqb[:, j * 128:(j + 1) * 128], deps=[hv2, hqf_war],
                                   stream="d_tr"))
            tlx = em.dma("sync", xt3[:], x_in[i * 128:(i + 1) * 128, :], stream="d_x2")
            for n in range(D // 512):
                bi, bank, war = ps.take()
                mD = None
                for j in range(NDF):
                    sl3 = j % 2
                    l3 = em.dma("sync", wq3[:, sl3 * 512:sl3 * 512 + 512],
                                w3q_s[j * 128:(j + 1) * 128, n * 512:(n + 1) * 512],
                                deps=[slot_war3[sl3]], stream=f"d_wq3{sl3}")
                    mD = em.emit("tensor", lambda e, bank=bank, j=j, sl3=sl3, st=(j == 0), sp=(j == NDF - 1):
                                 e.matmul(bank[:, 0:512], hqf[:, j * 128:(j + 1) * 128],
                                          wq3[:, sl3 * 512:sl3 * 512 + 512],
                                          start=st, stop=sp),
                                 deps=[l3, trh[-1], war], inc="t")
                    slot_war3[sl3] = mD
                hqf_war = mD
                sgd = em.emit("scalar", lambda e, bank=bank, n=n: e.activation(
                    g0t[:, n * 512:(n + 1) * 512], bank[:, 0:512], AF.Silu, scale=sc3[:]),
                    deps=[mD, sc3tok], inc="s")
                ps.done(bi, sgd)
                bi2, bank2, war2 = ps.take()
                ms = em.emit("tensor", lambda e, bank2=bank2, i=i, n=n: e.matmul(
                    bank2[:, 0:512], o_fm[:, i * 128:(i + 1) * 128],
                    wo0_sb[:, n * 512:(n + 1) * 512], start=True, stop=True),
                    deps=[war2, ld_wo], inc="t")
                vx = em.emit("vector", lambda e, bank2=bank2, n=n: e.tensor_tensor(
                    x1t[:, n * 512:(n + 1) * 512], bank2[:, 0:512],
                    g0t[:, n * 512:(n + 1) * 512], op=OP.mult), deps=[ms, sgd, tlx], inc="v")
                vx2 = em.emit("vector", lambda e, n=n: e.tensor_tensor(
                    x1t[:, n * 512:(n + 1) * 512], x1t[:, n * 512:(n + 1) * 512],
                    xt3[:, n * 512:(n + 1) * 512], op=OP.add), inc="v")
                ps.done(bi2, vx2)
            x1done = em.tok("v")
            em.dma("gpsimd", x1_s[i * 128:(i + 1) * 128, :], x1t[:], deps=[x1done], stream="d_st")
            r1 = em.emit("vector", lambda e: e.reduce_sum(smv[:], x1t[:], axis=AX.X),
                         deps=[x1done], inc="v")
            s1_ = em.emit("scalar", lambda e: e.activation(
                g0t[:, 0:D], x1t[:], AF.Square, accum_out=sqv[:]), deps=[x1done], inc="s")
            t_ = em.emit("vector", lambda e: e.tensor_scalar(
                mc3[:], smv[:], 1.0 / D, None, op0=OP.mult), deps=[em.force(r1)], inc="v")
            t_ = em.emit("vector", lambda e: e.tensor_scalar(
                vc3[:], sqv[:], 1.0 / D, None, op0=OP.mult),
                deps=[s1_, em.force(t_)], inc="v")
            t_ = em.emit("vector", lambda e: e.scalar_tensor_tensor(
                vc3[:], mc3[:], mc3[:], vc3[:], op0=OP.mult, op1=OP.subtract),
                deps=[em.force(t_)], inc="v")
            t_ = em.emit("vector", lambda e: e.tensor_scalar(
                vc3[:], vc3[:], -1.0, 1e-5, op0=OP.mult, op1=OP.add),
                deps=[em.force(t_)], inc="v")
            sv = em.emit("scalar", lambda e: e.activation(rs3[:], vc3[:], AF.Sqrt),
                         deps=[em.tok("v")], inc="s")
            rv3 = em.emit("vector", lambda e: e.reciprocal(rs3[:], rs3[:]), deps=[sv], inc="v")
            xn1done = em.emit("vector", lambda e: e.tensor_scalar(
                xnt3[:], x1t[:], mc3[:], rs3[:], op0=OP.subtract, op1=OP.mult),
                deps=[em.force(rv3)], inc="v")
            prev_dma = None
            for j in range(ND):
                bi, bank, war = ps.take()
                tt = em.emit("tensor", lambda e, bank=bank, j=j: e.transpose(
                    bank[:, 0:128], xnt3[:, j * 128:(j + 1) * 128], c_ident[:]),
                    deps=[xn1done, war], inc="t")
                ev = em.emit("vector", lambda e, bank=bank, j=j: e.tensor_copy(
                    g0t[:, (j % (D // 128)) * 128:(j % (D // 128)) * 128 + 128], bank[:, 0:128]),
                    deps=[tt, prev_dma], inc="v")
                prev_dma = em.dma("gpsimd", xn1f_s[j * 128:(j + 1) * 128, i * 128:(i + 1) * 128],
                                  g0t[:, (j % (D // 128)) * 128:(j % (D // 128)) * 128 + 128],
                                  deps=[ev], stream="d_st")
                ps.done(bi, prev_dma)
            prev_iter = [em.tok("d_st"), em.tok("d_tr"), em.tok("s"), em.tok("v")]
    em.barrier()
    phase_reset()
    group_reset()

    NMZ = VD1 // 128
    NQK = (2 * KD1) // 128
    NQ = KD1 // 128
    gb2 = sb("p4_gb2", (64, Tc), life="group")
    GbT_all = sb("p4_GbT", (128, NC1 * 64), life="group")
    GCb_all = sb("p4_GCb", (128, NC1 * NV), life="group")
    gamT_all = sb("p4_gamT", (128, NC1 * NV), life="group")
    gCb_all = sb("p4_gCb", (128, NC1 * NV), life="group")
    ratT_all = sb("p4_ratT", (128, NC1 * NV), life="group")
    if phases >= 4:
        ba_f = sb("p4_ba", (64, Tc), life="phase")
        xb4 = sb("p4_xb", (128, ND * 512), life="phase")
        wb4 = sb("p4_wb", (128, 2 * ND * 128), life="phase")
        ev4 = sb("p4_ev", (128, 512), life="phase")
        halo = sb("p4_halo", (128, NM * (KS - 1)), life="phase")
        halo_in = sb("p4_halo_in", (128, NM * (KS - 1)), life="phase")
        TT4 = min(512, Tc)
        NT4 = Tc // TT4
        MBLK = NM + NMZ + 1
        xld = None
        wwar_sl = [None, None]
        ev4_war = None
        hv4 = em.emit("vector", lambda e: e.memset(halo[:], 0.0), inc="v")
        for n in range(NT4):
            xld_list = []
            for j in range(ND):
                xld_list.append(em.dma("sync", xb4[:, j * TT4:(j + 1) * TT4],
                                       xn1f_s[j * 128:(j + 1) * 128, n * TT4:(n + 1) * TT4],
                                       deps=[xld], stream="d_x"))
            for m in range(MBLK):
                if m < NM:
                    wsrc, wcol = WqkvT, m * 128
                    wwid = 128
                elif m < NM + NMZ:
                    wsrc, wcol = WzT, (m - NM) * 128
                    wwid = 128
                else:
                    wsrc, wcol = WbaT, 0
                    wwid = 64
                slm = m % 2
                wbase = slm * ND * 128
                wld = []
                for j in range(ND):
                    wld.append(em.dma("sync", wb4[:, wbase + j * 128:wbase + j * 128 + wwid],
                                      wsrc[j * 128:(j + 1) * 128, wcol:wcol + wwid],
                                      deps=[wwar_sl[slm]], stream=f"d_p4w{slm}"))
                bi, bank, war = ps.take()
                mmw = None
                for j in range(ND):
                    mmw = em.emit("tensor", lambda e, bank=bank, j=j, wwid=wwid, wbase=wbase, st=(j == 0), sp=(j == ND - 1):
                                  e.matmul(bank[0:wwid, 0:TT4], wb4[:, wbase + j * 128:wbase + j * 128 + wwid],
                                           xb4[:, j * TT4:(j + 1) * TT4], start=st, stop=sp),
                                  deps=[wld[-1], xld_list[-1], war], inc="t")
                wwar_sl[slm] = mmw
                if m < NM + NMZ:
                    vv = em.emit("vector", lambda e, bank=bank, wwid=wwid: e.tensor_copy(
                        ev4[:wwid, 0:TT4], bank[0:wwid, 0:TT4]), deps=[mmw, ev4_war], inc="v")
                    dst = qkvp_s if m < NM else z1_s
                    row = m * 128 if m < NM else (m - NM) * 128
                    sd = em.dma("gpsimd", dst[row:row + 128, n * TT4:(n + 1) * TT4],
                                ev4[:, 0:TT4], deps=[vv], stream="d_st")
                    ev4_war = sd
                    if m < NM and n == NT4 - 1:
                        vv2 = em.emit("vector", lambda e, bank=bank, m=m: e.tensor_scalar(
                            halo[:, m * (KS - 1):(m + 1) * (KS - 1)],
                            bank[:, TT4 - (KS - 1):TT4], c_fe128[:], None, op0=OP.mult),
                            deps=[mmw, hv4], inc="v")
                        ps.done(bi, vv2)
                    else:
                        ps.done(bi, sd)
                else:
                    vv = em.emit("vector", lambda e, bank=bank, n=n: e.tensor_copy(
                        ba_f[:, n * TT4:(n + 1) * TT4], bank[0:64, 0:TT4]),
                        deps=[mmw], inc="v")
                    ps.done(bi, vv)
            xld = em.tok("t")
        d1 = em.dma("gpsimd", cch_i[:], halo[:], deps=[em.tok("v")], stream="d_cc")
        cht = em.emit("gpsimd", lambda e: e.collective_compute(
            "AllReduce", OP.add, replica_groups=rg, ins=[cch_i.ap().opt()],
            outs=[cch_o.ap().opt()]), deps=[d1], inc="cc", add=1)
        d2 = em.dma("gpsimd", halo_in[:], cch_o[:], deps=[cht], stream="d_cc")
        vhi = em.emit("vector", lambda e: e.tensor_scalar(
            halo_in[:], halo_in[:], c_fo128[:], None, op0=OP.mult), deps=[d2], inc="v")
        em.emit("vector", lambda e: e.memset(gb2[:], 0.0), inc="v")
        sB = em.emit("scalar", lambda e: e.activation(
            gb2[32:32 + NV, :], ba_f[0:NV, :], AF.Sigmoid), deps=[em.tok("v")], inc="s")
        sE = em.emit("scalar", lambda e: e.activation(
            gb2[0:NV, :], ba_f[32:32 + NV, :], AF.Exp, bias=c_dtb[32:32 + NV, :]), inc="s")
        vE = em.emit("vector", lambda e: e.tensor_scalar(
            gb2[0:NV, :], gb2[0:NV, :], 1.0, None, op0=OP.add), deps=[sE], inc="v")
        sL = em.emit("scalar", lambda e: e.activation(
            gb2[0:NV, :], gb2[0:NV, :], AF.Ln), deps=[vE], inc="s")
        vG = em.emit("vector", lambda e: e.tensor_scalar(
            gb2[0:NV, :], gb2[0:NV, :], c_negexpA[:], None, op0=OP.mult), deps=[sL], inc="v")
        sc_t = vG
        for c in range(NC1):
            sc_t = em.emit("vector", lambda e, c=c: e.tensor_tensor_scan(
                gb2[0:NV, c * C1:(c + 1) * C1], c_onest[0:NV, 0:C1],
                gb2[0:NV, c * C1:(c + 1) * C1], 0.0, op0=OP.mult, op1=OP.add),
                deps=[vG], inc="v")
        for c in range(NC1):
            bi, bank, war = ps.take()
            ttk = em.emit("tensor", lambda e, bank=bank, c=c: e.transpose(
                bank[:, 0:64], gb2[:, c * C1:(c + 1) * C1], c_ident[0:64, 0:64]),
                deps=[sc_t, sB, war], inc="t")
            vtk = em.emit("vector", lambda e, bank=bank, c=c: e.tensor_copy(
                GbT_all[:, c * 64:(c + 1) * 64], bank[:, 0:64]),
                deps=[ttk], inc="v")
            ps.done(bi, vtk)
            bi, bank, war = ps.take()
            tgc = em.emit("tensor", lambda e, bank=bank, c=c: e.matmul(
                bank[:, 0:NV], c_lastrow[:], GbT_all[:, c * 64:c * 64 + NV],
                start=True, stop=True), deps=[vtk, war], inc="t")
            vgc = em.emit("vector", lambda e, bank=bank, c=c: e.tensor_copy(
                GCb_all[:, c * NV:(c + 1) * NV], bank[:, 0:NV]), deps=[tgc], inc="v")
            ps.done(bi, vgc)
            em.emit("scalar", lambda e, c=c: e.activation(
                gamT_all[:, c * NV:(c + 1) * NV], GbT_all[:, c * 64:c * 64 + NV],
                AF.Exp), deps=[vtk], inc="s")
            em.emit("scalar", lambda e, c=c: e.activation(
                gCb_all[:, c * NV:(c + 1) * NV], GCb_all[:, c * NV:(c + 1) * NV],
                AF.Exp), deps=[vgc], inc="s")
            vr = em.emit("vector", lambda e, c=c: e.tensor_tensor(
                ratT_all[:, c * NV:(c + 1) * NV], GCb_all[:, c * NV:(c + 1) * NV],
                GbT_all[:, c * 64:c * 64 + NV], op=OP.subtract),
                deps=[em.force(vgc)], inc="v")
            em.emit("scalar", lambda e, c=c: e.activation(
                ratT_all[:, c * NV:(c + 1) * NV], ratT_all[:, c * NV:(c + 1) * NV],
                AF.Exp), deps=[vr], inc="s")
        qcp = sb("p4_qcp", (128, Tc + KS - 1), life="phase")
        qcv = sb("p4_qcv", (128, Tc), life="phase")
        qcs = sb("p4_qcs", (128, Tc), life="phase")
        qsq = sb("p4_qsq", (128, Tc), life="phase")
        rq2 = sb("p4_rq2", (2, 512), life="phase")
        prevm = [vhi]
        for m in range(NM):
            vp = em.emit("vector", lambda e, m=m: e.tensor_copy(
                qcp[:, 0:KS - 1], halo_in[:, m * (KS - 1):(m + 1) * (KS - 1)]),
                deps=[em.force(vhi)] + prevm, inc="v")
            lq = em.dma("sync", qcp[:, KS - 1:KS - 1 + Tc],
                        qkvp_s[m * 128:(m + 1) * 128, :], deps=[em.tok("d_st")] + prevm,
                        stream="d_x")
            cv0 = em.emit("vector", lambda e, m=m: e.tensor_scalar(
                qcv[:], qcp[:, 0:Tc], c_convw[:, m * KS:m * KS + 1], None, op0=OP.mult),
                deps=[lq, em.force(vp)], inc="v")
            for sft in range(1, KS):
                cv0 = em.emit("vector", lambda e, m=m, sft=sft: e.scalar_tensor_tensor(
                    qcv[:], qcp[:, sft:sft + Tc], c_convw[:, m * KS + sft:m * KS + sft + 1],
                    qcv[:], op0=OP.mult, op1=OP.add), inc="v")
            sq_ = em.emit("scalar", lambda e: e.activation(qcs[:], qcv[:], AF.Silu),
                          deps=[cv0], inc="s")
            prevm = [sq_]
            if m < NQK:
                ssq = em.emit("scalar", lambda e: e.activation(qsq[:], qcs[:], AF.Square),
                              deps=[sq_], inc="s")
                for n2 in range(Tc // 512):
                    bi, bank, war = ps.take()
                    tn = em.emit("tensor", lambda e, bank=bank, n2=n2: e.matmul(
                        bank[0:2, 0:512], c_blk64[:, 0:2], qsq[:, n2 * 512:(n2 + 1) * 512],
                        start=True, stop=True), deps=[ssq, war], inc="t")
                    vn = em.emit("vector", lambda e, bank=bank: e.tensor_scalar(
                        rq2[:], bank[0:2, 0:512], 1e-6, None, op0=OP.add),
                        deps=[tn], inc="v")
                    ps.done(bi, vn)
                    sn = em.emit("scalar", lambda e: e.activation(rq2[:], rq2[:], AF.Sqrt),
                                 deps=[vn], inc="s")
                    vn2 = em.emit("vector", lambda e: e.reciprocal(rq2[:], rq2[:]),
                                  deps=[sn], inc="v")
                    if m < NQ:
                        vn2 = em.emit("vector", lambda e: e.tensor_scalar(
                            rq2[:], rq2[:], float(DK) ** -0.5, None, op0=OP.mult), inc="v")
                    bi, bank, war = ps.take()
                    tb = em.emit("tensor", lambda e, bank=bank: e.matmul(
                        bank[:, 0:512], c_exp64[:, 0:128], rq2[:], start=True, stop=True),
                        deps=[vn2, war], inc="t")
                    vb_ = em.emit("vector", lambda e, bank=bank, n2=n2: e.tensor_tensor(
                        qcs[:, n2 * 512:(n2 + 1) * 512], qcs[:, n2 * 512:(n2 + 1) * 512],
                        bank[:, 0:512], op=OP.mult), deps=[tb], inc="v")
                    ps.done(bi, vb_)
                prevm = [em.tok("v")]
            prevm.append(em.dma("gpsimd", qkvc_s[m * 128:(m + 1) * 128, :], qcs[:],
                                deps=prevm, stream="d_st"))
    em.barrier()
    phase_reset()

    NVB = VD1 // 128
    og = sb("p5_og", (128, NVB * Tc), life="group")
    if phases >= 5:
        qh_sb = sb("p5_qh", (64, Tc), life="phase"); kh_sb = sb("p5_kh", (64, Tc), life="phase")
        vh_sb = sb("p5_vh", (64, Tc), life="phase")
        zh_sb = sb("p5_zh", (64, C1), life="phase")
        araw_all = sb("p5_araw", (C1, NC1 * C1), life="phase")
        mraw_all = sb("p5_mraw", (C1, NC1 * C1), life="phase")
        ktr_all = sb("p5_ktr", (C1, NC1 * 64), life="phase")
        wt_all = sb("p5_wt", (64, NC1 * C1), life="phase")
        rl_all = sb("p5_rl", (C1, NC1 * 64), life="phase")
        mt_all = sb("p5_mt", (C1, NC1 * C1), life="phase")
        kb_all = sb("p5_kb", (C1, NC1 * 64), life="phase")
        dA = sb("p5_dA", (C1, C1), life="phase"); dAT = sb("p5_dAT", (C1, C1), life="phase")
        bUT = sb("p5_bUT", (C1, C1), life="phase")
        n_sb = sb("p5_n", (C1, C1), life="phase"); nt_sb = sb("p5_nt", (C1, C1), life="phase")
        tt_a = sb("p5_tta", (C1, C1), life="phase"); tt_b = sb("p5_ttb", (C1, C1), life="phase")
        b_a = sb("p5_ba", (C1, C1), life="phase"); b_b = sb("p5_bb", (C1, C1), life="phase")
        cb_a = sb("p5_cba", (C1, C1), life="phase"); cb_b = sb("p5_cbb", (C1, C1), life="phase")
        kbg = sb("p5_kbg", (C1, 64), life="phase"); w_sb = sb("p5_w", (C1, 64), life="phase")
        vtr = sb("p5_vtr", (C1, 64), life="phase"); vb_sb = sb("p5_vb", (C1, 64), life="phase")
        mct = sb("p5_mct", (64, 64), life="phase")
        f_a = sb("p5_fa", (64, 64), life="phase"); f_b = sb("p5_fb", (64, 64), life="phase")
        s_a = sb("p5_sa", (64, 64), life="phase"); s_b = sb("p5_sb", (64, 64), life="phase")
        tmp64 = sb("p5_t64", (64, 64), life="phase")
        bgcol = sb("p5_bg", (C1, 1), life="phase")
        o12 = sb("p5_o12", (C1, 64), life="phase"); r_sb = sb("p5_r", (C1, 64), life="phase")
        ofT = sb("p5_ofT", (64, C1), life="phase"); zs_sb = sb("p5_zs", (64, C1), life="phase")
        sso = sb("p5_sso", (128, 1), life="phase"); ro = sb("p5_ro", (128, 1), life="phase")
        jnk5 = sb("p5_jnk5", (C1, 64), life="phase")
        growh = sb("p5_growh", (1, Tc), life="phase")
        browh = sb("p5_browh", (1, Tc), life="phase")
        ev5 = sb("p5_ev5", (128, 512), life="phase")

        def gcol(c, h):
            return GbT_all[:, c * 64 + h:c * 64 + h + 1]

        def bcol(c, h):
            return GbT_all[:, c * 64 + 32 + h:c * 64 + 32 + h + 1]

        for h in range(NV):
            kg = h // rep
            shr_tok = None
            if h % rep == 0:
                lq_ = em.dma("sync", qh_sb[:], qkvc_s[kg * 64:(kg + 1) * 64, :],
                             deps=[em.tok("t")], stream="d_q5")
                lk_ = em.dma("sync", kh_sb[:], qkvc_s[KD1 + kg * 64:KD1 + (kg + 1) * 64, :],
                             stream="d_k5")
                for c in range(NC1):
                    cs = slice(c * C1, (c + 1) * C1)
                    bi, bank, war = ps.take()
                    ta = em.emit("tensor", lambda e, bank=bank, cs=cs: e.matmul(
                        bank[0:C1, 0:C1], kh_sb[:, cs], kh_sb[:, cs], start=True, stop=True),
                        deps=[lk_, war], inc="t")
                    va = em.emit("vector", lambda e, bank=bank, c=c: e.tensor_copy(
                        araw_all[:, c * C1:(c + 1) * C1], bank[0:C1, 0:C1]),
                        deps=[ta], inc="v")
                    ps.done(bi, va)
                    bi, bank, war = ps.take()
                    tm_ = em.emit("tensor", lambda e, bank=bank, cs=cs: e.matmul(
                        bank[0:C1, 0:C1], kh_sb[:, cs], qh_sb[:, cs], start=True, stop=True),
                        deps=[lq_, war], inc="t")
                    vm_ = em.emit("vector", lambda e, bank=bank, c=c: e.tensor_copy(
                        mraw_all[:, c * C1:(c + 1) * C1], bank[0:C1, 0:C1]),
                        deps=[tm_], inc="v")
                    ps.done(bi, vm_)
                    bi, bank, war = ps.take()
                    tk_ = em.emit("tensor", lambda e, bank=bank, cs=cs: e.transpose(
                        bank[0:C1, 0:64], kh_sb[:, cs], c_ident[0:64, 0:64]), deps=[lk_, war], inc="t")
                    vk_ = em.emit("vector", lambda e, bank=bank, c=c: e.tensor_copy(
                        ktr_all[:, c * 64:(c + 1) * 64], bank[0:C1, 0:64]),
                        deps=[tk_], inc="v")
                    ps.done(bi, vk_)
                shr_tok = em.tok("v")
            lv_ = em.dma("sync", vh_sb[:], qkvc_s[2 * KD1 + h * 64:2 * KD1 + (h + 1) * 64, :],
                         deps=[em.tok("t")], stream="d_v5")

            lgr = em.dma("gpsimd", growh[0:1, :], gb2[h:h + 1, :],
                         deps=[em.tok("t")], stream="d_gr")
            lbr = em.dma("gpsimd", browh[0:1, :], gb2[32 + h:32 + h + 1, :], stream="d_br")
            fz = em.emit("vector", lambda e: e.memset(f_a[:], 0.0), deps=[shr_tok], inc="v")
            f_cur, f_nxt = f_a, f_b
            for c in range(NC1):
                cs = slice(c * C1, (c + 1) * C1)
                bi, bank, war = ps.take()
                tg_ = em.emit("tensor", lambda e, bank=bank, c=c: e.matmul(
                    bank[0:C1, 0:C1], c_onesr[0:1, 0:C1],
                    growh[0:1, c * C1:(c + 1) * C1], start=True, stop=True),
                    deps=[war, em.tok("v"), lgr], inc="t")
                v1_ = em.emit("vector", lambda e, bank=bank, c=c, h=h: e.tensor_scalar(
                    dA[:], bank[0:C1, 0:C1], gcol(c, h), 0.0, op0=OP.subtract, op1=OP.max),
                    deps=[tg_], inc="v")
                v2_ = em.emit("vector", lambda e, bank=bank, c=c, h=h: e.tensor_scalar(
                    dAT[:], bank[0:C1, 0:C1], gcol(c, h), 0.0, op0=OP.subtract, op1=OP.min),
                    inc="v")
                ps.done(bi, v2_)
                sa_ = em.emit("scalar", lambda e: e.activation(dA[:], dA[:], AF.Exp, scale=-1.0),
                              deps=[v1_], inc="s")
                sat = em.emit("scalar", lambda e: e.activation(dAT[:], dAT[:], AF.Exp),
                              deps=[v2_], inc="s")
                bi, bank, war = ps.take()
                tb_ = em.emit("tensor", lambda e, bank=bank, c=c: e.matmul(
                    bank[0:C1, 0:C1], c_onesr[0:1, 0:C1],
                    browh[0:1, c * C1:(c + 1) * C1], start=True, stop=True),
                    deps=[war, lbr], inc="t")
                vbu = em.emit("vector", lambda e, bank=bank: e.tensor_tensor(
                    bUT[:], bank[0:C1, 0:C1], c_uts[:], op=OP.mult), deps=[tb_], inc="v")
                ps.done(bi, vbu)
                em.emit("vector", lambda e, c=c, h=h, cs=cs: e.scalar_tensor_tensor(
                    n_sb[:], dA[:], bcol(c, h), araw_all[:, cs], op0=OP.mult, op1=OP.mult),
                    deps=[sa_], inc="v")
                em.emit("vector", lambda e: e.tensor_tensor(
                    n_sb[:], n_sb[:], c_lts[:], op=OP.mult), inc="v")
                em.emit("vector", lambda e, cs=cs: e.tensor_tensor(
                    nt_sb[:], dAT[:], araw_all[:, cs], op=OP.mult), deps=[sat], inc="v")
                em.emit("vector", lambda e: e.tensor_tensor(
                    nt_sb[:], nt_sb[:], bUT[:], op=OP.mult), inc="v")
                vI = em.emit("vector", lambda e: e.tensor_tensor(
                    tt_a[:], c_ident[:], nt_sb[:], op=OP.subtract), inc="v")
                tt_cur, tt_nxt = tt_a, tt_b
                b_cur, cb_cur = n_sb, nt_sb
                b_nxt, cb_nxt = b_a, cb_a
                kk = 1
                while 2 * kk < C1:
                    bi, bank, war = ps.take()
                    tq = em.emit("tensor", lambda e, bank=bank, cb_cur=cb_cur, b_cur=b_cur:
                                 e.matmul(bank[0:C1, 0:C1], cb_cur[:], b_cur[:],
                                          start=True, stop=True),
                                 deps=[vI, war, em.tok("v")], inc="t")
                    vq = em.emit("vector", lambda e, bank=bank, b_nxt=b_nxt: e.tensor_copy(
                        b_nxt[:], bank[0:C1, 0:C1]), deps=[tq], inc="v")
                    ps.done(bi, vq)
                    if 4 * kk < C1:
                        bi, bank, war = ps.take()
                        tq2 = em.emit("tensor", lambda e, bank=bank, cb_cur=cb_cur, b_cur=b_cur:
                                      e.matmul(bank[0:C1, 0:C1], b_cur[:], cb_cur[:],
                                               start=True, stop=True), deps=[war], inc="t")
                        vq2 = em.emit("vector", lambda e, bank=bank, cb_nxt=cb_nxt: e.tensor_copy(
                            cb_nxt[:], bank[0:C1, 0:C1]), deps=[tq2], inc="v")
                        ps.done(bi, vq2)
                    bi, bank, war = ps.take()
                    tp = em.emit("tensor", lambda e, bank=bank, b_nxt=b_nxt, tt_cur=tt_cur:
                                 e.matmul(bank[0:C1, 0:C1], b_nxt[:], tt_cur[:],
                                          start=True, stop=True),
                                 deps=[vq, war, em.tok("v")], inc="t")
                    vp_ = em.emit("vector", lambda e, bank=bank, tt_nxt=tt_nxt, tt_cur=tt_cur:
                                  e.tensor_tensor(tt_nxt[:], bank[0:C1, 0:C1], tt_cur[:],
                                                  op=OP.add), deps=[tp], inc="v")
                    ps.done(bi, vp_)
                    tt_cur, tt_nxt = tt_nxt, tt_cur
                    b_cur, cb_cur = b_nxt, cb_nxt
                    b_nxt = b_b if b_nxt is b_a else b_a
                    cb_nxt = cb_b if cb_nxt is cb_a else cb_a
                    kk *= 2
                vbg = em.emit("vector", lambda e, c=c, h=h: e.tensor_tensor(
                    bgcol[:], gamT_all[:, c * NV + h:c * NV + h + 1], bcol(c, h),
                    op=OP.mult), inc="v")
                vkbg = em.emit("vector", lambda e, c=c: e.tensor_scalar(
                    kbg[:], ktr_all[:, c * 64:(c + 1) * 64], bgcol[:], None, op0=OP.mult),
                    deps=[em.force(vbg)], inc="v")
                bi, bank, war = ps.take()
                tw = em.emit("tensor", lambda e, bank=bank, tt_cur=tt_cur: e.matmul(
                    bank[0:C1, 0:64], tt_cur[:], kbg[:], start=True, stop=True),
                    deps=[vkbg, war, em.tok("v")], inc="t")
                vw = em.emit("vector", lambda e, bank=bank: e.tensor_copy(
                    w_sb[:], bank[0:C1, 0:64]), deps=[tw], inc="v")
                ps.done(bi, vw)
                bi, bank, war = ps.take()
                twt = em.emit("tensor", lambda e, bank=bank, tt_cur=tt_cur: e.matmul(
                    bank[0:64, 0:C1], kbg[:], tt_cur[:], start=True, stop=True),
                    deps=[vkbg, war], inc="t")
                vwt = em.emit("vector", lambda e, bank=bank, c=c: e.tensor_copy(
                    wt_all[:, c * C1:(c + 1) * C1], bank[0:64, 0:C1]), deps=[twt], inc="v")
                ps.done(bi, vwt)
                bi, bank, war = ps.take()
                tv_ = em.emit("tensor", lambda e, bank=bank, cs=cs: e.transpose(
                    bank[0:C1, 0:64], vh_sb[:, cs], c_ident[0:64, 0:64]), deps=[lv_, war], inc="t")
                vv_ = em.emit("vector", lambda e, bank=bank: e.tensor_copy(
                    vtr[:], bank[0:C1, 0:64]), deps=[tv_], inc="v")
                ps.done(bi, vv_)
                vvb = em.emit("vector", lambda e, c=c, h=h: e.tensor_scalar(
                    vb_sb[:], vtr[:], bcol(c, h), None, op0=OP.mult), inc="v")
                bi, bank, war = ps.take()
                trl = em.emit("tensor", lambda e, bank=bank, tt_cur=tt_cur: e.matmul(
                    bank[0:C1, 0:64], tt_cur[:], vb_sb[:], start=True, stop=True),
                    deps=[vvb, war], inc="t")
                vrl = em.emit("vector", lambda e, bank=bank, c=c: e.tensor_copy(
                    rl_all[:, c * 64:(c + 1) * 64], bank[0:C1, 0:64]), deps=[trl], inc="v")
                ps.done(bi, vrl)
                vkb = em.emit("vector", lambda e, c=c, h=h: e.tensor_scalar(
                    kb_all[:, c * 64:(c + 1) * 64], ktr_all[:, c * 64:(c + 1) * 64],
                    ratT_all[:, c * NV + h:c * NV + h + 1], None, op0=OP.mult), inc="v")
                em.emit("vector", lambda e, cs=cs: e.tensor_tensor(
                    mt_all[:, cs], dAT[:], mraw_all[:, cs], op=OP.mult), inc="v")
                vmt = em.emit("vector", lambda e, cs=cs: e.tensor_tensor(
                    mt_all[:, cs], mt_all[:, cs], c_uti[:], op=OP.mult), inc="v")
                bi, bank, war = ps.take()
                tmc = em.emit("tensor", lambda e, bank=bank, c=c: e.matmul(
                    bank[0:64, 0:64], w_sb[:], kb_all[:, c * 64:(c + 1) * 64],
                    start=True, stop=True), deps=[vw, vkb, war], inc="t")
                vmc = em.emit("vector", lambda e, bank=bank, c=c, h=h: e.scalar_tensor_tensor(
                    mct[:], c_ident[0:64, 0:64], gCb_all[0:64, c * NV + h:c * NV + h + 1],
                    bank[0:64, 0:64], op0=OP.mult, op1=OP.subtract), deps=[tmc], inc="v")
                ps.done(bi, vmc)
                bi, bank, war = ps.take()
                tf1 = em.emit("tensor", lambda e, bank=bank, f_cur=f_cur: e.matmul(
                    bank[0:64, 0:64], mct[:], f_cur[:], start=True, stop=False),
                    deps=[vmc, fz, war, em.tok("v")], inc="t")
                tf2 = em.emit("tensor", lambda e, bank=bank, c=c: e.matmul(
                    bank[0:64, 0:64], kb_all[:, c * 64:(c + 1) * 64],
                    rl_all[:, c * 64:(c + 1) * 64], start=False, stop=True), inc="t")
                vf = em.emit("vector", lambda e, bank=bank, f_nxt=f_nxt: e.tensor_copy(
                    f_nxt[:], bank[0:64, 0:64]), deps=[tf2], inc="v")
                ps.done(bi, vf)
                f_cur, f_nxt = f_nxt, f_cur
            vsn = em.emit("vector", lambda e, f_cur=f_cur: e.tensor_scalar(
                tmp64[:], f_cur[:], c_fe128[0:64, :], None, op0=OP.mult),
                deps=[em.tok("v")], inc="v")
            dh1 = em.dma("gpsimd", cc1_i[h][:], tmp64[:], deps=[vsn], stream="d_cc")
            ccth = em.emit("gpsimd", lambda e, h=h: e.collective_compute(
                "AllReduce", OP.add, replica_groups=rg, ins=[cc1_i[h].ap().opt()],
                outs=[cc1_o[h].ap().opt()]), deps=[dh1], inc="cc", add=1)
            dh2 = em.dma("gpsimd", s_a[:], cc1_o[h][:], deps=[ccth], stream="d_cc")
            vsi = em.emit("vector", lambda e: e.tensor_scalar(
                s_a[:], s_a[:], c_fo128[0:64, :], None, op0=OP.mult), deps=[dh2], inc="v")
            s_cur, s_nxt = s_a, s_b
            zh_war = None
            for c in range(NC1):
                cs = slice(c * C1, (c + 1) * C1)
                bi, bank, war = ps.take()
                tr_ = em.emit("tensor", lambda e, bank=bank, cs=cs, s_cur=s_cur: e.matmul(
                    bank[0:C1, 0:64], wt_all[:, cs], s_cur[:], start=True, stop=True),
                    deps=[vsi, war, em.tok("v")], inc="t")
                vr_ = em.emit("vector", lambda e, bank=bank, c=c: e.scalar_tensor_tensor(
                    r_sb[:], rl_all[:, c * 64:(c + 1) * 64], 1.0, bank[0:C1, 0:64],
                    op0=OP.mult, op1=OP.subtract), deps=[tr_], inc="v")
                ps.done(bi, vr_)
                bi, bank, war = ps.take()
                to1 = em.emit("tensor", lambda e, bank=bank, cs=cs, s_cur=s_cur: e.matmul(
                    bank[0:C1, 0:64], qh_sb[:, cs], s_cur[:], start=True, stop=True),
                    deps=[war], inc="t")
                vo1 = em.emit("vector", lambda e, bank=bank, c=c, h=h: e.tensor_scalar(
                    o12[:], bank[0:C1, 0:64], gamT_all[:, c * NV + h:c * NV + h + 1],
                    None, op0=OP.mult), deps=[to1], inc="v")
                ps.done(bi, vo1)
                bi, bank, war = ps.take()
                to2 = em.emit("tensor", lambda e, bank=bank, cs=cs: e.matmul(
                    bank[0:C1, 0:64], mt_all[:, cs], r_sb[:], start=True, stop=True),
                    deps=[vr_, war], inc="t")
                vo2 = em.emit("vector", lambda e, bank=bank: e.tensor_tensor(
                    o12[:], o12[:], bank[0:C1, 0:64], op=OP.add), deps=[to2, vo1], inc="v")
                ps.done(bi, vo2)
                so_ = em.emit("scalar", lambda e: e.activation(
                    jnk5[:], o12[:], AF.Square, accum_out=sso[0:C1, :]), deps=[vo2], inc="s")
                em.emit("vector", lambda e: e.tensor_scalar(
                    ro[0:C1, :], sso[0:C1, :], 1.0 / DV, 1e-6, op0=OP.mult, op1=OP.add),
                    deps=[so_], inc="v")
                sr_ = em.emit("scalar", lambda e: e.activation(
                    ro[0:C1, :], ro[0:C1, :], AF.Sqrt), deps=[em.tok("v")], inc="s")
                vrr = em.emit("vector", lambda e: e.reciprocal(ro[0:C1, :], ro[0:C1, :]),
                              deps=[sr_], inc="v")
                vof = em.emit("vector", lambda e: e.scalar_tensor_tensor(
                    o12[:], o12[:], ro[0:C1, :], c_normwr[:], op0=OP.mult, op1=OP.mult),
                    deps=[em.force(vrr)], inc="v")
                bi, bank, war = ps.take()
                tof = em.emit("tensor", lambda e, bank=bank: e.transpose(
                    bank[0:64, 0:C1], o12[:], c_ident[:]), deps=[vof, war], inc="t")
                vofT = em.emit("vector", lambda e, bank=bank: e.tensor_copy(
                    ofT[:], bank[0:64, 0:C1]), deps=[tof], inc="v")
                ps.done(bi, vofT)
                lz_ = em.dma("sync", zh_sb[:], z1_s[h * 64:(h + 1) * 64, c * C1:(c + 1) * C1],
                             deps=[zh_war], stream="d_z5")
                sz_ = em.emit("scalar", lambda e: e.activation(
                    zs_sb[:], zh_sb[:], AF.Silu), deps=[lz_], inc="s")
                zh_war = sz_
                vog = em.emit("vector", lambda e, c=c, h=h: e.tensor_tensor(
                    og[(h % 2) * 64:(h % 2) * 64 + 64,
                       (h // 2) * Tc + c * C1:(h // 2) * Tc + (c + 1) * C1],
                    ofT[:], zs_sb[:], op=OP.mult), deps=[vofT, sz_], inc="v")
                bi, bank, war = ps.take()
                ts2 = em.emit("tensor", lambda e, bank=bank, c=c: e.matmul(
                    bank[0:64, 0:64], kb_all[:, c * 64:(c + 1) * 64], r_sb[:],
                    start=True, stop=True), deps=[vr_, war], inc="t")
                vs2 = em.emit("vector", lambda e, bank=bank, c=c, h=h, s_cur=s_cur, s_nxt=s_nxt:
                              e.scalar_tensor_tensor(
                    s_nxt[:], s_cur[:], gCb_all[0:64, c * NV + h:c * NV + h + 1],
                    bank[0:64, 0:64], op0=OP.mult, op1=OP.add), deps=[ts2], inc="v")
                ps.done(bi, vs2)
                s_cur, s_nxt = s_nxt, s_cur
        if debug:
            for mb in range(NVB):
                em.dma("gpsimd", og_dbg[mb * 128:(mb + 1) * 128, :],
                       og[:, mb * Tc:(mb + 1) * Tc], deps=[em.tok("v")], stream="d_st")
        wob = sb("p5_wob", (128, 2 * 512), life="phase")
        ev5_war = None
        wob_war = [None, None]
        for nblk in range(D // 512):
            for i in range(NTC):
                bi, bank, war = ps.take()
                mmo = None
                for mb in range(NVB):
                    sl5 = mb % 2
                    wld5_ = em.dma("sync", wob[:, sl5 * 512:(sl5 + 1) * 512],
                                   WoutT[mb * 128:(mb + 1) * 128, nblk * 512:(nblk + 1) * 512],
                                   deps=[wob_war[sl5]], stream=f"d_wo{sl5}")
                    mmo = em.emit("tensor", lambda e, bank=bank, mb=mb, i=i, sl5=sl5, st=(mb == 0), sp=(mb == NVB - 1):
                                  e.matmul(bank[:, 0:512], og[:, mb * Tc + i * 128:mb * Tc + (i + 1) * 128],
                                           wob[:, sl5 * 512:(sl5 + 1) * 512], start=st, stop=sp),
                                  deps=[wld5_, war, em.tok("v") if nblk == 0 and i == 0 and mb == 0 else None],
                                  inc="t")
                    wob_war[sl5] = mmo
                vev = em.emit("vector", lambda e, bank=bank: e.tensor_copy(
                    ev5[:, 0:512], bank[:, 0:512]), deps=[mmo, ev5_war], inc="v")
                sd5 = em.dma("gpsimd", s1_s[i * 128:(i + 1) * 128, nblk * 512:(nblk + 1) * 512],
                             ev5[:, 0:512], deps=[vev], stream="d_st")
                ev5_war = sd5
                ps.done(bi, sd5)
    em.barrier()
    phase_reset()
    group_reset()

    if phases >= 6:
        NDI = I1 // 128
        wgb = sb("p6_wgb", (128, ND * 512), life="phase")
        hg6 = sb("p6_hg6", (128, NTC * 512), life="phase")
        xb6 = sb("p6_xb6", (128, ND * 128), life="phase")
        ab6 = sb("p6_ab6", (128, 512), life="phase")
        q_u = sb("p6_qu", (128, 128), life="phase"); q_l = sb("p6_ql", (128, 128), life="phase")
        q_v = sb("p6_qv", (128, 128), life="phase"); q_w = sb("p6_qw", (128, 128), life="phase")
        sec6 = sb("p6_sec", (128, 128), life="phase"); kq6 = sb("p6_kq", (128, 128), life="phase")
        hs6 = sb("p6_hs6", (128, 512), life="phase")
        wgb_war = None
        tr6_done = None
        hg_war = [None] * NTC
        for n in range(I1 // 512):
            wldg = []
            for j in range(ND):
                wldg.append(em.dma("sync", wgb[:, j * 512:(j + 1) * 512],
                                   wg_in[j * 128:(j + 1) * 128, n * 512:(n + 1) * 512],
                                   deps=[wgb_war], stream="d_w"))
            xld6 = [None] * NTC
            for i in range(NTC):
                xl = []
                for j in range(ND):
                    xl.append(em.dma("sync", xb6[:, j * 128:(j + 1) * 128],
                                     xn1f_s[j * 128:(j + 1) * 128, i * 128:(i + 1) * 128],
                                     deps=[xld6[i - 1]] if i > 0 else [em.tok("t")],
                                     stream="d_x"))
                bi, bank, war = ps.take()
                mmg = None
                for j in range(ND):
                    mmg = em.emit("tensor", lambda e, bank=bank, j=j, st=(j == 0), sp=(j == ND - 1):
                                  e.matmul(bank[:, 0:512], xb6[:, j * 128:(j + 1) * 128],
                                           wgb[:, j * 512:(j + 1) * 512], start=st, stop=sp),
                                  deps=[xl[-1], wldg[-1], war], inc="t")
                xld6[i] = mmg
                sg6 = em.emit("scalar", lambda e, bank=bank, i=i: e.activation(
                    hg6[:, i * 512:(i + 1) * 512], bank[:, 0:512], AF.Silu),
                    deps=[mmg, hg_war[i]], inc="s")
                ps.done(bi, sg6)
            gate_done = em.tok("s")
            wldu = []
            for j in range(ND):
                wldu.append(em.dma("sync", wgb[:, j * 512:(j + 1) * 512],
                                   wu_in[j * 128:(j + 1) * 128, n * 512:(n + 1) * 512],
                                   deps=[em.tok("t"), gate_done], stream="d_w"))
            for i in range(NTC):
                xl = []
                for j in range(ND):
                    xl.append(em.dma("sync", xb6[:, j * 128:(j + 1) * 128],
                                     xn1f_s[j * 128:(j + 1) * 128, i * 128:(i + 1) * 128],
                                     deps=[xld6[i - 1] if i > 0 else xld6[NTC - 1]],
                                     stream="d_x"))
                bi, bank, war = ps.take()
                mmu = None
                for j in range(ND):
                    mmu = em.emit("tensor", lambda e, bank=bank, j=j, st=(j == 0), sp=(j == ND - 1):
                                  e.matmul(bank[:, 0:512], xb6[:, j * 128:(j + 1) * 128],
                                           wgb[:, j * 512:(j + 1) * 512], start=st, stop=sp),
                                  deps=[xl[-1], wldu[-1], war], inc="t")
                xld6[i] = mmu
                vh6 = em.emit("vector", lambda e, bank=bank, i=i: e.tensor_tensor(
                    hg6[:, i * 512:(i + 1) * 512], hg6[:, i * 512:(i + 1) * 512],
                    bank[:, 0:512], op=OP.mult), deps=[mmu, gate_done], inc="v")
                ps.done(bi, vh6)
                hsl = hg6[:, i * 512:(i + 1) * 512]
                dbg_tok = None
                if debug:
                    dbg_tok = em.dma("gpsimd", hg6_dbg[i * 128:(i + 1) * 128, n * 512:(n + 1) * 512],
                                     hsl, deps=[vh6], stream="d_st")
                sa6 = em.emit("scalar", lambda e, hsl=hsl: e.activation(
                    ab6[:], hsl, AF.Abs), deps=[vh6, tr6_done], inc="s")
                qs = [ab6[:, q:512:4] for q in range(4)]
                em.emit("vector", lambda e, qs=qs: e.tensor_tensor(q_u[:], qs[0], qs[1], op=OP.max),
                        deps=[sa6], inc="v")
                em.emit("vector", lambda e, qs=qs: e.tensor_tensor(q_l[:], qs[0], qs[1], op=OP.min), inc="v")
                em.emit("vector", lambda e, qs=qs: e.tensor_tensor(q_v[:], qs[2], qs[3], op=OP.max), inc="v")
                em.emit("vector", lambda e, qs=qs: e.tensor_tensor(q_w[:], qs[2], qs[3], op=OP.min), inc="v")
                em.emit("vector", lambda e: e.tensor_tensor(sec6[:], q_u[:], q_v[:], op=OP.min), inc="v")
                em.emit("vector", lambda e: e.tensor_tensor(q_l[:], q_l[:], q_w[:], op=OP.max), inc="v")
                em.emit("vector", lambda e: e.tensor_tensor(sec6[:], sec6[:], q_l[:], op=OP.max), inc="v")
                for q in range(4):
                    em.emit("vector", lambda e, q=q: e.tensor_tensor(
                        kq6[:], ab6[:, q:512:4], sec6[:], op=OP.is_ge), inc="v")
                    em.emit("vector", lambda e, q=q, hsl=hsl: e.tensor_tensor(
                        hs6[:, q:512:4], hsl[:, q:512:4], kq6[:], op=OP.mult), inc="v")
                vmask = em.tok("v")
                hg_war[i] = dbg_tok if dbg_tok is not None else vmask
                for kk in range(4):
                    bi, bank, war = ps.take()
                    t6 = em.emit("tensor", lambda e, bank=bank, kk=kk: e.transpose(
                        bank[:, 0:128], hs6[:, kk * 128:(kk + 1) * 128], c_ident[:]),
                        deps=[vmask, war], inc="t")
                    v6 = em.emit("vector", lambda e, bank=bank, kk=kk: e.tensor_copy(
                        ab6[:, kk * 128:(kk + 1) * 128], bank[:, 0:128]), deps=[t6], inc="v")
                    d6 = em.dma("gpsimd", hsp_s[n * 512 + kk * 128:n * 512 + (kk + 1) * 128,
                                                i * 128:(i + 1) * 128],
                                ab6[:, kk * 128:(kk + 1) * 128], deps=[v6], stream="d_st")
                    ps.done(bi, d6)
                tr6_done = em.tok("d_st")
            wgb_war = em.tok("t")
        wdb = sb("p6_wdb", (128, 2 * 512), life="phase")
        hdb = sb("p6_hdb", (128, 2 * 512), life="phase")
        g6 = sb("p6_g6", (128, 512), life="phase")
        x1t6 = sb("p6_x1t", (128, 512), life="phase")
        s1t6 = sb("p6_s1t", (128, 512), life="phase")
        wdb_war = [None, None]
        hdb_war = [None, None]
        g6_war = None
        all_sp = em.tok("d_st")
        for o2 in range(D // 512):
            for tg in range(NTC // 4):
                pbs = []
                for q in range(4):
                    pbs.append(ps.take())
                mmd = [None] * 4
                for j in range(NDI):
                    sl6 = j % 2
                    lw = em.dma("sync", wdb[:, sl6 * 512:(sl6 + 1) * 512],
                                wd_in[j * 128:(j + 1) * 128, o2 * 512:(o2 + 1) * 512],
                                deps=[wdb_war[sl6]], stream=f"d_p6w{sl6}")
                    lh = em.dma("sync", hdb[:, sl6 * 512:(sl6 + 1) * 512],
                                hsp_s[j * 128:(j + 1) * 128, tg * 512:(tg + 1) * 512],
                                deps=[hdb_war[sl6], all_sp], stream=f"d_p6h{sl6}")
                    for q in range(4):
                        bi, bank, war = pbs[q]
                        mmd[q] = em.emit("tensor", lambda e, bank=bank, j=j, sl6=sl6, q=q,
                                         st=(j == 0), sp=(j == NDI - 1):
                                         e.matmul(bank[:, 0:512],
                                                  hdb[:, sl6 * 512 + q * 128:sl6 * 512 + (q + 1) * 128],
                                                  wdb[:, sl6 * 512:(sl6 + 1) * 512],
                                                  start=st, stop=sp),
                                         deps=[lw, lh] + ([war] if j == 0 else []), inc="t")
                    wdb_war[sl6] = mmd[3]
                    hdb_war[sl6] = mmd[3]
                for q in range(4):
                    i = tg * 4 + q
                    bi, bank, war = pbs[q]
                    sg_ = em.emit("scalar", lambda e, bank=bank: e.activation(
                        g6[:], bank[:, 0:512], AF.Silu), deps=[mmd[q], g6_war], inc="s")
                    dtok6 = None
                    if debug:
                        dtok6 = em.dma("gpsimd", g1_dbg[i * 128:(i + 1) * 128, o2 * 512:(o2 + 1) * 512],
                                       g6[:], deps=[sg_], stream="d_st")
                    lx1 = em.dma("sync", x1t6[:], x1_s[i * 128:(i + 1) * 128, o2 * 512:(o2 + 1) * 512],
                                 deps=[em.tok("v")], stream="d_x1f")
                    ls1 = em.dma("sync", s1t6[:], s1_s[i * 128:(i + 1) * 128, o2 * 512:(o2 + 1) * 512],
                                 stream="d_s1f")
                    vg1 = em.emit("vector", lambda e: e.tensor_tensor(
                        g6[:], g6[:], s1t6[:], op=OP.mult), deps=[sg_, ls1, dtok6], inc="v")
                    vg2 = em.emit("vector", lambda e: e.tensor_tensor(
                        g6[:], g6[:], x1t6[:], op=OP.add), deps=[lx1], inc="v")
                    do_ = em.dma("gpsimd", x_out[i * 128:(i + 1) * 128, o2 * 512:(o2 + 1) * 512],
                                 g6[:], deps=[vg2], stream="d_out")
                    g6_war = do_
                    ps.done(bi, do_)
    em.emit("gpsimd", lambda e: e.nop(), deps=[em.tok("d_out"), em.tok("d_st")])
    em.barrier()

    with nc.Block() as block:
        em.finalize(block)
    return nc


def kernel(**inputs):
    cfg = FULL_CFG
    per_core = host_prepare(cfg, inputs)
    nc = build_program(cfg, debug=False)
    res = run_bass_kernel_spmd(nc, per_core, core_ids=list(range(cfg["n_cores"])))
    B, T, D = cfg["B"], cfg["T"], cfg["D"]
    Tc = T // 2
    out = np.empty((B, T, D), np.float32)
    for c in range(cfg["n_cores"]):
        b, half = c // 2, c % 2
        out[b, half * Tc:(half + 1) * Tc, :] = np.asarray(res.results[c]["x_out"])
    return out
